# version 02
# baseline (speedup 1.0000x reference)
import numpy as np
import ml_dtypes
import concourse.bass as bass
import concourse.mybir as mybir
from concourse.bass_utils import run_bass_kernel_spmd

F32 = mybir.dt.float32
BF16 = mybir.dt.bfloat16
AF = mybir.ActivationFunctionType
ALU = mybir.AluOpType
AX = mybir.AxisListType

N_CORES = 8
T = 2048
NCH = 16
D = 1024
KD = 8
MAGIC = float(np.float32(3 * 2 ** 22))
LN_EPS = 1e-5

ENGINES = ("tensor", "vector", "scalar", "sync", "gpsimd")

N_SUBLAYERS = 6
import os as _os
STAGE = int(_os.environ.get("KMOD_STAGE", "99"))


_PHASE_N = [0]


class Phase:
    def __init__(self, kb, name):
        self.kb = kb
        _PHASE_N[0] += 1
        self.name = f"{name}_{_PHASE_N[0]}"
        self.q = {e: [] for e in ENGINES}

    def on(self, eng):
        def reg(fn):
            self.q[eng].append(fn)
            return fn
        return reg

    def run(self):
        with self.kb.nc.Block(self.name) as b:
            for eng in ENGINES:
                fns = self.q[eng]
                if not fns:
                    continue
                def body(e, fns=tuple(fns)):
                    for f in fns:
                        f(e)
                getattr(b, eng)(body)


class KB:
    def __init__(self):
        self.nc = bass.Bass("TRN2", target_bir_lowering=False,
                            detect_race_conditions=False)
        self._sems = {}
        self.t = {}

    def sem(self, name):
        if name not in self._sems:
            h = self.nc.semaphore(name).__enter__()
            self._sems[name] = [h, 0]
        return self._sems[name][0]

    def bump(self, name, k=1):
        s = self._sems[name]
        s[1] += k
        return s[1]

    def val(self, name):
        return self._sems[name][1]

    def sb(self, name, shape, dtype=F32, **kw):
        h = self.nc.alloc_sbuf_tensor(name, shape, dtype, **kw)
        self.t[name] = h
        return h

    def ps(self, name, shape, dtype=F32):
        h = self.nc.alloc_psum_tensor(name, shape, dtype)
        self.t[name] = h
        return h

    def dram(self, name, shape, dtype=F32, kind="Internal"):
        h = self.nc.dram_tensor(name, shape, dtype, kind=kind)
        self.t[name] = h
        return h


def fence(e, n=8):
    for _ in range(n):
        e.nop()


def flush(kb, e, ins):
    h = kb.sem("flv")
    v = kb.bump("flv")
    ins.then_inc(h, 1)
    e.wait_ge(h, v)



def _host_prep(inp):
    f = np.ascontiguousarray

    def row(v):
        return f(np.asarray(v, np.float32).reshape(1, -1))

    shared = {
        "wqkvT": f(np.asarray(inp["L0_attn_w_qkv"], np.float32).T),
        "woutT": f(np.asarray(inp["L0_attn_w_out"], np.float32).T),
        "proj": f(np.asarray(inp["L0_attn_proj"], np.float32)),
        "aw1T": f(np.asarray(inp["L0_attn_w1"], np.float32).T),
        "aw2T": f(np.asarray(inp["L0_attn_w2"], np.float32).T),
        "ab_out": row(inp["L0_attn_b_out"]), "agamma": row(inp["L0_attn_gamma"]),
        "abeta": row(inp["L0_attn_beta"]), "ab1": row(inp["L0_attn_b1"]),
        "ab2": row(inp["L0_attn_b2"]),
        "fb1": row(inp["L0_ffn_b1"]), "fb2": row(inp["L0_ffn_b2"]),
        "qw": f(np.asarray(inp["L1_attn_q_w"], np.float32)),
        "kw": f(np.asarray(inp["L1_attn_k_w"], np.float32)),
        "vw": f(np.asarray(inp["L1_attn_v_w"], np.float32)),
        "ow": f(np.asarray(inp["L1_attn_o_w"], np.float32)),
        "qb": row(inp["L1_attn_q_b"]), "kbias": row(inp["L1_attn_k_b"]),
        "vb": row(inp["L1_attn_v_b"]), "ob": row(inp["L1_attn_o_b"]),
        "ln1w": row(inp["L1_attn_ln1_w"]), "ln1b": row(inp["L1_attn_ln1_b"]),
        "ln2w": row(inp["L1_attn_ln2_w"]), "ln2b": row(inp["L1_attn_ln2_b"]),
        "decw": f(np.asarray(inp["L1_attn_dec_w"], np.float32)),
        "decb": row(inp["L1_attn_dec_b"]),
        "f1w": f(np.asarray(inp["L1_attn_f1_w"], np.float32)),
        "f1b": row(inp["L1_attn_f1_b"]),
        "f2w": f(np.asarray(inp["L1_attn_f2_w"], np.float32)),
        "f2b": row(inp["L1_attn_f2_b"]),
    }
    fw1 = np.asarray(inp["L0_ffn_w1"], np.float32)
    fw2 = np.asarray(inp["L0_ffn_w2"], np.float32)
    gw = np.asarray(inp["L1_ffn_w_gate"], np.float32)
    uw = np.asarray(inp["L1_ffn_w_up"], np.float32)
    dw = np.asarray(inp["L1_ffn_w_down"], np.float32)

    x = np.asarray(inp["L0_attn_x"], np.float32).reshape(N_CORES * T, D)
    per_core = []
    for c in range(N_CORES):
        m = dict(shared)
        m["x_in"] = f(x[c * T:(c + 1) * T])
        m["fw1_sh"] = f(fw1[c * 512:(c + 1) * 512])
        m["fw2_sh"] = f(fw2[c * 128:(c + 1) * 128])
        m["gw_sh"] = f(gw[c * 384:(c + 1) * 384])
        m["uw_sh"] = f(uw[c * 384:(c + 1) * 384])
        m["dw_sh"] = f(dw[c * 128:(c + 1) * 128])
        m["m_lt"] = row((np.arange(N_CORES) < c).astype(np.float32))
        m["m_gt"] = row((np.arange(N_CORES) > c).astype(np.float32))
        m["m_me"] = row((np.arange(N_CORES) == c).astype(np.float32))
        per_core.append(m)
    return per_core



def build_program():
    kb = KB()
    nc = kb.nc

    x_in = kb.dram("x_in", [T, D], F32, kind="ExternalInput")
    out = kb.dram("out", [T, D], F32, kind="ExternalOutput")
    for name, shape in [
        ("wqkvT", [D, 3072]), ("woutT", [D, D]), ("proj", [16, 64, 256]),
        ("aw1T", [D, 4096]), ("aw2T", [4096, D]),
        ("qw", [D, D]), ("kw", [D, D]), ("vw", [D, D]), ("ow", [D, D]),
        ("decw", [D, 16]), ("f1w", [D, 4096]), ("f2w", [4096, D]),
        ("fw1_sh", [512, D]), ("fw2_sh", [128, 4096]),
        ("gw_sh", [384, D]), ("uw_sh", [384, D]), ("dw_sh", [128, 3072]),
    ]:
        kb.dram(name, shape, F32, kind="ExternalInput")
    for name, n in [("ab_out", D), ("agamma", D), ("abeta", D), ("ab1", 4096),
                    ("ab2", D), ("fb1", 4096), ("fb2", D), ("qb", D),
                    ("kbias", D), ("vb", D), ("ob", D), ("ln1w", D),
                    ("ln1b", D), ("ln2w", D), ("ln2b", D), ("decb", 16),
                    ("f1b", 4096), ("f2b", D), ("m_lt", N_CORES),
                    ("m_gt", N_CORES), ("m_me", N_CORES)]:
        kb.dram(name, [1, n], F32, kind="ExternalInput")

    eye_bf = nc.inline_tensor(np.eye(128, dtype=ml_dtypes.bfloat16), "eye_bf")
    eye_fd = nc.inline_tensor(np.eye(128, dtype=np.float32), "eye_fd")
    sel_np = np.zeros((16, 2048), np.float32)
    for _h in range(16):
        sel_np[_h, _h * 128:(_h + 1) * 128] = 1.0
    kb.t["sel_d"] = nc.inline_tensor(sel_np, "sel_d")
    ones_d = nc.inline_tensor(np.ones((1, 128), np.float32), "ones_d")
    ones_bf_d = nc.inline_tensor(np.ones((1, 128), ml_dtypes.bfloat16), "ones_bf_d")

    kb.PPA, kb.PPB, kb.PPC, kb.PPD, kb.PPE = 0, 524288, 1048576, 1441792, 1835008
    kb.PPTOT = 2228224
    kb.dram("pp_in", [kb.PPTOT], BF16)
    kb.dram("pp_out", [N_CORES, kb.PPTOT], BF16)
    kb.dram("bw2Td", [32, 128, 1024], BF16)
    kb.dram("dwTd", [24, 128, 1024], BF16)
    kb.dram("aux_spill", [T, D], F32)
    kb.dram("v_stage", [NCH, 128, D], BF16)
    kb.dram("kv_loc", [128, 2080], F32)
    kb.dram("kv_red", [128, 2080], F32)
    for nm in ["up_d", "gate_d", "gs_d", "prod_d"]:
        kb.dram(nm, [24, 128, 2048], F32)
    kb.dram("mx_loc", [1, 8], F32)
    kb.dram("mx_red", [1, 8], F32)
    kb.dram("hop_d", [1024], F32)
    kb.dram("stream_spill", [T, D], F32)
    kb.dram("r_d", [16, 128, 1040], F32)
    kb.dram("s_d", [16, 128, 1040], F32)
    kb.dram("ll_d", [16, 2048], F32)
    kb.dram("lion_in", [133136], F32)
    kb.dram("lion_out", [N_CORES, 133136], F32)

    stream = kb.sb("stream", [128, NCH * D], F32)
    aux = kb.sb("aux", [128, NCH * D], F32)
    auxb = aux.bitcast(BF16)
    xnt = kb.sb("xnt", [128, KD * T], BF16)
    wb = [kb.sb(f"wb{i}", [128, 1024], BF16) for i in range(2)]
    wbig = kb.sb("wbig", [128, 16384], BF16)
    kb.t["stg0"], kb.t["stg1"] = wb[0], wb[1]
    for nm in ["s_sum", "s_sq", "s_mu", "s_rs", "s_nmr", "s_t0"]:
        kb.sb(nm, [128, NCH], F32)
    eye_s = kb.sb("eye_s", [128, 128], BF16)
    eye_f = kb.sb("eye_f", [128, 128], F32)
    kb.sb("ones1b", [1, 128], BF16)
    kb.sb("onesc_b", [128, 16], BF16)
    kb.sb("ones_all", [128, 128], F32)
    rowa = kb.sb("rowa", [128, 1024], F32)
    kb.row_slots = {}

    def row_ap(nm, c0, n):
        part = kb.row_slots[nm]
        return rowa[part:part + 1, c0:c0 + n]
    kb.row_ap = row_ap

    def row_ones(nm, n=128):
        part = kb.row_slots[nm]
        return kb.t["ones_all"][part:part + 1, 0:n]
    kb.row_ones = row_ones

    def row_load(e, dma_sem, assigns):
        for i, a in enumerate(assigns):
            nm, src, n = a[0], a[1], a[2]
            slot = a[3] if len(a) > 3 else i
            assert slot <= 2, "matmul base partition must be 0/32/64"
            kb.row_slots[nm] = slot * 32
            e.dma_start(out=rowa[slot * 32:slot * 32 + 1, 0:n],
                        in_=kb.t[src][:, :]).then_inc(dma_sem, 16)
            kb.bump("dma", 16)
    kb.row_load = row_load
    kb.sb("agamma_c", [128, KD], F32)
    kb.sb("abeta_c", [128, KD], F32)
    kb.sb("fb1_c", [128, 32], F32)
    kb.sb("ab1_c", [128, 32], F32)
    kb.sb("f1b_c", [128, 32], F32)
    kb.sb("qb_c", [128, KD], F32)
    kb.sb("kb_c", [128, KD], F32)
    kb.sb("ln1w_c", [128, KD], F32)
    kb.sb("ln1b_c", [128, KD], F32)
    kb.sb("ln2w_c", [128, KD], F32)
    kb.sb("ln2b_c", [128, KD], F32)

    kb.sb("qsc", [128, 16], F32)
    kb.sb("phi_st", [128, 1024], F32)
    kb.sb("s_rs_sl", [128, NCH], F32)
    kb.sb("s_nmr_sl", [128, NCH], F32)
    psum = [kb.ps(f"psb{i}", [128, 512], F32) for i in range(6)]
    kb.ps("ps_bf", [128, 1024], BF16)

    for s in ["dma", "dmag", "mm", "ev", "ea", "cc", "free", "ldw"]:
        kb.sem(s)

    kb.t["auxb"] = auxb
    kb.t["xntf"] = xnt.bitcast(F32)
    kb.t["wbigf"] = wbig.bitcast(F32)
    kb.t["out"] = out
    kb.t["x_in"] = x_in

    p = Phase(kb, "p0")
    dma = kb.sem("dma")

    @p.on("vector")
    def _(e):
        e.memset(kb.t["onesc_b"][:, :], 1.0)
        e.memset(kb.t["ones_all"][:, :], 1.0)

    @p.on("sync")
    def _(e):
        def ld(dst, src_ap):
            e.dma_start(out=dst, in_=src_ap).then_inc(dma, 16)
            kb.bump("dma", 16)
        ld(eye_s[:, :], eye_bf[:, :])
        ld(eye_f[:, :], eye_fd[:, :])
        ld(kb.t["ones1b"][:, :], ones_bf_d[:, :])

        for cn, src, n in [("agamma_c", "agamma", KD), ("abeta_c", "abeta", KD),
                           ("fb1_c", "fb1", 32), ("ab1_c", "ab1", 32),
                           ("f1b_c", "f1b", 32), ("qb_c", "qb", KD),
                           ("ln1w_c", "ln1w", KD), ("ln1b_c", "ln1b", KD),
                           ("ln2w_c", "ln2w", KD), ("ln2b_c", "ln2b", KD)]:
            e.dma_start(out=kb.t[cn][:, :],
                        in_=bass.AP(kb.t[src], 0, [[1, 128], [128, n]]),
                        allow_slow_non_contiguous=True).then_inc(dma, 16)
            kb.bump("dma", 16)
        e.dma_start(out=kb.t["kb_c"][:, :],
                    in_=bass.AP(kb.t["kbias"], 0, [[1, 128], [128, KD]]),
                    allow_slow_non_contiguous=True).then_inc(dma, 16)
        kb.bump("dma", 16)
        for tc in range(NCH):
            ld(stream[:, tc * D:(tc + 1) * D], x_in[tc * 128:(tc + 1) * 128, :])
        e.wait_ge(dma, kb.val("dma"))

    p.run()

    p = Phase(kb, "p0z")

    @p.on("vector")
    def _(e):
        e.memset(kb.t["wb0"].bitcast(F32)[0:1, 0:8], 0.0)

    p.run()
    p = Phase(kb, "p0z2")

    @p.on("sync")
    def _(e):
        e.dma_start(out=kb.t["mx_loc"][:, :],
                    in_=kb.t["wb0"].bitcast(F32)[0:1, 0:8]).then_inc(dma, 16)
        kb.bump("dma", 16)
        e.wait_ge(dma, kb.val("dma"))

    p.run()

    if STAGE >= 1:
        emit_prep(kb)
    p = Phase(kb, "p1_ag")
    scc = kb.sem("cc")

    if STAGE >= 2:
        @p.on("gpsimd")
        def _(e):
            e.collective_compute(
                "AllGather", ALU.bypass, replica_groups=[list(range(N_CORES))],
                ins=[kb.t["pp_in"].ap().opt()], outs=[kb.t["pp_out"].ap().opt()],
            ).then_inc(scc, 1)
            kb.bump("cc", 1)

    p.run()
    prep_cc = kb.val("cc")

    p = Phase(kb, "p1_repack")
    dma2 = kb.sem("dma")

    @(p.on("sync") if STAGE >= 3 else (lambda f: f))
    def _(e):
        e.wait_ge(scc, prep_cc)
        rp = kb.t["stg0"]
        rp2 = kb.t["stg1"]
        n0 = kb.val("dma")
        for kH in range(32):
            buf = rp if kH % 2 == 0 else rp2
            if kH >= 2:
                e.wait_ge(dma2, n0 + (kH - 1) * 32)
            e.dma_start(out=buf[:, :],
                        in_=bass.AP(kb.t["pp_out"], kb.PPB + kH * 16384,
                                    [[128, 128], [kb.PPTOT, 8], [1, 128]]),
                        allow_slow_non_contiguous=True).then_inc(dma2, 16)
            kb.bump("dma", 16)
            e.wait_ge(dma2, n0 + kH * 32 + 16)
            e.dma_start(out=bass.AP(kb.t["bw2Td"], kH * 131072,
                                    [[1024, 128], [1, 1024]]),
                        in_=buf[:, :]).then_inc(dma2, 16)
            kb.bump("dma", 16)
        n1 = kb.val("dma")
        for kH in range(24):
            buf = rp if kH % 2 == 0 else rp2
            if kH >= 2:
                e.wait_ge(dma2, n1 + (kH - 1) * 32)
            e.dma_start(out=buf[:, :],
                        in_=bass.AP(kb.t["pp_out"], kb.PPE + kH * 16384,
                                    [[128, 128], [kb.PPTOT, 8], [1, 128]]),
                        allow_slow_non_contiguous=True).then_inc(dma2, 16)
            kb.bump("dma", 16)
            e.wait_ge(dma2, n1 + kH * 32 + 16)
            e.dma_start(out=bass.AP(kb.t["dwTd"], kH * 131072,
                                    [[1024, 128], [1, 1024]]),
                        in_=buf[:, :]).then_inc(dma2, 16)
            kb.bump("dma", 16)
        e.wait_ge(dma2, kb.val("dma"))

    p.run()

    if N_SUBLAYERS >= 1 and STAGE >= 4:
        emit_L0_ffn_sublayer(kb, tag="sl0", wait_cc=prep_cc)
    if N_SUBLAYERS >= 2:
        emit_L0_attn_sublayer(kb, tag="sl1")
    if N_SUBLAYERS >= 3:
        emit_L0_ffn_sublayer(kb, tag="sl2", wait_cc=prep_cc)
    if N_SUBLAYERS >= 4:
        emit_L1_ffn_sublayer(kb, tag="sl3")
    if N_SUBLAYERS >= 5:
        emit_L1_attn_sublayer(kb, tag="sl4")
    if N_SUBLAYERS >= 6:
        emit_L1_ffn_sublayer(kb, tag="sl5")

    p = Phase(kb, "p_out")

    @p.on("sync")
    def _(e):
        for tc in range(NCH):
            e.dma_start(out=out[tc * 128:(tc + 1) * 128, :],
                        in_=stream[:, tc * D:(tc + 1) * D]).then_inc(dma, 16)
            kb.bump("dma", 16)
        e.wait_ge(dma, kb.val("dma"))

    p.run()
    return kb



def emit_prep(kb):
    dma = kb.sem("dma")
    w_nat = kb.t["aux"]
    w_bin = kb.t["xnt"]
    evb = kb.t["xnt"]
    EV0 = 4096
    alpha = kb.sb("pp_al", [128, 8], F32)
    scl = kb.sb("pp_scl", [128, 96], F32)
    rcl = kb.sb("pp_rcl", [128, 96], F32)
    eye_s = kb.t["eye_s"]
    pp_in = kb.t["pp_in"]

    def load_shard(name, ncols, packed):
        p = Phase(kb, f"prep_ld_{name}")

        @p.on("sync")
        def _(e):
            if packed:
                na = ncols // 1024
                src = bass.AP(kb.t[name], 0,
                              [[1024, 128], [128 * 1024, na], [1, 1024]])
                dst = bass.AP(w_nat, 0, [[16384, 128], [1024, na], [1, 1024]])
            else:
                src = kb.t[name][:, :]
                dst = w_nat[:, 0:ncols]
            e.dma_start(out=dst, in_=src).then_inc(dma, 16)
            kb.bump("dma", 16)
            e.wait_ge(dma, kb.val("dma"))

        p.run()

    def binarize(n_tiles, tile_cols):
        p = Phase(kb, "prep_bin_a")

        @p.on("vector")
        def _(e):
            for a in range(n_tiles):
                e.tensor_reduce(alpha[:, a:a + 1],
                                w_nat[:, a * tile_cols:(a + 1) * tile_cols],
                                AX.X, ALU.add, apply_absolute_value=True)

        @p.on("scalar")
        def _(e):
            e.activation(w_bin[:, 0:n_tiles * tile_cols],
                         w_nat[:, 0:n_tiles * tile_cols], AF.Sign,
                         bias=0.0, scale=1.0)

        p.run()
        p = Phase(kb, "prep_bin_b")

        @p.on("vector")
        def _(e):
            for a in range(n_tiles):
                sl = w_bin[:, a * tile_cols:(a + 1) * tile_cols]
                e.tensor_scalar(sl, sl, alpha[:, a:a + 1], 1.0 / tile_cols,
                                ALU.mult, ALU.mult)

        p.run()

    def lpbq(ncols):
        nb = ncols // 32
        p = Phase(kb, "prep_lpbq_a")

        @p.on("vector")
        def _(e):
            view = bass.AP(w_nat, 0, [[16384, 128], [32, nb], [1, 32]])
            e.tensor_reduce(scl[:, 0:nb], view, AX.X, ALU.max,
                            apply_absolute_value=True)
            fence(e)
            e.tensor_scalar(scl[:, 0:nb], scl[:, 0:nb], 1.0 / 7.0, 1e-12,
                            ALU.mult, ALU.max)
            fence(e)
            e.reciprocal(rcl[:, 0:nb], scl[:, 0:nb])

        p.run()
        p = Phase(kb, "prep_lpbq_b")

        @p.on("vector")
        def _(e):
            wv = bass.AP(w_nat, 0, [[16384, 128], [32, nb], [1, 32]])
            rb = bass.AP(rcl, 0, [[96, 128], [1, nb], [0, 32]])
            sbb = bass.AP(scl, 0, [[96, 128], [1, nb], [0, 32]])
            e.tensor_tensor(wv, wv, rb, ALU.mult)
            fence(e)
            e.tensor_scalar(w_nat[:, 0:ncols], w_nat[:, 0:ncols], MAGIC, MAGIC,
                            ALU.add, ALU.subtract)
            fence(e)
            bv = bass.AP(w_bin, 0, [[16384, 128], [32, nb], [1, 32]])
            e.tensor_tensor(bv, wv, sbb, ALU.mult)

        p.run()

    def transpose_out(base, n_a, tile_major):
        pst = kb.t["ps_bf"]
        for a in range(n_a):
            p = Phase(kb, f"prep_tr_{base}_{a}")

            @p.on("tensor")
            def _(e, a=a):
                for k in range(8):
                    e.transpose(pst[:, k * 128:(k + 1) * 128],
                                w_bin[:, a * 1024 + k * 128:a * 1024 + (k + 1) * 128],
                                eye_s[:, :])

            p.run()
            p = Phase(kb, f"prep_ev_{base}_{a}")

            @p.on("vector")
            def _(e):
                e.tensor_copy(evb[:, EV0:EV0 + 1024], pst[:, :])

            p.run()
            p = Phase(kb, f"prep_st_{base}_{a}")

            @p.on("sync")
            def _(e, a=a):
                if tile_major:
                    dst = bass.AP(pp_in, base + a * 131072,
                                  [[1024, 128], [1, 1024]])
                else:
                    dst = bass.AP(pp_in, base + a * 131072,
                                  [[128, 128], [16384, 8], [1, 128]])
                e.dma_start(out=dst,
                            in_=evb[:, EV0:EV0 + 1024]).then_inc(dma, 16)
                kb.bump("dma", 16)
                e.wait_ge(dma, kb.val("dma"))

            p.run()

    load_shard("fw1_sh", 4096, packed=True)
    binarize(4, 1024)
    transpose_out(kb.PPA, 4, tile_major=True)
    load_shard("fw2_sh", 4096, packed=False)
    binarize(1, 4096)
    transpose_out(kb.PPB, 4, tile_major=False)
    load_shard("gw_sh", 3072, packed=True)
    lpbq(3072)
    transpose_out(kb.PPC, 3, tile_major=True)
    load_shard("uw_sh", 3072, packed=True)
    lpbq(3072)
    transpose_out(kb.PPD, 3, tile_major=True)
    load_shard("dw_sh", 3072, packed=False)
    lpbq(3072)
    transpose_out(kb.PPE, 3, tile_major=False)



def emit_ln_transpose(kb, src, dst_xnt, tag, gamma_beta=None,
                      tm_affine_out=None):
    LN_STAGE = int(_os.environ.get("KMOD_LN_STAGE", "99"))
    dma, smm, sev, sea = (kb.sem("dma"), kb.sem("mm"), kb.sem("ev"),
                          kb.sem("ea"))
    s_sum, s_sq = kb.t["s_sum"], kb.t["s_sq"]
    s_mu, s_rs, s_nmr, s_t0 = (kb.t["s_mu"], kb.t["s_rs"], kb.t["s_nmr"],
                               kb.t["s_t0"])
    stg = [kb.t["stg0"], kb.t["stg1"]]
    eye_s = kb.t["eye_s"]
    pst = kb.t["ps_bf"]

    p = Phase(kb, f"{tag}_st")

    if LN_STAGE >= 1:
        @p.on("vector")
        def _(e):
            for tc in range(NCH):
                e.tensor_reduce(s_sum[:, tc:tc + 1], src[:, tc * D:(tc + 1) * D],
                                AX.X, ALU.add)

    if LN_STAGE >= 2:
        @p.on("scalar")
        def _(e):
            for tc in range(NCH):
                e.activation(stg[tc % 2][:, :], src[:, tc * D:(tc + 1) * D],
                             AF.Square, bias=0.0, scale=1.0,
                             accum_out=s_sq[:, tc:tc + 1])

    p.run()

    p = Phase(kb, f"{tag}_mu")

    @(p.on("vector") if LN_STAGE >= 3 else (lambda f: f))
    def _(e):
        e.tensor_scalar(s_mu[:, :], s_sum[:, :], 1.0 / D, None, ALU.mult)
        e.tensor_scalar(s_t0[:, :], s_sq[:, :], 1.0 / D, None, ALU.mult)
        fence(e)
        e.tensor_tensor(s_sum[:, :], s_mu[:, :], s_mu[:, :], ALU.mult)
        fence(e)
        e.tensor_tensor(s_t0[:, :], s_t0[:, :], s_sum[:, :], ALU.subtract)
        fence(e)
        e.tensor_scalar(s_t0[:, :], s_t0[:, :], LN_EPS, None, ALU.add)

    p.run()

    p = Phase(kb, f"{tag}_rs")

    @(p.on("scalar") if LN_STAGE >= 4 else (lambda f: f))
    def _(e):
        e.activation(s_rs[:, :], s_t0[:, :], AF.Ln, bias=0.0, scale=1.0)
        fence(e, 12)
        e.activation(s_rs[:, :], s_rs[:, :], AF.Exp, bias=0.0, scale=-0.5)

    p.run()

    p = Phase(kb, f"{tag}_nm")

    @(p.on("vector") if LN_STAGE >= 5 else (lambda f: f))
    def _(e):
        e.tensor_tensor(s_nmr[:, :], s_mu[:, :], s_rs[:, :], ALU.mult)
        fence(e)
        e.tensor_scalar(s_nmr[:, :], s_nmr[:, :], -1.0, None, ALU.mult)

    p.run()

    if LN_STAGE < 6:
        return
    p = Phase(kb, f"{tag}_tr")
    ea0, mm0, ev0 = kb.val("ea"), kb.val("mm"), kb.val("ev")

    @p.on("scalar")
    def _(e):
        for tc in range(NCH):
            if tc >= 2:
                e.wait_ge(smm, mm0 + KD * (tc - 1))
            e.activation(stg[tc % 2][:, :], src[:, tc * D:(tc + 1) * D],
                         AF.Identity, bias=s_nmr[:, tc:tc + 1],
                         scale=s_rs[:, tc:tc + 1]).then_inc(sea, 1)
    kb.bump("ea", NCH)

    @p.on("tensor")
    def _(e):
        for tc in range(NCH):
            e.wait_ge(sea, ea0 + tc + 1)
            if tc >= 1:
                e.wait_ge(sev, ev0 + tc)
            for j in range(KD):
                e.transpose(pst[:, j * 128:(j + 1) * 128],
                            stg[tc % 2][:, j * 128:(j + 1) * 128],
                            eye_s[:, :]).then_inc(smm, 1)
    kb.bump("mm", NCH * KD)

    @p.on("vector")
    def _(e):
        for tc in range(NCH):
            e.wait_ge(smm, mm0 + KD * (tc + 1))
            if gamma_beta is None:
                dst = bass.AP(dst_xnt, tc * 128, [[KD * T, 128], [T, KD], [1, 128]])
                e.tensor_copy(dst, pst[:, :].rearrange("p (j f) -> p j f", j=KD)
                              ).then_inc(sev, 1)
            else:
                g, bt = gamma_beta
                for j in range(KD):
                    dj = bass.AP(dst_xnt, j * T + tc * 128, [[KD * T, 128], [1, 128]])
                    ins = e.tensor_scalar(dj, pst[:, j * 128:(j + 1) * 128],
                                          g[:, j:j + 1], bt[:, j:j + 1],
                                          ALU.mult, ALU.add)
                    if j == KD - 1:
                        ins.then_inc(sev, 1)
    kb.bump("ev", NCH)

    p.run()

    if tm_affine_out is not None:
        dst_tm, gslot, bslot = tm_affine_out
        psum = [kb.t[f"psb{i}"] for i in range(6)]
        ust = kb.t["phi_st"]
        p = Phase(kb, f"{tag}_tmbc")

        @p.on("tensor")
        def _(e):
            for du in range(2):
                e.matmul(psum[du][:, :], kb.row_ones(gslot),
                         kb.row_ap(gslot, du * 512, 512), start=True,
                         stop=True).then_inc(smm, 1)
                e.matmul(psum[2 + du][:, :], kb.row_ones(bslot),
                         kb.row_ap(bslot, du * 512, 512), start=True,
                         stop=True).then_inc(smm, 1)
        kb.bump("mm", 4)

        p.run()
        p = Phase(kb, f"{tag}_tmap")
        ea1, ev1 = kb.val("ea"), kb.val("ev")

        @p.on("scalar")
        def _(e):
            for tc in range(NCH):
                if tc >= 1:
                    e.wait_ge(sev, ev1 + tc)
                e.activation(ust[:, 0:1024], src[:, tc * D:(tc + 1) * D],
                             AF.Identity, bias=s_nmr[:, tc:tc + 1],
                             scale=s_rs[:, tc:tc + 1]).then_inc(sea, 1)
        kb.bump("ea", NCH)

        @p.on("vector")
        def _(e):
            for tc in range(NCH):
                e.wait_ge(sea, ea1 + tc + 1)
                last = None
                for du in range(2):
                    u = ust[:, du * 512:(du + 1) * 512]
                    flush(kb, e, e.tensor_tensor(u, u, psum[du][:, :], ALU.mult))
                    sl = dst_tm[:, tc * D + du * 512:tc * D + (du + 1) * 512]
                    last = e.tensor_tensor(sl, u, psum[2 + du][:, :], ALU.add)
                last.then_inc(sev, 1)
        kb.bump("ev", NCH)

        p.run()



def emit_L0_ffn_sublayer(kb, tag, wait_cc):
    dma, smm, sev, sea, scc = (kb.sem("dma"), kb.sem("mm"), kb.sem("ev"),
                               kb.sem("ea"), kb.sem("cc"))
    stream, xnt, auxb = kb.t["stream"], kb.t["xnt"], kb.t["auxb"]
    hbuf = auxb
    wbufs = [kb.t["wb0"], kb.t["wb1"]]
    psum = [kb.t[f"psb{i}"] for i in range(6)]
    pp_out = kb.t["pp_out"]
    fb1_c = kb.t["fb1_c"]

    emit_ln_transpose(kb, stream, xnt, tag + "_ln")

    p = Phase(kb, tag + "_rows")

    @p.on("sync")
    def _(e):
        e.wait_ge(scc, wait_cc)
        kb.row_load(e, dma, [("fb2_r", "fb2", D)])
        e.wait_ge(dma, kb.val("dma"))

    p.run()

    for ttH in range(2):
        if STAGE >= 5:
            _l0ffn_h_half(kb, tag, ttH, hbuf, wbufs, psum, pp_out, xnt, fb1_c)
        if STAGE >= 6:
            _l0ffn_d_half(kb, tag, ttH, hbuf, wbufs, psum, pp_out, None, stream)


def _l0ffn_h_half(kb, tag, ttH, hbuf, wbufs, psum, pp_out, xnt, fb1_c):
    dma, smm, sea = kb.sem("dma"), kb.sem("mm"), kb.sem("ea")
    NB = 4
    p = Phase(kb, f"{tag}_h{ttH}")
    dm0, mm0, ea0 = kb.val("dma"), kb.val("mm"), kb.val("ea")

    @p.on("sync")
    def _(e):
        for m in range(32):
            buf = wbufs[m % 2]
            if m >= 2:
                e.wait_ge(smm, mm0 + (m - 1) * 2)
            e.dma_start(out=buf[:, :],
                        in_=bass.AP(pp_out, (m // 4) * kb.PPTOT + kb.PPA + (m % 4) * 131072,
                                    [[1024, 128], [1, 1024]])
                        ).then_inc(dma, 16)
            kb.bump("dma", 16)

    @p.on("tensor")
    def _(e):
        n = 0
        for m in range(32):
            buf = wbufs[m % 2]
            e.wait_ge(dma, dm0 + (m + 1) * 16)
            for t2 in range(2):
                bank = psum[n % NB]
                if n >= NB:
                    e.wait_ge(sea, ea0 + n - NB + 1)
                tok0 = ttH * 1024 + t2 * 512
                for k in range(8):
                    ins = e.matmul(bank[:, :], buf[:, k * 128:(k + 1) * 128],
                                   xnt[:, k * T + tok0:k * T + tok0 + 512],
                                   start=(k == 0), stop=(k == 7))
                ins.then_inc(smm, 1)
                n += 1
    kb.bump("mm", 64)

    @p.on("scalar")
    def _(e):
        n = 0
        for m in range(32):
            for t2 in range(2):
                e.wait_ge(smm, mm0 + n + 1)
                bank = psum[n % NB]
                e.activation(hbuf[:, m * 1024 + t2 * 512:m * 1024 + (t2 + 1) * 512],
                             bank[:, :], AF.Gelu, bias=fb1_c[:, m:m + 1],
                             scale=1.0).then_inc(sea, 1)
                n += 1
    kb.bump("ea", 64)

    p.run()


def _l0ffn_d_half(kb, tag, ttH, hbuf, wbufs, psum, pp_out, ones1, stream):
    dma, smm, sev = kb.sem("dma"), kb.sem("mm"), kb.sem("ev")
    groups = [[0, 1, 2], [3, 4, 5], [6, 7]]
    for gi, taus in enumerate(groups):
        p = Phase(kb, f"{tag}_d{ttH}_{gi}")
        dm0, mm0, ev0 = kb.val("dma"), kb.val("mm"), kb.val("ev")
        nt = len(taus)

        @p.on("sync")
        def _(e, taus=taus, nt=nt, mm0=mm0):
            for kH in range(32):
                buf = wbufs[kH % 2]
                if kH >= 2:
                    e.wait_ge(smm, mm0 + kH - 1)
                e.dma_start(out=buf[:, :],
                            in_=bass.AP(kb.t["bw2Td"], kH * 131072,
                                        [[1024, 128], [1, 1024]])
                            ).then_inc(dma, 16)
                kb.bump("dma", 16)

        @p.on("tensor")
        def _(e, taus=taus, nt=nt, ttH=ttH, dm0=dm0, mm0=mm0):
            for kH in range(32):
                buf = wbufs[kH % 2]
                e.wait_ge(dma, dm0 + (kH + 1) * 16)
                last = None
                for ti, tau in enumerate(taus):
                    for dh in range(2):
                        bank = psum[ti * 2 + dh]
                        last = e.matmul(
                            bank[:, :],
                            hbuf[:, kH * 1024 + tau * 128:kH * 1024 + (tau + 1) * 128],
                            buf[:, dh * 512:(dh + 1) * 512],
                            start=(kH == 0), stop=False)
                last.then_inc(smm, 1)
            for ti, tau in enumerate(taus):
                for dh in range(2):
                    bank = psum[ti * 2 + dh]
                    e.matmul(bank[:, :], kb.row_ones("fb2_r"),
                             kb.row_ap("fb2_r", dh * 512, 512),
                             start=False, stop=True).then_inc(smm, 1)
        kb.bump("mm", 32 + nt * 2)

        @p.on("vector")
        def _(e, taus=taus, ttH=ttH, mm0=mm0, nt=nt):
            e.wait_ge(smm, mm0 + 32 + nt * 2)
            for ti, tau in enumerate(taus):
                tc = ttH * 8 + tau
                for dh in range(2):
                    bank = psum[ti * 2 + dh]
                    sl = stream[:, tc * D + dh * 512:tc * D + (dh + 1) * 512]
                    e.scalar_tensor_tensor(sl, bank[:, :], 0.5, sl, ALU.mult, ALU.add)

        p.run()



def emit_proj_B(kb, tag, lhs_fn, n_kH, wsrc_fn, cast_w, bias_slot, out_cb,
                wait_first=None):
    dma, dmag, smm, sev = (kb.sem("dma"), kb.sem("dmag"), kb.sem("mm"),
                           kb.sem("ev"))
    wbufs = [kb.t["wb0"], kb.t["wb1"]]
    psum = [kb.t[f"psb{i}"] for i in range(6)]
    groups = [[0, 1, 2], [3, 4, 5], [6, 7, 8], [9, 10, 11], [12, 13], [14, 15]]
    for gi, taus in enumerate(groups):
        p = Phase(kb, f"{tag}_pB{gi}")
        wsem = dmag if cast_w else dma
        wsname = "dmag" if cast_w else "dma"
        dm0, mm0, ev0 = kb.val(wsname), kb.val("mm"), kb.val("ev")
        nt = len(taus)

        @p.on("gpsimd" if cast_w else "sync")
        def _(e, taus=taus, dm0=dm0, mm0=mm0, gi=gi):
            if gi == 0 and wait_first is not None:
                sem_h, val = wait_first
                e.wait_ge(sem_h, val)
            for kH in range(n_kH):
                buf = wbufs[kH % 2]
                if kH >= 2:
                    e.wait_ge(smm, mm0 + kH - 1)
                e.dma_start(out=buf[:, :], in_=wsrc_fn(kH)).then_inc(wsem, 16)
                kb.bump(wsname, 16)

        @p.on("tensor")
        def _(e, taus=taus, nt=nt, dm0=dm0, mm0=mm0, ev0=ev0):
            for kH in range(n_kH):
                buf = wbufs[kH % 2]
                e.wait_ge(wsem, dm0 + (kH + 1) * 16)
                last = None
                stop = (kH == n_kH - 1) and (bias_slot is None)
                for ti, tau in enumerate(taus):
                    for dh in range(2):
                        bank = psum[ti * 2 + dh]
                        last = e.matmul(bank[:, :], lhs_fn(kH, tau),
                                        buf[:, dh * 512:(dh + 1) * 512],
                                        start=(kH == 0), stop=stop)
                last.then_inc(smm, 1)
            if bias_slot is not None:
                nm, width = bias_slot
                for ti, tau in enumerate(taus):
                    for dh in range(2):
                        bank = psum[ti * 2 + dh]
                        e.matmul(bank[:, :], kb.row_ones(nm),
                                 kb.row_ap(nm, dh * 512, 512),
                                 start=False, stop=True).then_inc(smm, 1)
        nb = nt * 2 if bias_slot is not None else 0
        kb.bump("mm", n_kH + nb)

        @p.on("vector")
        def _(e, taus=taus, nt=nt, mm0=mm0):
            e.wait_ge(smm, mm0 + n_kH + (nt * 2 if bias_slot is not None else 0))
            for ti, tau in enumerate(taus):
                for dh in range(2):
                    bank = psum[ti * 2 + dh]
                    ins = out_cb(e, bank, tau, dh)
                    if ins is not None:
                        ins.then_inc(sev, 1)
                        kb.bump("ev", 1)

        p.run()



def emit_L0_attn_sublayer(kb, tag):
    SL1S = int(_os.environ.get("KMOD_SL1_STAGE", "99"))
    nc = kb.nc
    dma, dmag, smm, sev, sea, scc = (kb.sem("dma"), kb.sem("dmag"),
                                     kb.sem("mm"), kb.sem("ev"),
                                     kb.sem("ea"), kb.sem("cc"))
    stream, xnt, auxb, aux = kb.t["stream"], kb.t["xnt"], kb.t["auxb"], kb.t["aux"]
    wbig = kb.t["wbig"]
    wbufs = [kb.t["wb0"], kb.t["wb1"]]
    psum = [kb.t[f"psb{i}"] for i in range(6)]
    scr = kb.t["xntf"]
    R0 = lambda c0, n: auxb[:, c0:c0 + n]
    R1 = lambda c0, n: auxb[:, 16384 + c0:16384 + c0 + n]
    PRJ, KFH, QFB, KVB = 0, 4096, 8192, 12288
    phist = kb.t["phi_st"]

    emit_ln_transpose(kb, stream, xnt, tag + "_ln")
    p = Phase(kb, tag + "_sv")

    @p.on("vector")
    def _(e):
        e.tensor_copy(kb.t["s_rs_sl"][:, :], kb.t["s_rs"][:, :])
        e.tensor_copy(kb.t["s_nmr_sl"][:, :], kb.t["s_nmr"][:, :])

    @p.on("sync")
    def _(e):
        kb.row_load(e, dma, [("ab_out_r", "ab_out", D, 0),
                             ("ab2_r", "ab2", D, 2)])
        e.wait_ge(dma, kb.val("dma"))

    @p.on("gpsimd")
    def _(e):
        e.dma_start(out=bass.AP(wbig, PRJ, [[16384, 64], [256, 16], [1, 256]]),
                    in_=bass.AP(kb.t["proj"], 0,
                                [[256, 64], [16384, 16], [1, 256]])
                    ).then_inc(dmag, 16)
        kb.bump("dmag", 16)
        e.dma_start(out=bass.AP(wbig, PRJ + 64 * 16384,
                                [[16384, 64], [256, 16], [1, 256]]),
                    in_=bass.AP(kb.t["proj"], 0,
                                [[256, 64], [16384, 16], [1, 256]])
                    ).then_inc(dmag, 16)
        kb.bump("dmag", 16)
        e.wait_ge(dmag, kb.val("dmag"))

    p.run()

    p = Phase(kb, tag + "_qk")
    dg0, mm0, ea0 = kb.val("dmag"), kb.val("mm"), kb.val("ea")

    @p.on("gpsimd")
    def _(e):
        for m in range(16):
            buf = wbufs[m % 2]
            if m >= 2:
                e.wait_ge(smm, mm0 + (m - 1) * 4)
            e.dma_start(out=bass.AP(buf, 0, [[1024, 128], [128, 8], [1, 128]]),
                        in_=bass.AP(kb.t["wqkvT"], 128 * m,
                                    [[3072, 128], [128 * 3072, 8], [1, 128]])
                        ).then_inc(dmag, 16)
            kb.bump("dmag", 16)

    @p.on("tensor")
    def _(e):
        n = 0
        for m in range(16):
            buf = wbufs[m % 2]
            e.wait_ge(dmag, dg0 + (m + 1) * 16)
            for t4 in range(4):
                bank = psum[n % 4]
                if n >= 4:
                    e.wait_ge(sea, ea0 + n - 4 + 1)
                last = None
                for k in range(8):
                    last = e.matmul(bank[:, :], buf[:, k * 128:(k + 1) * 128],
                                    xnt[:, k * T + t4 * 512:k * T + (t4 + 1) * 512],
                                    start=(k == 0), stop=(k == 7))
                last.then_inc(smm, 1)
                n += 1
    kb.bump("mm", 64)

    @p.on("scalar")
    def _(e):
        n = 0
        for m in range(16):
            for t4 in range(4):
                e.wait_ge(smm, mm0 + n + 1)
                bank = psum[n % 4]
                dst = (R0(m * 2048 + t4 * 512, 512) if m < 8
                       else R1((m - 8) * 2048 + t4 * 512, 512))
                e.activation(dst, bank[:, :], AF.Identity, bias=0.0,
                             scale=1.0).then_inc(sea, 1)
                n += 1
    kb.bump("ea", 64)

    p.run()

    def v_lhs(kH, tc):
        return xnt[:, kH * T + tc * 128:kH * T + (tc + 1) * 128]

    def v_wsrc(kH):
        return bass.AP(kb.t["wqkvT"], 128 * kH * 3072 + 2048,
                       [[3072, 128], [1, 1024]])

    if SL1S < 3:
        return
    emit_sl1_v(kb, tag + "_v", v_lhs, v_wsrc, wbig, QFB)

    if SL1S < 4:
        return
    emit_sl1_kfkv(kb, tag)

    p = Phase(kb, tag + "_ar")

    @p.on("sync")
    def _(e):
        e.dma_start(out=kb.t["kv_loc"][:, :], in_=scr[:, 0:2080]).then_inc(dma, 16)
        kb.bump("dma", 16)
        e.wait_ge(dma, kb.val("dma"))

    p.run()
    p = Phase(kb, tag + "_ar2")

    @p.on("gpsimd")
    def _(e):
        e.collective_compute(
            "AllReduce", ALU.add, replica_groups=[list(range(N_CORES))],
            ins=[kb.t["kv_loc"].ap().opt()], outs=[kb.t["kv_red"].ap().opt()],
        ).then_inc(scc, 1)
        kb.bump("cc", 1)

    p.run()
    ar_cc = kb.val("cc")

    if SL1S < 6:
        return
    emit_sl1_qfattn(kb, tag, ar_cc)

    if SL1S < 7:
        return
    p = Phase(kb, tag + "_rec")

    @p.on("scalar")
    def _(e):
        for tc in range(NCH):
            e.activation(aux[:, tc * D:(tc + 1) * D],
                         stream[:, tc * D:(tc + 1) * D], AF.Identity,
                         bias=kb.t["s_nmr_sl"][:, tc:tc + 1],
                         scale=kb.t["s_rs_sl"][:, tc:tc + 1])

    p.run()

    def wo_lhs(kH, tc):
        return xnt[:, kH * 2048 + tc * 128:kH * 2048 + (tc + 1) * 128]

    def wo_wsrc(kH):
        return bass.AP(kb.t["woutT"], 128 * kH * 1024, [[1024, 128], [1, 1024]])

    def wo_out(e, bank, tc, dh):
        sl = aux[:, tc * D + dh * 512:tc * D + (dh + 1) * 512]
        return e.scalar_tensor_tensor(sl, bank[:, :], 1.0, sl, ALU.mult, ALU.add)

    emit_proj_B(kb, tag + "_wo", wo_lhs, 8, wo_wsrc, True,
                ("ab_out_r", D), wo_out)

    if SL1S < 8:
        return
    p = Phase(kb, tag + "_gbrows")

    @p.on("sync")
    def _(e):
        kb.row_load(e, dma, [("lng_r", "agamma", D, 0), ("lnb_r", "abeta", D, 1)])
        e.wait_ge(dma, kb.val("dma"))

    p.run()
    emit_ln_transpose(kb, aux, xnt, tag + "_ln1",
                      gamma_beta=(kb.t["agamma_c"], kb.t["abeta_c"]),
                      tm_affine_out=(aux, "lng_r", "lnb_r"))

    if SL1S < 9:
        return
    emit_sl1_ffn(kb, tag)

    if SL1S < 10:
        return
    emit_ln_final_add(kb, tag + "_ln2", aux, stream, "agamma", "abeta",
                      rows_loaded=True)


def emit_fm_transpose(kb, tag, src_tm, dst_xnt):
    smm, sev = kb.sem("mm"), kb.sem("ev")
    eye_s = kb.t["eye_s"]
    pst = kb.t["ps_bf"]
    p = Phase(kb, f"{tag}_tr")
    mm0, ev0 = kb.val("mm"), kb.val("ev")

    @p.on("tensor")
    def _(e):
        for tc in range(NCH):
            if tc >= 1:
                e.wait_ge(sev, ev0 + tc)
            for j in range(KD):
                e.transpose(pst[:, j * 128:(j + 1) * 128],
                            src_tm[:, tc * 1024 + j * 128:tc * 1024 + (j + 1) * 128],
                            eye_s[:, :]).then_inc(smm, 1)
    kb.bump("mm", NCH * KD)

    @p.on("vector")
    def _(e):
        for tc in range(NCH):
            e.wait_ge(smm, mm0 + KD * (tc + 1))
            dst = bass.AP(dst_xnt, tc * 128, [[KD * T, 128], [T, KD], [1, 128]])
            e.tensor_copy(dst, pst[:, :].rearrange("p (j f) -> p j f", j=KD)
                          ).then_inc(sev, 1)
    kb.bump("ev", NCH)

    p.run()


def _qsc_from_max(kb, tag, sources):
    dma, smm, sev, scc = (kb.sem("dma"), kb.sem("mm"), kb.sem("ev"),
                          kb.sem("cc"))
    psum = [kb.t[f"psb{i}"] for i in range(6)]
    qsc = kb.t["qsc"]
    scrrow = kb.t["wb0"].bitcast(F32)
    scrrow2 = kb.t["wb1"].bitcast(F32)

    p = Phase(kb, tag + "_r1")

    @p.on("vector")
    def _(e):
        for i, (ap, qcol, qmax) in enumerate(sources):
            e.tensor_reduce(scrrow[:, i:i + 1], ap, AX.X, ALU.max,
                            apply_absolute_value=True)

    p.run()
    p = Phase(kb, tag + "_hop")

    @p.on("sync")
    def _(e):
        n = len(sources)
        for i in range(n):
            e.dma_start(out=bass.AP(kb.t["hop_d"], i * 128, [[1, 128]]),
                        in_=scrrow[:, i:i + 1]).then_inc(dma, 16)
            kb.bump("dma", 16)
        e.wait_ge(dma, kb.val("dma"))
        for i in range(n):
            e.dma_start(out=scrrow[0:1, 128 + i * 128:128 + (i + 1) * 128],
                        in_=bass.AP(kb.t["hop_d"], i * 128, [[128, 1], [1, 128]])
                        ).then_inc(dma, 16)
            kb.bump("dma", 16)
        e.wait_ge(dma, kb.val("dma"))

    p.run()
    p = Phase(kb, tag + "_r2")

    @p.on("vector")
    def _(e):
        for i in range(len(sources)):
            e.tensor_reduce(scrrow2[0:1, i:i + 1],
                            scrrow[0:1, 128 + i * 128:128 + (i + 1) * 128],
                            AX.X, ALU.max)

    p.run()
    p = Phase(kb, tag + "_st")

    @p.on("sync")
    def _(e):
        e.dma_start(out=kb.t["mx_loc"][0:1, 0:len(sources)],
                    in_=scrrow2[0:1, 0:len(sources)]).then_inc(dma, 16)
        kb.bump("dma", 16)
        e.wait_ge(dma, kb.val("dma"))

    p.run()
    p = Phase(kb, tag + "_ar")

    @p.on("gpsimd")
    def _(e):
        e.collective_compute(
            "AllReduce", ALU.max, replica_groups=[list(range(N_CORES))],
            ins=[kb.t["mx_loc"].ap().opt()], outs=[kb.t["mx_red"].ap().opt()],
        ).then_inc(scc, 1)
        kb.bump("cc", 1)

    p.run()
    p = Phase(kb, tag + "_ld")

    @p.on("sync")
    def _(e):
        e.wait_ge(scc, kb.val("cc"))
        e.dma_start(out=scrrow2[0:1, 8:8 + len(sources)],
                    in_=kb.t["mx_red"][0:1, 0:len(sources)]).then_inc(dma, 16)
        kb.bump("dma", 16)
        e.wait_ge(dma, kb.val("dma"))

    p.run()
    p = Phase(kb, tag + "_bc")
    mm0 = kb.val("mm")

    @p.on("tensor")
    def _(e):
        for i in range(len(sources)):
            e.matmul(psum[i][:, 0:1], kb.t["ones_all"][0:1, 0:128],
                     scrrow2[0:1, 8 + i:9 + i], start=True,
                     stop=True).then_inc(smm, 1)
    kb.bump("mm", len(sources))

    p.run()
    p = Phase(kb, tag + "_sc")

    @p.on("vector")
    def _(e):
        for i, (ap, qcol, qmax) in enumerate(sources):
            ins = e.tensor_scalar(qsc[:, qcol:qcol + 1], psum[i][:, 0:1],
                                  1.0 / qmax, 1e-12, ALU.mult, ALU.max)
            ins.then_inc(sev, 1)
            kb.bump("ev", 1)
            e.wait_ge(sev, kb.val("ev"))
            e.reciprocal(qsc[:, qcol + 1:qcol + 2], qsc[:, qcol:qcol + 1]
                         ).then_inc(sev, 1)
            kb.bump("ev", 1)
            e.wait_ge(sev, kb.val("ev"))

    p.run()


def emit_L1_ffn_sublayer(kb, tag):
    dma, smm, sev, sea, scc = (kb.sem("dma"), kb.sem("mm"), kb.sem("ev"),
                               kb.sem("ea"), kb.sem("cc"))
    stream, xnt, auxb, aux = kb.t["stream"], kb.t["xnt"], kb.t["auxb"], kb.t["aux"]
    wbig = kb.t["wbig"]
    psum = [kb.t[f"psb{i}"] for i in range(6)]
    qsc = kb.t["qsc"]
    ustage = kb.t["phi_st"]
    s_rs, s_nmr, s_t0 = kb.t["s_rs"], kb.t["s_nmr"], kb.t["s_t0"]
    QMAXA = 2.0 ** 15 - 1.0

    emit_ln_stats(kb, stream, tag + "_ln")

    p = Phase(kb, tag + "_xmax")
    ea0, ev0 = kb.val("ea"), kb.val("ev")

    @p.on("scalar")
    def _(e):
        for tc in range(NCH):
            if tc >= 1:
                e.wait_ge(sev, ev0 + tc)
            e.activation(ustage[:, 0:1024], stream[:, tc * D:(tc + 1) * D],
                         AF.Identity, bias=s_nmr[:, tc:tc + 1],
                         scale=s_rs[:, tc:tc + 1]).then_inc(sea, 1)
    kb.bump("ea", NCH)

    @p.on("vector")
    def _(e):
        for tc in range(NCH):
            e.wait_ge(sea, ea0 + tc + 1)
            e.tensor_reduce(s_t0[:, tc:tc + 1], ustage[:, 0:1024], AX.X,
                            ALU.max, apply_absolute_value=True).then_inc(sev, 1)
    kb.bump("ev", NCH)

    p.run()
    _qsc_from_max(kb, tag + "_a1", [(s_t0[:, :], 0, QMAXA)])

    xq_tm = auxb[:, 0:16384]
    p = Phase(kb, tag + "_xq")
    ea0, ev0 = kb.val("ea"), kb.val("ev")

    @p.on("scalar")
    def _(e):
        for tc in range(NCH):
            if tc >= 1:
                e.wait_ge(sev, ev0 + 2 * tc)
            e.activation(ustage[:, 0:1024], stream[:, tc * D:(tc + 1) * D],
                         AF.Identity, bias=s_nmr[:, tc:tc + 1],
                         scale=s_rs[:, tc:tc + 1]).then_inc(sea, 1)
    kb.bump("ea", NCH)

    @p.on("vector")
    def _(e):
        for tc in range(NCH):
            e.wait_ge(sea, ea0 + tc + 1)
            e.tensor_scalar(ustage[:, 0:1024], ustage[:, 0:1024],
                                         qsc[:, 1:2], MAGIC, ALU.mult,
                                         ALU.add).then_inc(sev, 1)
            fence(e, 12)
            e.tensor_scalar(xq_tm[:, tc * 1024:(tc + 1) * 1024],
                            ustage[:, 0:1024], MAGIC, qsc[:, 0:1],
                            ALU.subtract, ALU.mult).then_inc(sev, 1)
    kb.bump("ev", 2 * NCH)

    p.run()
    emit_fm_transpose(kb, tag + "_xqt", xq_tm, xnt)

    for nm, ppoff, dnm, mxc in [("up", kb.PPD, "up_d", 0),
                                ("gate", kb.PPC, "gate_d", 96)]:
        _l1ffn_proj_spill(kb, f"{tag}_{nm}", ppoff, dnm, mxc)
    _qsc_from_max(kb, tag + "_a2",
                  [(kb.t["phi_st"][:, 0:96], 2, QMAXA),
                   (kb.t["phi_st"][:, 96:192], 4, QMAXA)])

    p = Phase(kb, tag + "_sb")

    @p.on("vector")
    def _(e):
        e.tensor_scalar(qsc[:, 8:9], qsc[:, 4:5], -MAGIC, None, ALU.mult
                        ).then_inc(sev, 1)
        kb.bump("ev", 1)
        e.wait_ge(sev, kb.val("ev"))

    p.run()

    _l1ffn_stageB(kb, tag + "_gs")
    _qsc_from_max(kb, tag + "_a3", [(kb.t["phi_st"][:, 0:24], 6, QMAXA)])

    _l1ffn_stageC(kb, tag + "_pr")
    _qsc_from_max(kb, tag + "_a4", [(kb.t["phi_st"][:, 0:24], 10, QMAXA)])

    _l1ffn_stageD(kb, tag + "_oq")

    _l1ffn_down(kb, tag + "_dn")


def _l1ffn_proj_spill(kb, tag, ppoff, dram_name, mxcol0):
    dma, dmag, smm, sev = (kb.sem("dma"), kb.sem("dmag"), kb.sem("mm"),
                           kb.sem("ev"))
    xnt, aux = kb.t["xnt"], kb.t["aux"]
    wbufs = [kb.t["wb0"], kb.t["wb1"]]
    psum = [kb.t[f"psb{i}"] for i in range(6)]
    mxb = kb.t["phi_st"]
    pp_out = kb.t["pp_out"]
    p = Phase(kb, tag)
    dm0, dg0, mm0, ev0 = (kb.val("dma"), kb.val("dmag"), kb.val("mm"),
                          kb.val("ev"))

    @p.on("sync")
    def _(e):
        for m in range(24):
            buf = wbufs[m % 2]
            if m >= 2:
                e.wait_ge(smm, mm0 + (m - 1) * 4)
            c, ml = m // 3, m % 3
            e.dma_start(out=buf[:, :],
                        in_=bass.AP(pp_out, c * kb.PPTOT + ppoff + ml * 131072,
                                    [[1024, 128], [1, 1024]])).then_inc(dma, 16)
            kb.bump("dma", 16)

    @p.on("gpsimd")
    def _(e):
        for n in range(96):
            e.wait_ge(sev, ev0 + n * 2 + 1)
            m, t4 = n // 4, n % 4
            e.dma_start(out=bass.AP(kb.t[dram_name], m * 262144 + t4 * 512,
                                    [[2048, 128], [1, 512]]),
                        in_=aux[:, (n % 4) * 512:(n % 4 + 1) * 512]
                        ).then_inc(dmag, 16)
            kb.bump("dmag", 16)
        e.wait_ge(dmag, kb.val("dmag"))

    @p.on("tensor")
    def _(e):
        n = 0
        for m in range(24):
            buf = wbufs[m % 2]
            e.wait_ge(dma, dm0 + (m + 1) * 16)
            for t4 in range(4):
                bank = psum[n % 4]
                if n >= 4:
                    e.wait_ge(sev, ev0 + (n - 4) * 2 + 2)
                last = None
                for k in range(8):
                    last = e.matmul(bank[:, :], buf[:, k * 128:(k + 1) * 128],
                                    xnt[:, k * T + t4 * 512:k * T + (t4 + 1) * 512],
                                    start=(k == 0), stop=(k == 7))
                last.then_inc(smm, 1)
                n += 1
    kb.bump("mm", 96)

    @p.on("vector")
    def _(e):
        n = 0
        for m in range(24):
            for t4 in range(4):
                e.wait_ge(smm, mm0 + n + 1)
                bank = psum[n % 4]
                sbuf = aux[:, (n % 4) * 512:(n % 4 + 1) * 512]
                if n >= 4:
                    e.wait_ge(dmag, dg0 + (n - 3) * 16)
                e.tensor_copy(sbuf, bank[:, :]).then_inc(sev, 1)
                e.tensor_reduce(mxb[:, mxcol0 + n:mxcol0 + n + 1], bank[:, :],
                                AX.X, ALU.max,
                                apply_absolute_value=True).then_inc(sev, 1)
                n += 1
    kb.bump("ev", 192)

    p.run()


def _l1ffn_stageB(kb, tag):
    dma, dmag, sev, sea = (kb.sem("dma"), kb.sem("dmag"), kb.sem("ev"),
                           kb.sem("ea"))
    aux = kb.t["aux"]
    qsc = kb.t["qsc"]
    mxb = kb.t["phi_st"]
    gb = [aux[:, 2048:4096], aux[:, 4096:6144]]
    sigb = aux[:, 6144:8192]
    gsb = [aux[:, 8192:10240], aux[:, 10240:12288]]
    p = Phase(kb, tag)
    dm0, dg0, ev0, ea0 = (kb.val("dma"), kb.val("dmag"), kb.val("ev"),
                          kb.val("ea"))

    @p.on("sync")
    def _(e):
        for m in range(24):
            if m >= 2:
                e.wait_ge(sev, ev0 + (m - 1) * 5)
            e.dma_start(out=gb[m % 2][:, :],
                        in_=bass.AP(kb.t["gate_d"], m * 262144,
                                    [[2048, 128], [1, 2048]])).then_inc(dma, 16)
            kb.bump("dma", 16)

    @p.on("gpsimd")
    def _(e):
        for m in range(24):
            e.wait_ge(sev, ev0 + m * 5 + 5)
            e.dma_start(out=bass.AP(kb.t["gs_d"], m * 262144,
                                    [[2048, 128], [1, 2048]]),
                        in_=gsb[m % 2][:, :]).then_inc(dmag, 16)
            kb.bump("dmag", 16)
        e.wait_ge(dmag, kb.val("dmag"))

    @p.on("vector")
    def _(e):
        for m in range(24):
            g = gb[m % 2]
            gs = gsb[m % 2]
            e.wait_ge(dma, dm0 + (m + 1) * 16)
            if m >= 2:
                e.wait_ge(dmag, dg0 + (m - 1) * 16)
            e.tensor_scalar(g[:, :], g[:, :], qsc[:, 5:6], MAGIC,
                                         ALU.mult, ALU.add).then_inc(sev, 1)
            fence(e, 12)
            e.wait_ge(sea, ea0 + m + 1)
            e.tensor_scalar(sigb[:, :], sigb[:, :], 65536.0,
                                         MAGIC, ALU.mult,
                                         ALU.add).then_inc(sev, 1)
            fence(e, 12)
            e.tensor_scalar(sigb[:, :], sigb[:, :],
                                         MAGIC + 65535.0, MAGIC, ALU.min,
                                         ALU.subtract).then_inc(sev, 1)
            fence(e, 12)
            e.tensor_scalar(g[:, :], g[:, :], MAGIC, qsc[:, 4:5],
                                         ALU.subtract,
                                         ALU.mult).then_inc(sev, 1)
            fence(e, 12)
            e.scalar_tensor_tensor(gs[:, :], sigb[:, :],
                                                1.0 / 65536.0, g[:, :],
                                                ALU.mult,
                                                ALU.mult).then_inc(sev, 1)
            fence(e, 12)
            e.tensor_reduce(mxb[:, m:m + 1], gs[:, :], AX.X, ALU.max,
                            apply_absolute_value=True)
    kb.bump("ev", 24 * 5)

    @p.on("scalar")
    def _(e):
        for m in range(24):
            g = gb[m % 2]
            e.wait_ge(sev, ev0 + m * 5 + 1)
            e.activation(sigb[:, :], g[:, :], AF.Sigmoid, bias=qsc[:, 8:9],
                         scale=qsc[:, 4:5]).then_inc(sea, 1)
    kb.bump("ea", 24)

    p.run()


def _l1ffn_stageC(kb, tag):
    dma, dmag, sev = kb.sem("dma"), kb.sem("dmag"), kb.sem("ev")
    aux = kb.t["aux"]
    qsc = kb.t["qsc"]
    mxb = kb.t["phi_st"]
    ub = [aux[:, 2048:4096], aux[:, 4096:6144]]
    gb = [aux[:, 6144:8192], aux[:, 8192:10240]]
    pb = [aux[:, 10240:12288], aux[:, 12288:14336]]
    p = Phase(kb, tag)
    dm0, dg0, ev0 = kb.val("dma"), kb.val("dmag"), kb.val("ev")

    @p.on("sync")
    def _(e):
        for m in range(24):
            if m >= 2:
                e.wait_ge(sev, ev0 + (m - 1) * 5)
            e.dma_start(out=ub[m % 2][:, :],
                        in_=bass.AP(kb.t["up_d"], m * 262144,
                                    [[2048, 128], [1, 2048]])).then_inc(dma, 16)
            kb.bump("dma", 16)
            e.dma_start(out=gb[m % 2][:, :],
                        in_=bass.AP(kb.t["gs_d"], m * 262144,
                                    [[2048, 128], [1, 2048]])).then_inc(dma, 16)
            kb.bump("dma", 16)

    @p.on("gpsimd")
    def _(e):
        for m in range(24):
            e.wait_ge(sev, ev0 + m * 5 + 5)
            e.dma_start(out=bass.AP(kb.t["prod_d"], m * 262144,
                                    [[2048, 128], [1, 2048]]),
                        in_=pb[m % 2][:, :]).then_inc(dmag, 16)
            kb.bump("dmag", 16)
        e.wait_ge(dmag, kb.val("dmag"))

    @p.on("vector")
    def _(e):
        for m in range(24):
            u, g, pr = ub[m % 2], gb[m % 2], pb[m % 2]
            e.wait_ge(dma, dm0 + (m + 1) * 32)
            if m >= 2:
                e.wait_ge(dmag, dg0 + (m - 1) * 16)
            e.tensor_scalar(u[:, :], u[:, :], qsc[:, 3:4],
                                         MAGIC, ALU.mult,
                                         ALU.add).then_inc(sev, 1)
            fence(e, 12)
            e.tensor_scalar(u[:, :], u[:, :], MAGIC, qsc[:, 2:3],
                                         ALU.subtract,
                                         ALU.mult).then_inc(sev, 1)
            fence(e, 12)
            e.tensor_scalar(g[:, :], g[:, :], qsc[:, 7:8], MAGIC,
                                         ALU.mult, ALU.add).then_inc(sev, 1)
            fence(e, 12)
            e.tensor_scalar(g[:, :], g[:, :], MAGIC, qsc[:, 6:7],
                                         ALU.subtract,
                                         ALU.mult).then_inc(sev, 1)
            fence(e, 12)
            e.tensor_tensor(pr[:, :], g[:, :], u[:, :], ALU.mult
                                         ).then_inc(sev, 1)
            fence(e, 12)
            e.tensor_reduce(mxb[:, m:m + 1], pr[:, :], AX.X, ALU.max,
                            apply_absolute_value=True)
    kb.bump("ev", 24 * 5)

    p.run()


def _l1ffn_stageD(kb, tag):
    dma, sev = kb.sem("dma"), kb.sem("ev")
    aux, auxb = kb.t["aux"], kb.t["auxb"]
    xnt, wbig = kb.t["xnt"], kb.t["wbig"]
    qsc = kb.t["qsc"]
    pb = [aux[:, 8192:10240], aux[:, 10240:12288]]
    p = Phase(kb, tag)
    dm0, ev0 = kb.val("dma"), kb.val("ev")

    def oslot(m):
        if m < 8:
            return xnt[:, m * 2048:(m + 1) * 2048]
        if m < 16:
            return wbig[:, (m - 8) * 2048:(m - 7) * 2048]
        return auxb[:, (m - 16) * 2048:(m - 15) * 2048]

    @p.on("sync")
    def _(e):
        for m in range(24):
            if m >= 2:
                e.wait_ge(sev, ev0 + (m - 1) * 2)
            e.dma_start(out=pb[m % 2][:, :],
                        in_=bass.AP(kb.t["prod_d"], m * 262144,
                                    [[2048, 128], [1, 2048]])).then_inc(dma, 16)
            kb.bump("dma", 16)

    @p.on("vector")
    def _(e):
        for m in range(24):
            pr = pb[m % 2]
            e.wait_ge(dma, dm0 + (m + 1) * 16)
            e.tensor_scalar(pr[:, :], pr[:, :], qsc[:, 11:12],
                                         MAGIC, ALU.mult,
                                         ALU.add).then_inc(sev, 1)
            fence(e, 12)
            e.tensor_scalar(oslot(m), pr[:, :], MAGIC, qsc[:, 10:11],
                            ALU.subtract, ALU.mult).then_inc(sev, 1)
    kb.bump("ev", 24 * 2)

    p.run()


def _l1ffn_down(kb, tag):
    dma, smm, sev = kb.sem("dma"), kb.sem("mm"), kb.sem("ev")
    stream, aux, auxb = kb.t["stream"], kb.t["aux"], kb.t["auxb"]
    xnt, wbig = kb.t["xnt"], kb.t["wbig"]
    wbufs = [kb.t["wb0"], kb.t["wb1"]]
    psum = [kb.t[f"psb{i}"] for i in range(6)]

    def oslot_tile(m, tau):
        if m < 8:
            base = xnt
            c0 = m * 2048
        elif m < 16:
            base = wbig
            c0 = (m - 8) * 2048
        else:
            base = auxb
            c0 = (m - 16) * 2048
        return base[:, c0 + tau * 128:c0 + (tau + 1) * 128]

    groups = [[0, 1, 2], [3, 4, 5], [6, 7, 8], [9, 10, 11], [12, 13], [14, 15]]
    for gi, taus in enumerate(groups):
        p = Phase(kb, f"{tag}_{gi}")
        dm0, mm0 = kb.val("dma"), kb.val("mm")
        nt = len(taus)

        @p.on("sync")
        def _(e, taus=taus, mm0=mm0):
            for m in range(24):
                buf = wbufs[m % 2]
                if m >= 2:
                    e.wait_ge(smm, mm0 + m - 1)
                e.dma_start(out=buf[:, :],
                            in_=bass.AP(kb.t["dwTd"], m * 131072,
                                        [[1024, 128], [1, 1024]])
                            ).then_inc(dma, 16)
                kb.bump("dma", 16)

        @p.on("tensor")
        def _(e, taus=taus, dm0=dm0):
            for m in range(24):
                buf = wbufs[m % 2]
                e.wait_ge(dma, dm0 + (m + 1) * 16)
                last = None
                for ti, tau in enumerate(taus):
                    for dh in range(2):
                        bank = psum[ti * 2 + dh]
                        last = e.matmul(bank[:, :], oslot_tile(m, tau),
                                        buf[:, dh * 512:(dh + 1) * 512],
                                        start=(m == 0), stop=(m == 23))
                last.then_inc(smm, 1)
        kb.bump("mm", 24)

        @p.on("vector")
        def _(e, taus=taus, mm0=mm0):
            e.wait_ge(smm, mm0 + 24)
            for ti, tau in enumerate(taus):
                for dh in range(2):
                    bank = psum[ti * 2 + dh]
                    sl = stream[:, tau * D + dh * 512:tau * D + (dh + 1) * 512]
                    e.scalar_tensor_tensor(sl, bank[:, :], 0.5, sl, ALU.mult,
                                           ALU.add)

        p.run()



def emit_sl1_v(kb, tag, lhs_fn, wsrc_fn, wbig, QFB):
    dma, dmag, smm, sev = (kb.sem("dma"), kb.sem("dmag"), kb.sem("mm"),
                           kb.sem("ev"))
    wbufs = [kb.t["wb0"], kb.t["wb1"]]
    psum = [kb.t[f"psb{i}"] for i in range(6)]
    vstg = [wbig[:, QFB + i * 1024:QFB + (i + 1) * 1024] for i in range(4)]
    groups = [[0, 1, 2], [3, 4, 5], [6, 7, 8], [9, 10, 11], [12, 13], [14, 15]]
    dm_start = kb.val("dma")
    n_kH = 8
    for gi, taus in enumerate(groups):
        p = Phase(kb, f"{tag}_{gi}")
        dg0, mm0, ev0 = kb.val("dmag"), kb.val("mm"), kb.val("ev")
        nt = len(taus)

        @p.on("gpsimd")
        def _(e, taus=taus, mm0=mm0):
            for kH in range(n_kH):
                buf = wbufs[kH % 2]
                if kH >= 2:
                    e.wait_ge(smm, mm0 + kH - 1)
                e.dma_start(out=buf[:, :], in_=wsrc_fn(kH)).then_inc(dmag, 16)
                kb.bump("dmag", 16)

        @p.on("tensor")
        def _(e, taus=taus, dg0=dg0):
            for kH in range(n_kH):
                buf = wbufs[kH % 2]
                e.wait_ge(dmag, dg0 + (kH + 1) * 16)
                last = None
                for ti, tau in enumerate(taus):
                    for dh in range(2):
                        bank = psum[ti * 2 + dh]
                        last = e.matmul(bank[:, :], lhs_fn(kH, tau),
                                        buf[:, dh * 512:(dh + 1) * 512],
                                        start=(kH == 0), stop=(kH == n_kH - 1))
                last.then_inc(smm, 1)
        kb.bump("mm", n_kH)

        @p.on("vector")
        def _(e, taus=taus, mm0=mm0):
            e.wait_ge(smm, mm0 + n_kH)
            for ti, tau in enumerate(taus):
                for dh in range(2):
                    bank = psum[ti * 2 + dh]
                    if tau >= 4:
                        e.wait_ge(dma, dm_start + (tau - 3) * 16)
                    ins = e.tensor_copy(vstg[tau % 4][:, dh * 512:(dh + 1) * 512],
                                        bank[:, :])
                    if dh == 1:
                        ins.then_inc(sev, 1)
                        kb.bump("ev", 1)

        @p.on("sync")
        def _(e, taus=taus, ev0=ev0):
            for ti, tau in enumerate(taus):
                e.wait_ge(sev, ev0 + ti + 1)
                e.dma_start(out=bass.AP(kb.t["v_stage"], tau * 131072,
                                        [[1024, 128], [1, 1024]]),
                            in_=vstg[tau % 4][:, :]).then_inc(dma, 16)
                kb.bump("dma", 16)

        p.run()
    p = Phase(kb, f"{tag}_fin")

    @p.on("sync")
    def _(e):
        e.wait_ge(dma, kb.val("dma"))

    p.run()


def emit_sl1_kfkv(kb, tag):
    dma, smm, sev, sea = (kb.sem("dma"), kb.sem("mm"), kb.sem("ev"),
                          kb.sem("ea"))
    wbig, auxb = kb.t["wbig"], kb.t["auxb"]
    psum = [kb.t[f"psb{i}"] for i in range(6)]
    scr = kb.t["xntf"]
    phist = kb.t["phi_st"]
    onesc = kb.t["onesc_b"]
    vh = [kb.t["wb0"], kb.t["wb1"]]
    PRJ, KFH = 0, 4096

    p = Phase(kb, tag + "_kfkv")
    dm0, mm0, ev0, ea0 = (kb.val("dma"), kb.val("mm"), kb.val("ev"),
                          kb.val("ea"))
    KF_MM, KV_INC = 8, 2
    V_PER_H = 8 * 2 + 2
    A_PER_H = 8

    @p.on("sync")
    def _(e):
        for h in range(16):
            if h >= 2:
                e.wait_ge(smm, mm0 + (h - 1) * (KF_MM + KV_INC))
            e.dma_start(out=vh[h % 2][:, :],
                        in_=bass.AP(kb.t["v_stage"], 64 * h,
                                    [[1024, 128], [131072, 16], [1, 64]]),
                        allow_slow_non_contiguous=True).then_inc(dma, 16)
            kb.bump("dma", 16)

    @p.on("tensor")
    def _(e):
        for h in range(16):
            base = (h % 2) * 64
            for tcp in range(8):
                bank = psum[tcp % 2]
                if h > 0 or tcp >= 2:
                    gp = h * 8 + tcp
                    if gp >= 2:
                        e.wait_ge(sev, ev0 + (gp - 2) // 8 * V_PER_H
                                  + _p3_count((gp - 2) % 8))
                last = None
                for u in range(2):
                    tc = tcp * 2 + u
                    lhsT = auxb[base:base + 64,
                                16384 + (h // 2) * 2048 + tc * 128:
                                16384 + (h // 2) * 2048 + (tc + 1) * 128]
                    last = e.matmul(bank[:, u * 256:(u + 1) * 256], lhsT,
                                    wbig[base:base + 64,
                                         PRJ + 256 * h:PRJ + 256 * (h + 1)],
                                    start=(u == 0), stop=(u == 1))
                last.then_inc(smm, 1)
            e.wait_ge(sea, ea0 + (h + 1) * A_PER_H)
            e.wait_ge(sev, ev0 + h * V_PER_H + 16)
            e.wait_ge(dma, dm0 + (h + 1) * 16)
            for fc in range(2):
                bank = psum[2 + (h % 2) * 2 + fc]
                if h >= 2:
                    e.wait_ge(sev, ev0 + (h - 2) * V_PER_H + 16 + 2)
                last = None
                for tc in range(16):
                    lhsT = wbig[:, KFH + 256 * tc + 128 * fc:
                                KFH + 256 * tc + 128 * (fc + 1)]
                    last = e.matmul(bank[:, 0:64], lhsT,
                                    vh[h % 2][:, 64 * tc:64 * (tc + 1)],
                                    start=(tc == 0), stop=False)
                    last = e.matmul(bank[:, 64:65], lhsT, onesc[:, 0:1],
                                    start=False, stop=(tc == 15))
                last.then_inc(smm, 1)
    kb.bump("mm", 16 * (KF_MM + KV_INC))

    @p.on("vector")
    def _(e):
        for h in range(16):
            for tcp in range(8):
                gp = h * 8 + tcp
                bank = psum[tcp % 2]
                e.wait_ge(smm, mm0 + h * (KF_MM + KV_INC) + tcp + 1)
                e.tensor_scalar(phist[:, (tcp % 2) * 512:(tcp % 2 + 1) * 512],
                                bank[:, :], 0.0, None,
                                ALU.min).then_inc(sev, 1)
                e.wait_ge(sea, ea0 + h * A_PER_H + tcp + 1)
                e.scalar_tensor_tensor(
                    wbig[:, 4096 + 512 * tcp:4096 + 512 * (tcp + 1)],
                    bank[:, :], 0.0,
                    phist[:, (tcp % 2) * 512:(tcp % 2 + 1) * 512],
                    ALU.max, ALU.add).then_inc(sev, 1)
            for fc in range(2):
                bank = psum[2 + (h % 2) * 2 + fc]
                e.wait_ge(smm, mm0 + h * (KF_MM + KV_INC) + KF_MM + fc + 1)
                e.tensor_copy(scr[:, 65 * (2 * h + fc):65 * (2 * h + fc) + 65],
                              bank[:, 0:65]).then_inc(sev, 1)
    kb.bump("ev", 16 * V_PER_H)

    @p.on("scalar")
    def _(e):
        for h in range(16):
            for tcp in range(8):
                e.wait_ge(sev, ev0 + h * V_PER_H + _p1_count(tcp))
                half = phist[:, (tcp % 2) * 512:(tcp % 2 + 1) * 512]
                e.activation(half, half, AF.Exp, bias=0.0,
                             scale=1.0).then_inc(sea, 1)
    kb.bump("ea", 16 * A_PER_H)

    p.run()


def _p1_count(tcp):
    return 2 * tcp + 1


def _p3_count(tcp):
    return 2 * tcp + 2


def emit_sl1_qfattn(kb, tag, ar_cc):
    dma, smm, sev, sea, scc = (kb.sem("dma"), kb.sem("mm"), kb.sem("ev"),
                               kb.sem("ea"), kb.sem("cc"))
    wbig, auxb, xnt = kb.t["wbig"], kb.t["auxb"], kb.t["xnt"]
    psum = [kb.t[f"psb{i}"] for i in range(6)]
    scr = kb.t["xntf"]
    dstage = kb.t["wb0"].bitcast(F32)[0:1, 0:512]
    astage = kb.t["wb1"].bitcast(F32)
    phist = kb.t["phi_st"]
    PRJ, QFB, KVB = 0, 8192, 12288

    p = Phase(kb, tag + "_ld")

    @p.on("sync")
    def _(e):
        e.wait_ge(scc, ar_cc)
        e.dma_start(out=scr[:, 0:2080], in_=kb.t["kv_red"][:, :]).then_inc(dma, 16)
        kb.bump("dma", 16)
        e.wait_ge(dma, kb.val("dma"))

    p.run()
    p = Phase(kb, tag + "_cast")

    @p.on("vector")
    def _(e):
        e.tensor_copy(wbig[:, KVB:KVB + 2080], scr[:, 0:2080])

    p.run()

    p = Phase(kb, tag + "_main")
    mm0, ev0, ea0 = kb.val("mm"), kb.val("ev"), kb.val("ea")
    T_PER_H, V_PER_H, A_PER_H = 20, 28, 8

    @p.on("tensor")
    def _(e):
        for h in range(16):
            base = (h % 2) * 64
            hm, ht = mm0 + h * T_PER_H, h * T_PER_H
            for i in range(8):
                fh, tt = i // 4, i % 4
                bank = psum[i % 2]
                gq = h * 8 + i
                if gq >= 2:
                    ph, pi = (gq - 2) // 8, (gq - 2) % 8
                    e.wait_ge(sev, ev0 + ph * V_PER_H + _p3_count(pi))
                e.matmul(bank[:, :],
                         wbig[base:base + 64, PRJ + 256 * h + 128 * fh:
                              PRJ + 256 * h + 128 * (fh + 1)],
                         auxb[base:base + 64,
                              (h // 2) * 2048 + tt * 512:(h // 2) * 2048 + (tt + 1) * 512],
                         start=True, stop=True).then_inc(smm, 1)
            for tt in range(4):
                e.wait_ge(sev, ev0 + h * V_PER_H + 16)
                A = psum[2 + tt % 2]
                Dk = psum[4]
                gidx = h * 4 + tt
                if gidx >= 2:
                    pg, pt = (gidx - 2) // 4, (gidx - 2) % 4
                    e.wait_ge(sev, ev0 + pg * V_PER_H + 16 + (pt + 1) * 3)
                if gidx >= 1:
                    pg2, pt2 = (gidx - 1) // 4, (gidx - 1) % 4
                    e.wait_ge(sev, ev0 + pg2 * V_PER_H + 16 + pt2 * 3 + 2)
                last = None
                for fc in range(2):
                    last = e.matmul(A[base:base + 64, :],
                                    wbig[:, KVB + 65 * (2 * h + fc):
                                         KVB + 65 * (2 * h + fc) + 64],
                                    wbig[:, QFB + fc * 2048 + tt * 512:
                                         QFB + fc * 2048 + (tt + 1) * 512],
                                    start=(fc == 0), stop=(fc == 1))
                last.then_inc(smm, 1)
                last = None
                for fc in range(2):
                    last = e.matmul(Dk[0:1, :],
                                    wbig[:, KVB + 65 * (2 * h + fc) + 64:
                                         KVB + 65 * (2 * h + fc) + 65],
                                    wbig[:, QFB + fc * 2048 + tt * 512:
                                         QFB + fc * 2048 + (tt + 1) * 512],
                                    start=(fc == 0), stop=(fc == 1))
                last.then_inc(smm, 1)
                e.wait_ge(sev, ev0 + h * V_PER_H + 16 + tt * 3 + 2)
                e.matmul(psum[5][base:base + 64, :],
                         kb.t["ones_all"][0:1, 0:64], dstage,
                         start=True, stop=True).then_inc(smm, 1)
    kb.bump("mm", 16 * T_PER_H)

    @p.on("vector")
    def _(e):
        for h in range(16):
            base = (h % 2) * 64
            for i in range(8):
                fh, tt = i // 4, i % 4
                bank = psum[i % 2]
                e.wait_ge(smm, mm0 + h * T_PER_H + i + 1)
                e.tensor_scalar(phist[:, (i % 2) * 512:(i % 2 + 1) * 512],
                                bank[:, :], 0.0, None, ALU.min).then_inc(sev, 1)
                e.wait_ge(sea, ea0 + h * A_PER_H + i + 1)
                e.scalar_tensor_tensor(
                    wbig[:, QFB + fh * 2048 + tt * 512:QFB + fh * 2048 + (tt + 1) * 512],
                    bank[:, :], 0.0,
                    phist[:, (i % 2) * 512:(i % 2 + 1) * 512],
                    ALU.max, ALU.add).then_inc(sev, 1)
            for tt in range(4):
                e.wait_ge(smm, mm0 + h * T_PER_H + 8 + tt * 3 + 2)
                e.tensor_scalar(dstage, psum[4][0:1, :], 1e-8,
                                None, ALU.add).then_inc(sev, 1)
                e.wait_ge(sev, ev0 + h * V_PER_H + 16 + tt * 3 + 1)
                e.reciprocal(dstage, dstage).then_inc(sev, 1)
                e.wait_ge(smm, mm0 + h * T_PER_H + 8 + tt * 3 + 3)
                flush(kb, e, e.tensor_copy(astage[base:base + 64, 0:512],
                                           psum[2 + tt % 2][base:base + 64, :]))
                e.tensor_tensor(
                    xnt[base:base + 64,
                        (h // 2) * 2048 + tt * 512:(h // 2) * 2048 + (tt + 1) * 512],
                    astage[base:base + 64, 0:512],
                    psum[5][base:base + 64, :], ALU.mult).then_inc(sev, 1)
    kb.bump("ev", 16 * V_PER_H)

    @p.on("scalar")
    def _(e):
        for h in range(16):
            for i in range(8):
                e.wait_ge(sev, ev0 + h * V_PER_H + _p1_count(i))
                half = phist[:, (i % 2) * 512:(i % 2 + 1) * 512]
                e.activation(half, half, AF.Exp, bias=0.0,
                             scale=1.0).then_inc(sea, 1)
    kb.bump("ea", 16 * A_PER_H)

    p.run()


def emit_sl1_ffn(kb, tag):
    dmag, smm, sev, sea = (kb.sem("dmag"), kb.sem("mm"), kb.sem("ev"),
                           kb.sem("ea"))
    wbig, xnt, aux = kb.t["wbig"], kb.t["xnt"], kb.t["aux"]
    wbufs = [kb.t["wb0"], kb.t["wb1"]]
    psum = [kb.t[f"psb{i}"] for i in range(6)]
    ab1_c = kb.t["ab1_c"]

    for ttQ in range(4):
        p = Phase(kb, f"{tag}_fh{ttQ}")
        dg0, mm0, ea0 = kb.val("dmag"), kb.val("mm"), kb.val("ea")

        @p.on("gpsimd")
        def _(e, ttQ=ttQ, mm0=mm0):
            for m in range(32):
                buf = wbufs[m % 2]
                if m >= 2:
                    e.wait_ge(smm, mm0 + m - 1)
                e.dma_start(out=bass.AP(buf, 0, [[1024, 128], [128, 8], [1, 128]]),
                            in_=bass.AP(kb.t["aw1T"], 128 * m,
                                        [[4096, 128], [128 * 4096, 8], [1, 128]])
                            ).then_inc(dmag, 16)
                kb.bump("dmag", 16)

        @p.on("tensor")
        def _(e, ttQ=ttQ, dg0=dg0, ea0=ea0):
            for m in range(32):
                buf = wbufs[m % 2]
                e.wait_ge(dmag, dg0 + (m + 1) * 16)
                bank = psum[m % 4]
                if m >= 4:
                    e.wait_ge(sea, ea0 + m - 3)
                last = None
                for k in range(8):
                    last = e.matmul(bank[:, :], buf[:, k * 128:(k + 1) * 128],
                                    xnt[:, k * T + ttQ * 512:k * T + (ttQ + 1) * 512],
                                    start=(k == 0), stop=(k == 7))
                last.then_inc(smm, 1)
        kb.bump("mm", 32)

        @p.on("scalar")
        def _(e, ttQ=ttQ, mm0=mm0):
            for m in range(32):
                e.wait_ge(smm, mm0 + m + 1)
                bank = psum[m % 4]
                e.activation(wbig[:, m * 512:(m + 1) * 512], bank[:, :],
                             AF.Gelu, bias=ab1_c[:, m:m + 1],
                             scale=1.0).then_inc(sea, 1)
        kb.bump("ea", 32)

        p.run()

        for gi, taus in enumerate([[0, 1], [2, 3]]):
            p = Phase(kb, f"{tag}_fd{ttQ}_{gi}")
            dg0, mm0, ev0 = kb.val("dmag"), kb.val("mm"), kb.val("ev")
            nt = len(taus)

            @p.on("gpsimd")
            def _(e, taus=taus, mm0=mm0):
                for kH in range(32):
                    buf = wbufs[kH % 2]
                    if kH >= 2:
                        e.wait_ge(smm, mm0 + kH - 1)
                    e.dma_start(out=buf[:, :],
                                in_=bass.AP(kb.t["aw2T"], 128 * kH * 1024,
                                            [[1024, 128], [1, 1024]])
                                ).then_inc(dmag, 16)
                    kb.bump("dmag", 16)

            @p.on("tensor")
            def _(e, taus=taus, ttQ=ttQ, dg0=dg0):
                for kH in range(32):
                    buf = wbufs[kH % 2]
                    e.wait_ge(dmag, dg0 + (kH + 1) * 16)
                    last = None
                    for ti, tau in enumerate(taus):
                        for dh in range(2):
                            bank = psum[ti * 2 + dh]
                            last = e.matmul(
                                bank[:, :],
                                wbig[:, kH * 512 + tau * 128:kH * 512 + (tau + 1) * 128],
                                buf[:, dh * 512:(dh + 1) * 512],
                                start=(kH == 0), stop=False)
                    last.then_inc(smm, 1)
                for ti, tau in enumerate(taus):
                    for dh in range(2):
                        bank = psum[ti * 2 + dh]
                        e.matmul(bank[:, :], kb.row_ones("ab2_r"),
                                 kb.row_ap("ab2_r", dh * 512, 512),
                                 start=False, stop=True).then_inc(smm, 1)
            kb.bump("mm", 32 + nt * 2)

            @p.on("vector")
            def _(e, taus=taus, ttQ=ttQ, mm0=mm0, nt=nt):
                e.wait_ge(smm, mm0 + 32 + nt * 2)
                for ti, tau in enumerate(taus):
                    tc = ttQ * 4 + tau
                    for dh in range(2):
                        bank = psum[ti * 2 + dh]
                        sl = aux[:, tc * D + dh * 512:tc * D + (dh + 1) * 512]
                        e.scalar_tensor_tensor(sl, bank[:, :], 1.0, sl,
                                               ALU.mult, ALU.add)

            p.run()



_SC = dict(zrow=0, CLS=2048, zer=4096, T_all=4224, totr=4480, Gr=4736,
           lam=4992, pre=5248, suf=5504, DF=5760, DE=6016, S=6272, Rb=7312,
           Sin=8352, Rin=9392, gat=10432, tmp=11472, psx=12512, attf=12544,
           attg=13824, dmat=15104, wwk=15120, den=15248, cew=15264,
           wfl=15360, mrow=15616, wwk2=15648, negT=15776)
_WB = dict(decw=0, ktm=128, kdf=1152, dfb=2176, sbf=2304, rbf=3344,
           att=4384, wwork=4608)


def emit_L1_attn_sublayer(kb, tag):
    SL4S = int(_os.environ.get("KMOD_SL4_STAGE", "99"))
    dma, dmag, smm, sev, sea, scc = (kb.sem("dma"), kb.sem("dmag"),
                                     kb.sem("mm"), kb.sem("ev"),
                                     kb.sem("ea"), kb.sem("cc"))
    stream, xnt, auxb, aux = (kb.t["stream"], kb.t["xnt"], kb.t["auxb"],
                              kb.t["aux"])
    wbig = kb.t["wbig"]
    psum = [kb.t[f"psb{i}"] for i in range(6)]

    emit_ln_stats(kb, stream, tag + "_ln0")
    p = Phase(kb, tag + "_xp")
    s_rs, s_nmr = kb.t["s_rs"], kb.t["s_nmr"]

    @p.on("scalar")
    def _(e):
        for tc in range(NCH):
            e.activation(aux[:, tc * D:(tc + 1) * D],
                         stream[:, tc * D:(tc + 1) * D], AF.Identity,
                         bias=s_nmr[:, tc:tc + 1], scale=s_rs[:, tc:tc + 1])

    @p.on("sync")
    def _(e):
        e.dma_start(out=kb.t["qsc"][0:16, 14:15],
                    in_=bass.AP(kb.t["decb"], 0, [[1, 16], [1, 1]]),
                    allow_slow_non_contiguous=True).then_inc(dma, 16)
        kb.bump("dma", 16)
        e.wait_ge(dma, kb.val("dma"))

    p.run()

    emit_ln_transpose(kb, aux, xnt, tag + "_ln1",
                      gamma_beta=(kb.t["ln1w_c"], kb.t["ln1b_c"]))

    p = Phase(kb, tag + "_sp")

    @p.on("vector")
    def _(e):
        e.tensor_scalar(kb.t["qsc"][0:16, 14:15], kb.t["qsc"][0:16, 14:15],
                        -1.0, None, ALU.mult)

    @p.on("sync")
    def _(e):
        for tc in range(NCH):
            e.dma_start(out=kb.t["stream_spill"][tc * 128:(tc + 1) * 128, :],
                        in_=stream[:, tc * D:(tc + 1) * D]).then_inc(dma, 16)
            kb.bump("dma", 16)
            e.dma_start(out=kb.t["aux_spill"][tc * 128:(tc + 1) * 128, :],
                        in_=aux[:, tc * D:(tc + 1) * D]).then_inc(dma, 16)
            kb.bump("dma", 16)
        e.wait_ge(dma, kb.val("dma"))

    @p.on("gpsimd")
    def _(e):
        e.dma_start(out=bass.AP(wbig, _WB["decw"],
                                [[16384, 128], [16, 8], [1, 16]]),
                    in_=bass.AP(kb.t["decw"], 0,
                                [[16, 128], [128 * 16, 8], [1, 16]])
                    ).then_inc(dmag, 16)
        kb.bump("dmag", 16)
        e.wait_ge(dmag, kb.val("dmag"))

    p.run()

    _sl4_dec(kb, tag)
    if SL4S < 2:
        return
    _sl4_qkv(kb, tag)
    if SL4S < 3:
        return
    _sl4_anchors(kb, tag)
    if SL4S < 4:
        return
    _sl4_bwd(kb, tag)
    if SL4S < 5:
        return
    _sl4_ag(kb, tag)
    if SL4S < 6:
        return
    _sl4_combine(kb, tag)
    if SL4S < 7:
        return
    _sl4_fwd_main(kb, tag)
    if SL4S < 8:
        return

    p = Phase(kb, tag + "_rld")

    @p.on("sync")
    def _(e):
        for tc in range(NCH):
            e.dma_start(out=stream[:, tc * D:(tc + 1) * D],
                        in_=kb.t["aux_spill"][tc * 128:(tc + 1) * 128, :]
                        ).then_inc(dma, 16)
            kb.bump("dma", 16)
        kb.row_load(e, dma, [("ob_row", "ob", D, 0), ("f2b_row", "f2b", D, 2)])
        e.wait_ge(dma, kb.val("dma"))

    p.run()

    def wo_lhs(kH, tc):
        return auxb[:, kH * 2048 + tc * 128:kH * 2048 + (tc + 1) * 128]

    def wo_wsrc(kH):
        return bass.AP(kb.t["ow"], 128 * kH * 1024, [[1024, 128], [1, 1024]])

    def wo_out(e, bank, tc, dh):
        sl = stream[:, tc * D + dh * 512:tc * D + (dh + 1) * 512]
        return e.scalar_tensor_tensor(sl, bank[:, :], 1.0, sl, ALU.mult, ALU.add)

    emit_proj_B(kb, tag + "_wo", wo_lhs, 8, wo_wsrc, True, ("ob_row", D), wo_out)
    if SL4S < 9:
        return

    emit_ln_transpose(kb, stream, xnt, tag + "_ln2",
                      gamma_beta=(kb.t["ln2w_c"], kb.t["ln2b_c"]))
    _sl4_ffn(kb, tag, stream)
    if SL4S < 10:
        return

    p = Phase(kb, tag + "_rls")

    @p.on("sync")
    def _(e):
        for tc in range(NCH):
            e.dma_start(out=aux[:, tc * D:(tc + 1) * D],
                        in_=kb.t["stream_spill"][tc * 128:(tc + 1) * 128, :]
                        ).then_inc(dma, 16)
            kb.bump("dma", 16)
        e.wait_ge(dma, kb.val("dma"))

    p.run()
    p = Phase(kb, tag + "_add")

    @p.on("vector")
    def _(e):
        for tc in range(NCH):
            sl = stream[:, tc * D:(tc + 1) * D]
            e.tensor_tensor(sl, sl, aux[:, tc * D:(tc + 1) * D], ALU.add)

    p.run()


def _scv(kb, name, p0=0, np_=128, w=None):
    c0 = _SC[name]
    widths = dict(zrow=2048, CLS=2048, zer=128, T_all=256, totr=256, Gr=256,
                  lam=256, pre=256, suf=256, DF=256, DE=256, S=1040, Rb=1040,
                  Sin=1040, Rin=1040, gat=1040, tmp=1040, psx=32, attf=1280,
                  attg=1280, dmat=16, wwk=128, den=16, cew=96, wfl=256,
                  mrow=32, wwk2=128, negT=256)
    ww = w if w is not None else widths[name]
    return kb.t["stream"][p0:p0 + np_, c0:c0 + ww]


def _scap(kb, name, c_extra, dims):
    return bass.AP(kb.t["stream"], _SC[name] + c_extra,
                   [[NCH * D, 128]] + dims)


def _sl4_dec(kb, tag):
    dma, smm, sev, sea = (kb.sem("dma"), kb.sem("mm"), kb.sem("ev"),
                          kb.sem("ea"))
    xnt, wbig = kb.t["xnt"], kb.t["wbig"]
    psum = [kb.t[f"psb{i}"] for i in range(6)]
    zrow = _scv(kb, "zrow", 0, 16)
    CLS = _scv(kb, "CLS", 0, 16)
    zer = _scv(kb, "zer", 0, 16)
    p = Phase(kb, tag + "_dec")
    mm0, ea0 = kb.val("mm"), kb.val("ea")

    @p.on("vector")
    def _(e):
        e.memset(zer[:, :], 0.0)

    @p.on("tensor")
    def _(e):
        for t4 in range(4):
            bank = psum[t4 % 2]
            if t4 >= 2:
                e.wait_ge(sea, ea0 + t4 - 1)
            last = None
            for k in range(8):
                last = e.matmul(
                    bank[0:16, :],
                    wbig[:, _WB["decw"] + k * 16:_WB["decw"] + (k + 1) * 16],
                    xnt[:, k * T + t4 * 512:k * T + (t4 + 1) * 512],
                    start=(k == 0), stop=(k == 7))
            last.then_inc(smm, 1)
    kb.bump("mm", 4)

    @p.on("scalar")
    def _(e):
        for t4 in range(4):
            e.wait_ge(smm, mm0 + t4 + 1)
            bank = psum[t4 % 2]
            e.activation(zrow[:, t4 * 512:(t4 + 1) * 512], bank[0:16, :],
                         AF.Exp, bias=kb.t["qsc"][0:16, 14:15],
                         scale=-1.0).then_inc(sea, 1)
    kb.bump("ea", 4)

    p.run()
    p = Phase(kb, tag + "_ll")

    @p.on("scalar")
    def _(e):
        e.activation(zrow[:, :], zrow[:, :], AF.Ln, bias=1.0, scale=1.0
                     ).then_inc(sea, 1)
        kb.bump("ea", 1)

    p.run()
    p = Phase(kb, tag + "_cls")

    @p.on("vector")
    def _(e):
        e.tensor_scalar(zrow[:, :], zrow[:, :], 18.420680743952367, None,
                        ALU.min).then_inc(sev, 1)
        kb.bump("ev", 1)
        e.wait_ge(sev, kb.val("ev"))
        for c in range(NCH):
            e.tensor_tensor_scan(CLS[:, c * 128:(c + 1) * 128],
                                 zrow[:, c * 128:(c + 1) * 128], zer[:, :],
                                 0.0, ALU.add, ALU.add)

    p.run()
    p = Phase(kb, tag + "_clsd")

    @p.on("sync")
    def _(e):
        e.dma_start(out=kb.t["ll_d"][:, :], in_=CLS[:, :]).then_inc(dma, 16)
        kb.bump("dma", 16)
        e.wait_ge(dma, kb.val("dma"))

    p.run()


def _sl4_qkv(kb, tag):
    dma, dmag, smm, sev, sea = (kb.sem("dma"), kb.sem("dmag"), kb.sem("mm"),
                                kb.sem("ev"), kb.sem("ea"))
    xnt, auxb = kb.t["xnt"], kb.t["auxb"]
    wbufs = [kb.t["wb0"], kb.t["wb1"]]
    psum = [kb.t[f"psb{i}"] for i in range(6)]
    phist = kb.t["phi_st"]

    for pj, (wname, bcol, dst0) in enumerate([("qw", "qb_c", 0),
                                              ("kw", "kb_c", 16384)]):
        p = Phase(kb, f"{tag}_{wname}")
        dg0, mm0, ev0, ea0 = (kb.val("dmag"), kb.val("mm"), kb.val("ev"),
                              kb.val("ea"))

        @p.on("gpsimd")
        def _(e, wname=wname, mm0=mm0):
            for m in range(8):
                buf = wbufs[m % 2]
                if m >= 2:
                    e.wait_ge(smm, mm0 + (m - 1) * 4)
                e.dma_start(out=bass.AP(buf, 0, [[1024, 128], [128, 8], [1, 128]]),
                            in_=bass.AP(kb.t[wname], 128 * m,
                                        [[1024, 128], [128 * 1024, 8], [1, 128]])
                            ).then_inc(dmag, 16)
                kb.bump("dmag", 16)

        @p.on("tensor")
        def _(e, dg0=dg0, ev0=ev0):
            n = 0
            for m in range(8):
                buf = wbufs[m % 2]
                e.wait_ge(dmag, dg0 + (m + 1) * 16)
                for t4 in range(4):
                    bank = psum[n % 4]
                    if n >= 4:
                        e.wait_ge(sev, ev0 + (n - 4) * 3 + 2)
                    last = None
                    for k in range(8):
                        last = e.matmul(
                            bank[:, :], buf[:, k * 128:(k + 1) * 128],
                            xnt[:, k * T + t4 * 512:k * T + (t4 + 1) * 512],
                            start=(k == 0), stop=(k == 7))
                    last.then_inc(smm, 1)
                    n += 1
        kb.bump("mm", 32)

        @p.on("vector")
        def _(e, dst0=dst0, bcol=bcol, mm0=mm0, ea0=ea0):
            n = 0
            for m in range(8):
                for t4 in range(4):
                    e.wait_ge(smm, mm0 + n + 1)
                    bank = psum[n % 4]
                    half = phist[:, (n % 2) * 512:(n % 2 + 1) * 512]
                    dst = auxb[:, dst0 + m * 2048 + t4 * 512:
                               dst0 + m * 2048 + (t4 + 1) * 512]
                    e.tensor_scalar(half, bank[:, :],
                                    kb.t[bcol][:, m:m + 1], 0.0,
                                    ALU.add, ALU.min).then_inc(sev, 1)
                    e.tensor_scalar(dst, bank[:, :],
                                    kb.t[bcol][:, m:m + 1], 0.0,
                                    ALU.add, ALU.max).then_inc(sev, 1)
                    e.wait_ge(sea, ea0 + n + 1)
                    e.tensor_tensor(dst, dst, half, ALU.add).then_inc(sev, 1)
                    n += 1
        kb.bump("ev", 96)

        @p.on("scalar")
        def _(e, mm0=mm0, ev0=ev0):
            for n in range(32):
                e.wait_ge(sev, ev0 + n * 3 + 1)
                half = phist[:, (n % 2) * 512:(n % 2 + 1) * 512]
                e.activation(half, half, AF.Exp, bias=0.0,
                             scale=1.0).then_inc(sea, 1)
        kb.bump("ea", 32)

        p.run()

    p = Phase(kb, tag + "_vrows")

    @p.on("sync")
    def _(e):
        kb.row_load(e, dma, [("vb_row", "vb", D, 0)])
        e.wait_ge(dma, kb.val("dma"))

    p.run()

    def v_lhs(kH, tc):
        return kb.t["xnt"][:, kH * T + tc * 128:kH * T + (tc + 1) * 128]

    vstg = [kb.t["wb0"], kb.t["wb1"]]
    p = Phase(kb, tag + "_v")
    dg0, mm0, ev0 = kb.val("dmag"), kb.val("mm"), kb.val("ev")
    groups = [[0, 1, 2], [3, 4, 5], [6, 7, 8], [9, 10, 11], [12, 13], [14, 15]]
    dm_start = kb.val("dma")
    wbig = kb.t["wbig"]
    vst = [wbig[:, _WB["wwork"] + i * 1024:_WB["wwork"] + (i + 1) * 1024]
           for i in range(4)]

    for gi, taus in enumerate(groups):
        p = Phase(kb, f"{tag}_v{gi}")
        dg0, mm0, ev0 = kb.val("dmag"), kb.val("mm"), kb.val("ev")
        nt = len(taus)

        @p.on("gpsimd")
        def _(e, taus=taus, mm0=mm0):
            for kH in range(8):
                buf = wbufs[kH % 2]
                if kH >= 2:
                    e.wait_ge(smm, mm0 + kH - 1)
                e.dma_start(out=buf[:, :],
                            in_=bass.AP(kb.t["vw"], 128 * kH * 1024,
                                        [[1024, 128], [1, 1024]])
                            ).then_inc(dmag, 16)
                kb.bump("dmag", 16)

        @p.on("tensor")
        def _(e, taus=taus, dg0=dg0):
            for kH in range(8):
                buf = wbufs[kH % 2]
                e.wait_ge(dmag, dg0 + (kH + 1) * 16)
                last = None
                for ti, tau in enumerate(taus):
                    for dh in range(2):
                        bank = psum[ti * 2 + dh]
                        last = e.matmul(bank[:, :], v_lhs(kH, tau),
                                        buf[:, dh * 512:(dh + 1) * 512],
                                        start=(kH == 0), stop=False)
                last.then_inc(smm, 1)
            for ti, tau in enumerate(taus):
                for dh in range(2):
                    bank = psum[ti * 2 + dh]
                    e.matmul(bank[:, :], kb.row_ones("vb_row"),
                             kb.row_ap("vb_row", dh * 512, 512),
                             start=False, stop=True).then_inc(smm, 1)
        kb.bump("mm", 8 + nt * 2)

        @p.on("vector")
        def _(e, taus=taus, nt=nt, mm0=mm0):
            e.wait_ge(smm, mm0 + 8 + nt * 2)
            for ti, tau in enumerate(taus):
                for dh in range(2):
                    bank = psum[ti * 2 + dh]
                    if tau >= 4:
                        e.wait_ge(dma, dm_start + (tau - 3) * 16)
                    ins = e.tensor_copy(vst[tau % 4][:, dh * 512:(dh + 1) * 512],
                                        bank[:, :])
                    if dh == 1:
                        ins.then_inc(sev, 1)
                        kb.bump("ev", 1)

        @p.on("sync")
        def _(e, taus=taus, ev0=ev0):
            for ti, tau in enumerate(taus):
                e.wait_ge(sev, ev0 + ti + 1)
                e.dma_start(out=bass.AP(kb.t["v_stage"], tau * 131072,
                                        [[1024, 128], [1, 1024]]),
                            in_=vst[tau % 4][:, :]).then_inc(dma, 16)
                kb.bump("dma", 16)

        p.run()

    p = Phase(kb, tag + "_vld")

    @p.on("sync")
    def _(e):
        e.dma_start(out=kb.t["xnt"][:, :],
                    in_=bass.AP(kb.t["v_stage"], 0,
                                [[1024, 128], [131072, 16], [1, 1024]])
                    ).then_inc(dma, 16)
        kb.bump("dma", 16)
        e.wait_ge(dma, kb.val("dma"))

    p.run()


def _sl4_anchors(kb, tag):
    dma, smm, sev, sea = (kb.sem("dma"), kb.sem("mm"), kb.sem("ev"),
                          kb.sem("ea"))
    psum = [kb.t[f"psb{i}"] for i in range(6)]
    eye_f = kb.t["eye_f"]
    CLS = _scv(kb, "CLS", 0, 16)
    T_all, totr, Gr = _scv(kb, "T_all"), _scv(kb, "totr", 0, 1), _scv(kb, "Gr", 0, 1)
    lam, pre, suf = (_scv(kb, "lam", 0, 1), _scv(kb, "pre", 0, 1),
                     _scv(kb, "suf", 0, 1))
    DF, DE = _scv(kb, "DF"), _scv(kb, "DE")
    p = Phase(kb, tag + "_anc")
    mm0, ev0, ea0, dm0 = kb.val("mm"), kb.val("ev"), kb.val("ea"), kb.val("dma")

    @p.on("sync")
    def _(e):
        e.dma_start(out=bass.AP(kb.t["stream"], _SC["totr"],
                                [[NCH * D, 1], [16, 16], [1, 16]]),
                    in_=bass.AP(kb.t["ll_d"], 127, [[128, 16], [2048, 16]]),
                    allow_slow_non_contiguous=True).then_inc(dma, 16)
        kb.bump("dma", 16)

    @p.on("tensor")
    def _(e):
        for c in range(NCH):
            e.matmul(psum[0][:, c * 16:(c + 1) * 16],
                     CLS[:, c * 128:(c + 1) * 128], eye_f[0:16, 0:16],
                     start=True, stop=True).then_inc(smm, 1)
        e.wait_ge(dma, dm0 + 16)
        e.wait_ge(sev, ev0 + 1)
        e.matmul(psum[1][:, 0:256], kb.t["ones_all"][0:1, 0:128],
                 totr[:, :], start=True, stop=True).then_inc(smm, 1)
    kb.bump("mm", NCH + 1)

    @p.on("vector")
    def _(e):
        e.wait_ge(smm, mm0 + NCH)
        e.tensor_copy(T_all[:, :], psum[0][:, 0:256]).then_inc(sev, 1)
        e.wait_ge(dma, dm0 + 16)
        flush(kb, e, e.tensor_copy(Gr[:, 0:16], totr[:, 0:16]))
        for c in range(1, NCH):
            flush(kb, e, e.tensor_tensor(Gr[:, c * 16:(c + 1) * 16],
                                         Gr[:, (c - 1) * 16:c * 16],
                                         totr[:, c * 16:(c + 1) * 16], ALU.add))
        e.nop().then_inc(sev, 1)
        e.tensor_tensor(pre[:, :], totr[:, :], Gr[:, :],
                        ALU.subtract).then_inc(sev, 1)
        gtot = bass.AP(kb.t["stream"], _SC["Gr"] + 240,
                       [[NCH * D, 1], [0, 16], [1, 16]])
        e.tensor_tensor(suf[:, :], Gr[:, :], gtot,
                        ALU.subtract).then_inc(sev, 1)
        e.wait_ge(smm, mm0 + NCH + 1)
        e.scalar_tensor_tensor(DE[:, :], psum[1][:, 0:256], -1.0, T_all[:, :],
                               ALU.mult, ALU.add).then_inc(sev, 1)
        e.tensor_scalar(_scv(kb, "negT")[:, :], T_all[:, :], -1.0, None,
                        ALU.mult).then_inc(sev, 1)
    kb.bump("ev", 6)

    @p.on("scalar")
    def _(e):
        e.wait_ge(sev, ev0 + 2)
        e.activation(lam[:, :], totr[:, :], AF.Exp, bias=0.0, scale=-1.0)
        e.wait_ge(sev, ev0 + 3)
        e.activation(pre[:, :], pre[:, :], AF.Exp, bias=0.0, scale=1.0)
        e.wait_ge(sev, ev0 + 4)
        e.activation(suf[:, :], suf[:, :], AF.Exp, bias=0.0, scale=1.0)
        e.activation(DF[:, :], T_all[:, :], AF.Exp, bias=0.0, scale=-1.0)
        e.wait_ge(sev, ev0 + 5)
        e.activation(DE[:, :], DE[:, :], AF.Exp, bias=0.0,
                     scale=1.0).then_inc(sea, 1)
    kb.bump("ea", 1)

    p.run()


def _sl4_scan(kb, tag, fwd):
    dma, smm, sev, sea = (kb.sem("dma"), kb.sem("mm"), kb.sem("ev"),
                          kb.sem("ea"))
    psum = [kb.t[f"psb{i}"] for i in range(6)]
    wbig, xnt, auxb = kb.t["wbig"], kb.t["xnt"], kb.t["auxb"]
    eye_s = kb.t["eye_s"]
    ps_bf = kb.t["ps_bf"]
    order = list(range(NCH)) if fwd else list(range(NCH - 1, -1, -1))
    dname = "DE" if fwd else "DF"
    snap = kb.t["s_d"] if fwd else kb.t["r_d"]
    st = _scv(kb, "S") if fwd else _scv(kb, "Rb")
    lam = _scv(kb, "lam", 0, 1)
    psx = _scv(kb, "psx")
    ktm = wbig[:, _WB["ktm"]:_WB["ktm"] + 1024]
    kdf = wbig[:, _WB["kdf"]:_WB["kdf"] + 1024]
    dfb = wbig[:, _WB["dfb"]:_WB["dfb"] + 16]

    p = Phase(kb, tag)
    mm0, ev0, dm0 = kb.val("mm"), kb.val("ev"), kb.val("dma")
    T_IT, V_IT = 73, 6

    @p.on("tensor")
    def _(e):
        for i, c in enumerate(order):
            if i >= 1:
                e.wait_ge(sev, ev0 + 1 + (i - 1) * V_IT + 1)
            for m in range(8):
                e.transpose(ps_bf[:, m * 128:(m + 1) * 128],
                            auxb[:, 16384 + m * 2048 + c * 128:
                                 16384 + m * 2048 + (c + 1) * 128],
                            eye_s[:, :]).then_inc(smm, 1)
            if i >= 1:
                e.wait_ge(sev, ev0 + 1 + (i - 1) * V_IT + 3)
            e.matmul(psum[3][:, 0:16], kb.t["ones_all"][0:1, 0:128],
                     lam[:, c * 16:(c + 1) * 16], start=True,
                     stop=True).then_inc(smm, 1)
            e.wait_ge(sev, ev0 + 1 + i * V_IT + 3)
            for h in range(16):
                b = 0 if h < 7 else (1 if h < 14 else 2)
                c0 = (h - 7 * b) * 65
                kl = kdf[:, h * 64:(h + 1) * 64]
                vs = xnt[:, c * 1024 + h * 64:c * 1024 + (h + 1) * 64]
                for base in (0, 64):
                    e.matmul(psum[b][base:base + 64, c0:c0 + 64], kl, vs,
                             start=True, stop=True).then_inc(smm, 1)
                    e.matmul(psum[b][base:base + 64, c0 + 64:c0 + 65], kl,
                             kb.t["onesc_b"][:, h:h + 1], start=True,
                             stop=True).then_inc(smm, 1)
    kb.bump("mm", T_IT * NCH)

    @p.on("vector")
    def _(e):
        e.memset(st[:, :], 0.0).then_inc(sev, 1)
        for i, c in enumerate(order):
            base = ev0 + 1 + i * V_IT
            e.wait_ge(smm, mm0 + i * T_IT + 8)
            e.tensor_copy(ktm[:, :], ps_bf[:, 0:1024]).then_inc(sev, 1)
            e.tensor_copy(dfb[:, :],
                          _scv(kb, dname)[:, c * 16:(c + 1) * 16])
            fence(e, 12)
            e.tensor_tensor(kdf[:, :], ktm[:, :],
                            bass.AP(wbig, _WB["dfb"],
                                    [[16384, 128], [1, 16], [0, 64]]),
                            ALU.mult).then_inc(sev, 1)
            e.wait_ge(smm, mm0 + i * T_IT + 9)
            e.tensor_copy(psx[:, 0:16], psum[3][:, 0:16]).then_inc(sev, 1)
            fence(e, 12)
            e.wait_ge(dma, dm0 + 16 * (i + 1))
            e.tensor_tensor(st[:, :], st[:, :],
                            _scap(kb, "psx", 0, [[1, 16], [0, 65]]),
                            ALU.mult).then_inc(sev, 1)
            e.wait_ge(smm, mm0 + (i + 1) * T_IT)
            e.tensor_tensor(st[:, 0:455], st[:, 0:455], psum[0][:, 0:455],
                            ALU.add).then_inc(sev, 1)
            e.tensor_tensor(st[:, 455:910], st[:, 455:910], psum[1][:, 0:455],
                            ALU.add)
            e.tensor_tensor(st[:, 910:1040], st[:, 910:1040],
                            psum[2][:, 0:130], ALU.add).then_inc(sev, 1)
    kb.bump("ev", 1 + V_IT * NCH)

    @p.on("sync")
    def _(e):
        for i, c in enumerate(order):
            if i == 0:
                e.wait_ge(sev, ev0 + 1)
            else:
                e.wait_ge(sev, ev0 + 1 + i * V_IT)
            e.dma_start(out=bass.AP(snap, c * 133120,
                                    [[1040, 128], [1, 1040]]),
                        in_=st[:, :]).then_inc(dma, 16)
            kb.bump("dma", 16)
        e.wait_ge(dma, kb.val("dma"))

    p.run()


def _sl4_bwd(kb, tag):
    _sl4_scan(kb, tag + "_bsc", fwd=False)


def _sl4_ag(kb, tag):
    _sl4_scan(kb, tag + "_fsc", fwd=True)
    dma, scc = kb.sem("dma"), kb.sem("cc")
    p = Phase(kb, tag + "_pub")

    @p.on("sync")
    def _(e):
        e.dma_start(out=bass.AP(kb.t["lion_in"], 0, [[1040, 64], [1, 1040]]),
                    in_=_scv(kb, "S", 0, 64)).then_inc(dma, 16)
        kb.bump("dma", 16)
        e.dma_start(out=bass.AP(kb.t["lion_in"], 66560,
                                [[1040, 64], [1, 1040]]),
                    in_=_scv(kb, "Rb", 0, 64)).then_inc(dma, 16)
        kb.bump("dma", 16)
        e.dma_start(out=bass.AP(kb.t["lion_in"], 133120, [[16, 1], [1, 16]]),
                    in_=_scv(kb, "Gr", 0, 1)[:, 240:256]).then_inc(dma, 16)
        kb.bump("dma", 16)
        e.wait_ge(dma, kb.val("dma"))

    p.run()
    p = Phase(kb, tag + "_cc")

    @p.on("gpsimd")
    def _(e):
        e.collective_compute(
            "AllGather", ALU.bypass, replica_groups=[list(range(N_CORES))],
            ins=[kb.t["lion_in"].ap().opt()],
            outs=[kb.t["lion_out"].ap().opt()],
        ).then_inc(scc, 1)
        kb.bump("cc", 1)

    p.run()


def _sl4_combine(kb, tag):
    dma, smm, sev, sea, scc = (kb.sem("dma"), kb.sem("mm"), kb.sem("ev"),
                               kb.sem("ea"), kb.sem("cc"))
    psum = [kb.t[f"psb{i}"] for i in range(6)]
    cew = _scv(kb, "cew", 0, 16)
    zer = _scv(kb, "zer", 0, 16)
    mrow = _scv(kb, "mrow", 0, 1)
    wfl = _scv(kb, "wfl", 0, 1)
    Sin, Rin = _scv(kb, "Sin"), _scv(kb, "Rin")
    Sg, Rg = _scv(kb, "gat"), _scv(kb, "tmp")
    tS, tR = _scv(kb, "S"), _scv(kb, "Rb")
    psx = _scv(kb, "psx")
    cc_target = kb.val("cc")

    p = Phase(kb, tag + "_w")
    mm0, ev0, ea0, dm0 = kb.val("mm"), kb.val("ev"), kb.val("ea"), kb.val("dma")

    @p.on("sync")
    def _(e):
        e.wait_ge(scc, cc_target)
        for r in range(N_CORES):
            e.dma_start(out=cew[:, r:r + 1],
                        in_=bass.AP(kb.t["lion_out"], r * 133136 + 133120,
                                    [[1, 16], [1, 1]]),
                        allow_slow_non_contiguous=True).then_inc(dma, 16)
            kb.bump("dma", 16)
        for j, nm in enumerate(["m_lt", "m_gt", "m_me"]):
            e.dma_start(out=mrow[:, j * 8:(j + 1) * 8],
                        in_=kb.t[nm][:, :]).then_inc(dma, 16)
            kb.bump("dma", 16)
        e.wait_ge(dma, kb.val("dma"))

    @p.on("tensor")
    def _(e):
        e.wait_ge(dma, dm0 + 11 * 16)
        for j in range(3):
            e.matmul(psum[4][0:16, j * 8:(j + 1) * 8],
                     kb.t["ones_all"][0:1, 0:16],
                     mrow[:, j * 8:(j + 1) * 8], start=True,
                     stop=True).then_inc(smm, 1)
    kb.bump("mm", 3)

    @p.on("vector")
    def _(e):
        e.wait_ge(smm, mm0 + 3)
        e.memset(cew[:, 8:9], 0.0)
        e.tensor_copy(cew[:, 17:41], psum[4][0:16, 0:24])
        fence(e, 12)
        e.tensor_tensor_scan(cew[:, 9:17], cew[:, 0:8], zer[:, 0:8], 0.0,
                             ALU.add, ALU.add)
        fence(e, 12)
        e.tensor_tensor(cew[:, 41:49], cew[:, 0:8], cew[:, 17:25], ALU.mult)
        fence(e, 12)
        e.tensor_reduce(cew[:, 49:50], cew[:, 41:49], AX.X, ALU.add)
        fence(e, 12)
        e.tensor_tensor(cew[:, 41:49], cew[:, 0:8], cew[:, 33:41], ALU.mult)
        fence(e, 12)
        e.tensor_reduce(cew[:, 50:51], cew[:, 41:49], AX.X, ALU.add)
        fence(e, 12)
        e.tensor_tensor(cew[:, 51:52], cew[:, 49:50], cew[:, 50:51], ALU.add)
        fence(e, 12)
        e.tensor_scalar(cew[:, 52:60], cew[:, 9:17], cew[:, 49:50], 0.0,
                        ALU.subtract, ALU.min).then_inc(sev, 1)
        e.tensor_scalar(cew[:, 60:68], cew[:, 8:16], cew[:, 51:52], 0.0,
                        ALU.subtract, ALU.max).then_inc(sev, 1)
        e.wait_ge(sea, ea0 + 2)
        e.tensor_tensor(cew[:, 52:60], cew[:, 52:60], cew[:, 17:25],
                        ALU.mult).then_inc(sev, 1)
        e.tensor_tensor(cew[:, 60:68], cew[:, 60:68], cew[:, 25:33],
                        ALU.mult).then_inc(sev, 1)
    kb.bump("ev", 4)

    @p.on("scalar")
    def _(e):
        e.wait_ge(sev, ev0 + 1)
        e.activation(cew[:, 52:60], cew[:, 52:60], AF.Exp, bias=0.0,
                     scale=1.0).then_inc(sea, 1)
        e.wait_ge(sev, ev0 + 2)
        e.activation(cew[:, 60:68], cew[:, 60:68], AF.Exp, bias=0.0,
                     scale=-1.0).then_inc(sea, 1)
    kb.bump("ea", 2)

    p.run()

    p = Phase(kb, tag + "_hop")

    @p.on("sync")
    def _(e):
        e.dma_start(out=bass.AP(kb.t["hop_d"], 0, [[8, 16], [1, 8]]),
                    in_=cew[:, 52:60]).then_inc(dma, 16)
        kb.bump("dma", 16)
        e.dma_start(out=bass.AP(kb.t["hop_d"], 128, [[8, 16], [1, 8]]),
                    in_=cew[:, 60:68]).then_inc(dma, 16)
        kb.bump("dma", 16)
        e.wait_ge(dma, kb.val("dma"))
        e.dma_start(out=bass.AP(kb.t["stream"], _SC["wfl"],
                                [[NCH * D, 1], [16, 8], [1, 16]]),
                    in_=bass.AP(kb.t["hop_d"], 0, [[1, 8], [8, 16]]),
                    allow_slow_non_contiguous=True).then_inc(dma, 16)
        kb.bump("dma", 16)
        e.dma_start(out=bass.AP(kb.t["stream"], _SC["wfl"] + 128,
                                [[NCH * D, 1], [16, 8], [1, 16]]),
                    in_=bass.AP(kb.t["hop_d"], 128, [[1, 8], [8, 16]]),
                    allow_slow_non_contiguous=True).then_inc(dma, 16)
        kb.bump("dma", 16)
        e.wait_ge(dma, kb.val("dma"))

    p.run()

    p = Phase(kb, tag + "_acc")
    mm0, ev0, dm0 = kb.val("mm"), kb.val("ev"), kb.val("dma")

    @p.on("sync")
    def _(e):
        for r in range(N_CORES):
            if r >= 1:
                e.wait_ge(sev, ev0 + 2 + (r - 1) * 3 + 3)
            for half in range(2):
                e.dma_start(out=Sg[half * 64:half * 64 + 64, :],
                            in_=bass.AP(kb.t["lion_out"], r * 133136,
                                        [[1040, 64], [1, 1040]])
                            ).then_inc(dma, 16)
                kb.bump("dma", 16)
                e.dma_start(out=Rg[half * 64:half * 64 + 64, :],
                            in_=bass.AP(kb.t["lion_out"], r * 133136 + 66560,
                                        [[1040, 64], [1, 1040]])
                            ).then_inc(dma, 16)
                kb.bump("dma", 16)

    @p.on("tensor")
    def _(e):
        for r in range(N_CORES):
            if r >= 1:
                e.wait_ge(sev, ev0 + 2 + (r - 1) * 3 + 1)
            e.matmul(psum[3][:, 0:16], kb.t["ones_all"][0:1, 0:128],
                     wfl[:, r * 16:(r + 1) * 16], start=True,
                     stop=True).then_inc(smm, 1)
            e.matmul(psum[3][:, 16:32], kb.t["ones_all"][0:1, 0:128],
                     wfl[:, 128 + r * 16:128 + (r + 1) * 16], start=True,
                     stop=True).then_inc(smm, 1)
    kb.bump("mm", 2 * N_CORES)

    @p.on("vector")
    def _(e):
        e.memset(Sin[:, :], 0.0)
        e.memset(Rin[:, :], 0.0).then_inc(sev, 2)
        for r in range(N_CORES):
            e.wait_ge(smm, mm0 + 2 * (r + 1))
            e.tensor_copy(psx[:, 0:32], psum[3][:, 0:32]).then_inc(sev, 1)
            fence(e, 12)
            e.wait_ge(dma, dm0 + 64 * (r + 1))
            e.tensor_tensor(tS[:, :], Sg[:, :],
                            _scap(kb, "psx", 0, [[1, 16], [0, 65]]),
                            ALU.mult)
            e.tensor_tensor(tR[:, :], Rg[:, :],
                            _scap(kb, "psx", 16, [[1, 16], [0, 65]]),
                            ALU.mult).then_inc(sev, 1)
            fence(e, 12)
            e.tensor_tensor(Sin[:, :], Sin[:, :], tS[:, :], ALU.add)
            e.tensor_tensor(Rin[:, :], Rin[:, :], tR[:, :],
                            ALU.add).then_inc(sev, 1)
    kb.bump("ev", 2 + 3 * N_CORES)

    p.run()


def _sl4_fwd_main(kb, tag):
    dma, smm, sev, sea = (kb.sem("dma"), kb.sem("mm"), kb.sem("ev"),
                          kb.sem("ea"))
    psum = [kb.t[f"psb{i}"] for i in range(6)]
    wbig, xnt, auxb = kb.t["wbig"], kb.t["xnt"], kb.t["auxb"]
    wbigf = kb.t["wbigf"]
    eye_s, ps_bf = kb.t["eye_s"], kb.t["ps_bf"]
    CLS = _scv(kb, "CLS", 0, 16)
    T_all = _scv(kb, "T_all")
    Sin, Rin = _scv(kb, "Sin"), _scv(kb, "Rin")
    St, Rt = _scv(kb, "S"), _scv(kb, "Rb")
    gat, tmp = _scv(kb, "gat"), _scv(kb, "tmp")
    psx, dmat = _scv(kb, "psx"), _scv(kb, "dmat")
    attf, attg = _scv(kb, "attf"), _scv(kb, "attg")
    wwkb = [_scv(kb, "wwk"), _scv(kb, "wwk2")]
    sel = wbigf[0:16, 5120:7168]
    sbf = wbig[:, _WB["sbf"]:_WB["sbf"] + 1040]
    rbf = wbig[:, _WB["rbf"]:_WB["rbf"] + 1040]
    att = wbig[:, _WB["att"]:_WB["att"] + 1024]
    asb = [wbig[:, 14336:14464], wbig[:, 14464:14592]]

    def phq(m, c, b=0, n=128):
        return auxb[b:b + n, m * 2048 + c * 128:m * 2048 + (c + 1) * 128]

    def phk(m, c, b=0, n=128):
        return auxb[b:b + n, 16384 + m * 2048 + c * 128:
                    16384 + m * 2048 + (c + 1) * 128]

    p = Phase(kb, tag + "_sel")

    @p.on("sync")
    def _(e):
        e.dma_start(out=sel[:, :], in_=kb.t["sel_d"][:, :]).then_inc(dma, 16)
        kb.bump("dma", 16)
        e.dma_start(out=gat[:, :],
                    in_=bass.AP(kb.t["s_d"], 0, [[1040, 128], [1, 1040]])
                    ).then_inc(dma, 16)
        kb.bump("dma", 16)
        e.dma_start(out=tmp[:, :],
                    in_=bass.AP(kb.t["r_d"], 0, [[1040, 128], [1, 1040]])
                    ).then_inc(dma, 16)
        kb.bump("dma", 16)
        e.wait_ge(dma, kb.val("dma"))

    p.run()

    p = Phase(kb, tag + "_mn")
    mm0, ev0, ea0, dm0 = kb.val("mm"), kb.val("ev"), kb.val("ea"), kb.val("dma")
    NPF = int(_os.environ.get("KMOD_FM_NP", "8"))
    NCF = int(_os.environ.get("KMOD_FM_NC", str(NCH)))
    DOTR = _os.environ.get("KMOD_FM_TR", "1") != "0"
    NTR = NPF if DOTR else 0
    T_C = 2 + NPF * 12 + NTR
    V_C = 7 + NPF * 10 + 1 + 2 * NPF + NTR

    def vbase(c):
        return ev0 + c * V_C

    def tbase(c):
        return mm0 + c * T_C

    @p.on("sync")
    def _(e):
        for c in range(1, NCF):
            e.wait_ge(sev, vbase(c - 1) + 7)
            e.dma_start(out=gat[:, :],
                        in_=bass.AP(kb.t["s_d"], c * 133120,
                                    [[1040, 128], [1, 1040]])
                        ).then_inc(dma, 16)
            kb.bump("dma", 16)
            e.dma_start(out=tmp[:, :],
                        in_=bass.AP(kb.t["r_d"], c * 133120,
                                    [[1040, 128], [1, 1040]])
                        ).then_inc(dma, 16)
            kb.bump("dma", 16)

    @p.on("tensor")
    def _(e):
        for c in range(NCF):
            if c >= 1:
                e.wait_ge(sev, vbase(c - 1) + 1)
            e.matmul(psum[3][:, 0:16], kb.t["ones_all"][0:1, 0:128],
                     _scv(kb, "pre", 0, 1)[:, c * 16:(c + 1) * 16],
                     start=True, stop=True).then_inc(smm, 1)
            e.matmul(psum[3][:, 16:32], kb.t["ones_all"][0:1, 0:128],
                     _scv(kb, "suf", 0, 1)[:, c * 16:(c + 1) * 16],
                     start=True, stop=True).then_inc(smm, 1)
            for m in range(NPF):
                tp = tbase(c) + 2 + m * 12
                g = c * NPF + m
                if g >= 1:
                    pc, pm = (g - 1) // NPF, (g - 1) % NPF
                    e.wait_ge(sev, vbase(pc) + 7 + pm * 10 + 2)
                for j, h in enumerate((2 * m, 2 * m + 1)):
                    b = 64 * j
                    e.matmul(psum[0][:, j * 256:j * 256 + 128],
                             sel[:, h * 128:(h + 1) * 128],
                             CLS[:, c * 128:(c + 1) * 128],
                             start=True, stop=True).then_inc(smm, 1)
                    e.matmul(psum[0][:, j * 256 + 128:j * 256 + 256],
                             phk(m, c, b, 64), phq(m, c, b, 64),
                             start=True, stop=True).then_inc(smm, 1)
                for j, h in enumerate((2 * m, 2 * m + 1)):
                    b = 64 * j
                    pa = psum[1 + j]
                    e.wait_ge(sev, vbase(c) + 7 + m * 10 + 1 + j)
                    if g >= 1:
                        pc, pm = (g - 1) // NPF, (g - 1) % NPF
                        e.wait_ge(sev, vbase(pc) + 7 + pm * 10 + 10)
                    e.matmul(pa[:, 0:64], asb[j][:, :],
                             xnt[:, c * 1024 + h * 64:
                                 c * 1024 + (h + 1) * 64],
                             start=True, stop=True).then_inc(smm, 1)
                    e.matmul(pa[:, 64:65], asb[j][:, :],
                             kb.t["onesc_b"][:, h:h + 1],
                             start=True, stop=True).then_inc(smm, 1)
                    if m == 0 and j == 0:
                        e.wait_ge(sev, vbase(c) + 4)
                    e.matmul(pa[:, 80:145], phq(m, c, b, 64),
                             sbf[b:b + 64, h * 65:(h + 1) * 65],
                             start=True, stop=True).then_inc(smm, 1)
                    if m == 0 and j == 0:
                        e.wait_ge(sev, vbase(c) + 7)
                    e.matmul(pa[:, 160:225], phq(m, c, b, 64),
                             rbf[b:b + 64, h * 65:(h + 1) * 65],
                             start=True, stop=True).then_inc(smm, 1)
            for m in range(NPF if DOTR else 0):
                e.wait_ge(sev, vbase(c) + 7 + NPF * 10 + 1 + 2 * (m + 1))
                if c >= 1 and m == 0:
                    e.wait_ge(sev, vbase(c - 1) + V_C)
                e.transpose(ps_bf[:, m * 128:(m + 1) * 128],
                            att[:, m * 128:(m + 1) * 128],
                            eye_s[:, :]).then_inc(smm, 1)
    kb.bump("mm", T_C * NCF)

    @p.on("scalar")
    def _(e):
        negT = _scv(kb, "negT")
        for c in range(NCF):
            for m in range(NPF):
                g = c * NPF + m
                tp = tbase(c) + 2 + m * 12
                for j, h in enumerate((2 * m, 2 * m + 1)):
                    e.wait_ge(smm, tp + 4)
                    if g >= 1:
                        pc, pm = (g - 1) // NPF, (g - 1) % NPF
                        e.wait_ge(sev, vbase(pc) + 7 + pm * 10 + 1 + j)
                    e.activation(wwkb[j][:, :],
                                 psum[0][:, j * 256:j * 256 + 128], AF.Abs,
                                 bias=negT[:, c * 16 + h:c * 16 + h + 1],
                                 scale=1.0).then_inc(sea, 1)
                for j in range(2):
                    e.activation(wwkb[j][:, :], wwkb[j][:, :], AF.Exp,
                                 bias=0.0, scale=-1.0).then_inc(sea, 1)
    kb.bump("ea", 4 * NPF * NCF)

    @p.on("vector")
    def _(e):
        for c in range(NCF):
            e.wait_ge(smm, tbase(c) + 2)
            if c >= 1:
                e.wait_ge(smm, tbase(c - 1) + 2 + NPF * 12)
            e.tensor_copy(psx[:, 0:32], psum[3][:, 0:32]).then_inc(sev, 1)
            fence(e, 12)
            if c >= 1:
                e.wait_ge(dma, dm0 + 32 * c)
            e.tensor_tensor(St[:, :], Sin[:, :],
                            _scap(kb, "psx", 0, [[1, 16], [0, 65]]),
                            ALU.mult).then_inc(sev, 1)
            e.tensor_tensor(St[:, :], St[:, :], gat[:, :],
                            ALU.add).then_inc(sev, 1)
            e.tensor_copy(sbf[:, :], St[:, :]).then_inc(sev, 1)
            e.tensor_tensor(Rt[:, :], Rin[:, :],
                            _scap(kb, "psx", 16, [[1, 16], [0, 65]]),
                            ALU.mult).then_inc(sev, 1)
            e.tensor_tensor(Rt[:, :], Rt[:, :], tmp[:, :],
                            ALU.add).then_inc(sev, 1)
            e.tensor_copy(rbf[:, :], Rt[:, :]).then_inc(sev, 1)
            for m in range(NPF):
                tp = tbase(c) + 2 + m * 12
                g = c * NPF + m
                for j, h in enumerate((2 * m, 2 * m + 1)):
                    e.wait_ge(sea, ea0 + 4 * g + 3 + j)
                    e.wait_ge(smm, tp + 4)
                    if g >= 1:
                        pc, pm = (g - 1) // NPF, (g - 1) % NPF
                        e.wait_ge(smm, tbase(pc) + 2 + pm * 12 + 4 + 4 * j + 2)
                    e.tensor_tensor(asb[j][:, :],
                                    psum[0][:, j * 256 + 128:j * 256 + 256],
                                    wwkb[j][:, :], ALU.mult).then_inc(sev, 1)
                for j, h in enumerate((2 * m, 2 * m + 1)):
                    e.wait_ge(smm, tp + 4 + 4 * j + 4)
                    e.tensor_scalar(attf[:, h * 80:h * 80 + 65],
                                    psum[1 + j][:, 80:145],
                                    _scv(kb, "DF")[:, c * 16 + h:
                                                   c * 16 + h + 1],
                                    None, ALU.mult).then_inc(sev, 1)
                for j, h in enumerate((2 * m, 2 * m + 1)):
                    e.scalar_tensor_tensor(attg[:, h * 80:h * 80 + 65],
                                           psum[1 + j][:, 160:225],
                                           _scv(kb, "DE")[:, c * 16 + h:
                                                          c * 16 + h + 1],
                                           attf[:, h * 80:h * 80 + 65],
                                           ALU.mult, ALU.add).then_inc(sev, 1)
                for j, h in enumerate((2 * m, 2 * m + 1)):
                    e.scalar_tensor_tensor(attf[:, h * 80:h * 80 + 65],
                                           psum[1 + j][:, 0:65], 1.0,
                                           attg[:, h * 80:h * 80 + 65],
                                           ALU.mult, ALU.add).then_inc(sev, 1)
                for j, h in enumerate((2 * m, 2 * m + 1)):
                    e.tensor_scalar(dmat[:, h:h + 1],
                                    attf[:, h * 80 + 64:h * 80 + 65], 1e-6,
                                    None, ALU.max).then_inc(sev, 1)
            fence(e, 12)
            e.reciprocal(dmat[:, :], dmat[:, :]).then_inc(sev, 1)
            fence(e, 12)
            for h in range(2 * NPF):
                if c >= 1 and h == 0:
                    e.wait_ge(smm, tbase(c - 1) + T_C)
                e.tensor_scalar(att[:, h * 64:(h + 1) * 64],
                                attf[:, h * 80:h * 80 + 64],
                                dmat[:, h:h + 1], None,
                                ALU.mult).then_inc(sev, 1)
            for m in range(NPF if DOTR else 0):
                e.wait_ge(smm, tbase(c) + T_C)
                e.tensor_copy(phq(m, c, 0, 128),
                              ps_bf[:, m * 128:(m + 1) * 128]
                              ).then_inc(sev, 1)
    kb.bump("ev", V_C * NCF)

    p.run()


def _sl4_ffn(kb, tag, dst):
    dmag, smm, sev, sea = (kb.sem("dmag"), kb.sem("mm"), kb.sem("ev"),
                           kb.sem("ea"))
    wbig, xnt = kb.t["wbig"], kb.t["xnt"]
    wbufs = [kb.t["wb0"], kb.t["wb1"]]
    psum = [kb.t[f"psb{i}"] for i in range(6)]
    f1b_c = kb.t["f1b_c"]

    for ttQ in range(4):
        p = Phase(kb, f"{tag}_fh{ttQ}")
        dg0, mm0, ea0 = kb.val("dmag"), kb.val("mm"), kb.val("ea")

        @p.on("gpsimd")
        def _(e, ttQ=ttQ, mm0=mm0):
            for m in range(32):
                buf = wbufs[m % 2]
                if m >= 2:
                    e.wait_ge(smm, mm0 + m - 1)
                e.dma_start(out=bass.AP(buf, 0, [[1024, 128], [128, 8], [1, 128]]),
                            in_=bass.AP(kb.t["f1w"], 128 * m,
                                        [[4096, 128], [128 * 4096, 8], [1, 128]])
                            ).then_inc(dmag, 16)
                kb.bump("dmag", 16)

        @p.on("tensor")
        def _(e, ttQ=ttQ, dg0=dg0, ea0=ea0):
            for m in range(32):
                buf = wbufs[m % 2]
                e.wait_ge(dmag, dg0 + (m + 1) * 16)
                bank = psum[m % 4]
                if m >= 4:
                    e.wait_ge(sea, ea0 + m - 3)
                last = None
                for k in range(8):
                    last = e.matmul(bank[:, :], buf[:, k * 128:(k + 1) * 128],
                                    xnt[:, k * T + ttQ * 512:k * T + (ttQ + 1) * 512],
                                    start=(k == 0), stop=(k == 7))
                last.then_inc(smm, 1)
        kb.bump("mm", 32)

        @p.on("scalar")
        def _(e, ttQ=ttQ, mm0=mm0):
            for m in range(32):
                e.wait_ge(smm, mm0 + m + 1)
                bank = psum[m % 4]
                e.activation(wbig[:, m * 512:(m + 1) * 512], bank[:, :],
                             AF.Gelu, bias=f1b_c[:, m:m + 1],
                             scale=1.0).then_inc(sea, 1)
        kb.bump("ea", 32)

        p.run()

        for gi, taus in enumerate([[0, 1], [2, 3]]):
            p = Phase(kb, f"{tag}_fd{ttQ}_{gi}")
            dg0, mm0, ev0 = kb.val("dmag"), kb.val("mm"), kb.val("ev")
            nt = len(taus)

            @p.on("gpsimd")
            def _(e, taus=taus, mm0=mm0):
                for kH in range(32):
                    buf = wbufs[kH % 2]
                    if kH >= 2:
                        e.wait_ge(smm, mm0 + kH - 1)
                    e.dma_start(out=buf[:, :],
                                in_=bass.AP(kb.t["f2w"], 128 * kH * 1024,
                                            [[1024, 128], [1, 1024]])
                                ).then_inc(dmag, 16)
                    kb.bump("dmag", 16)

            @p.on("tensor")
            def _(e, taus=taus, ttQ=ttQ, dg0=dg0):
                for kH in range(32):
                    buf = wbufs[kH % 2]
                    e.wait_ge(dmag, dg0 + (kH + 1) * 16)
                    last = None
                    for ti, tau in enumerate(taus):
                        for dh in range(2):
                            bank = psum[ti * 2 + dh]
                            last = e.matmul(
                                bank[:, :],
                                wbig[:, kH * 512 + tau * 128:
                                     kH * 512 + (tau + 1) * 128],
                                buf[:, dh * 512:(dh + 1) * 512],
                                start=(kH == 0), stop=False)
                    last.then_inc(smm, 1)
                for ti, tau in enumerate(taus):
                    for dh in range(2):
                        bank = psum[ti * 2 + dh]
                        e.matmul(bank[:, :], kb.row_ones("f2b_row"),
                                 kb.row_ap("f2b_row", dh * 512, 512),
                                 start=False, stop=True).then_inc(smm, 1)
            kb.bump("mm", 32 + nt * 2)

            @p.on("vector")
            def _(e, taus=taus, ttQ=ttQ, mm0=mm0, nt=nt):
                e.wait_ge(smm, mm0 + 32 + nt * 2)
                for ti, tau in enumerate(taus):
                    tc = ttQ * 4 + tau
                    for dh in range(2):
                        bank = psum[ti * 2 + dh]
                        sl = dst[:, tc * D + dh * 512:tc * D + (dh + 1) * 512]
                        e.scalar_tensor_tensor(sl, bank[:, :], 1.0, sl,
                                               ALU.mult, ALU.add)

            p.run()


def emit_ln_stats(kb, src, tag):
    s_sum, s_sq = kb.t["s_sum"], kb.t["s_sq"]
    s_mu, s_rs, s_nmr, s_t0 = (kb.t["s_mu"], kb.t["s_rs"], kb.t["s_nmr"],
                               kb.t["s_t0"])
    stg = [kb.t["stg0"], kb.t["stg1"]]
    p = Phase(kb, f"{tag}_st")

    @p.on("vector")
    def _(e):
        for tc in range(NCH):
            e.tensor_reduce(s_sum[:, tc:tc + 1], src[:, tc * D:(tc + 1) * D],
                            AX.X, ALU.add)

    @p.on("scalar")
    def _(e):
        for tc in range(NCH):
            e.activation(stg[tc % 2][:, :], src[:, tc * D:(tc + 1) * D],
                         AF.Square, bias=0.0, scale=1.0,
                         accum_out=s_sq[:, tc:tc + 1])

    p.run()
    p = Phase(kb, f"{tag}_mu")

    @p.on("vector")
    def _(e):
        e.tensor_scalar(s_mu[:, :], s_sum[:, :], 1.0 / D, None, ALU.mult)
        e.tensor_scalar(s_t0[:, :], s_sq[:, :], 1.0 / D, None, ALU.mult)
        fence(e)
        e.tensor_tensor(s_sum[:, :], s_mu[:, :], s_mu[:, :], ALU.mult)
        fence(e)
        e.tensor_tensor(s_t0[:, :], s_t0[:, :], s_sum[:, :], ALU.subtract)
        fence(e)
        e.tensor_scalar(s_t0[:, :], s_t0[:, :], LN_EPS, None, ALU.add)

    p.run()
    p = Phase(kb, f"{tag}_rs")

    @p.on("scalar")
    def _(e):
        e.activation(s_rs[:, :], s_t0[:, :], AF.Ln, bias=0.0, scale=1.0)
        fence(e, 12)
        e.activation(s_rs[:, :], s_rs[:, :], AF.Exp, bias=0.0, scale=-0.5)

    p.run()
    p = Phase(kb, f"{tag}_nm")

    @p.on("vector")
    def _(e):
        e.tensor_tensor(s_nmr[:, :], s_mu[:, :], s_rs[:, :], ALU.mult)
        fence(e)
        e.tensor_scalar(s_nmr[:, :], s_nmr[:, :], -1.0, None, ALU.mult)

    p.run()


def emit_ln_final_add(kb, tag, src, dst_stream, gamma_name, beta_name,
                      rows_loaded=False):
    dma, smm, sev, sea = (kb.sem("dma"), kb.sem("mm"), kb.sem("ev"),
                          kb.sem("ea"))
    psum = [kb.t[f"psb{i}"] for i in range(6)]
    lnst = kb.t["xntf"]
    emit_ln_stats(kb, src, tag)

    if not rows_loaded:
        p = Phase(kb, tag + "_rows")

        @p.on("sync")
        def _(e):
            kb.row_load(e, dma, [("lng_r", gamma_name, D),
                                 ("lnb_r", beta_name, D)])
            e.wait_ge(dma, kb.val("dma"))

        p.run()
    p = Phase(kb, tag + "_bc")

    @p.on("tensor")
    def _(e):
        for du in range(2):
            e.matmul(psum[du][:, :], kb.row_ones("lng_r"),
                     kb.row_ap("lng_r", du * 512, 512), start=True, stop=True
                     ).then_inc(smm, 1)
            e.matmul(psum[2 + du][:, :], kb.row_ones("lnb_r"),
                     kb.row_ap("lnb_r", du * 512, 512), start=True, stop=True
                     ).then_inc(smm, 1)
    kb.bump("mm", 4)

    p.run()
    p = Phase(kb, tag + "_app")
    s_rs, s_nmr = kb.t["s_rs"], kb.t["s_nmr"]
    ea0, ev0 = kb.val("ea"), kb.val("ev")

    @p.on("scalar")
    def _(e):
        for tc in range(NCH):
            if tc >= 1:
                e.wait_ge(sev, ev0 + tc)
            e.activation(lnst[:, 0:1024], src[:, tc * D:(tc + 1) * D],
                         AF.Identity, bias=s_nmr[:, tc:tc + 1],
                         scale=s_rs[:, tc:tc + 1]).then_inc(sea, 1)
    kb.bump("ea", NCH)

    @p.on("vector")
    def _(e):
        for tc in range(NCH):
            e.wait_ge(sea, ea0 + tc + 1)
            last = None
            for du in range(2):
                u = lnst[:, du * 512:(du + 1) * 512]
                flush(kb, e, e.tensor_tensor(u, u, psum[du][:, :], ALU.mult))
                flush(kb, e, e.tensor_tensor(u, u, psum[2 + du][:, :], ALU.add))
                sl = dst_stream[:, tc * D + du * 512:tc * D + (du + 1) * 512]
                last = e.tensor_tensor(sl, sl, u, ALU.add)
            last.then_inc(sev, 1)
    kb.bump("ev", NCH)

    p.run()



_CACHE = {}


def kernel(**inp):
    per_core = _host_prep(inp)
    kbk = build_program()
    res = run_bass_kernel_spmd(kbk.nc, per_core, core_ids=list(range(N_CORES)))
    outs = [np.asarray(r["out"], np.float32) for r in res.results]
    return np.concatenate(outs, axis=0).reshape(1, N_CORES * T, D)


# version 03 (final)
# speedup vs baseline: 1.0201x; 1.0201x over previous
import numpy as np
import ml_dtypes
import concourse.bass as bass
import concourse.mybir as mybir
from concourse.bass_utils import run_bass_kernel_spmd

F32 = mybir.dt.float32
BF16 = mybir.dt.bfloat16
AF = mybir.ActivationFunctionType
ALU = mybir.AluOpType
AX = mybir.AxisListType

N_CORES = 8
T = 2048
NCH = 16
D = 1024
KD = 8
MAGIC = float(np.float32(3 * 2 ** 22))
LN_EPS = 1e-5

ENGINES = ("tensor", "vector", "scalar", "sync", "gpsimd")

N_SUBLAYERS = 6
import os as _os
STAGE = int(_os.environ.get("KMOD_STAGE", "99"))


_PHASE_N = [0]


class Phase:
    def __init__(self, kb, name):
        self.kb = kb
        _PHASE_N[0] += 1
        self.name = f"{name}_{_PHASE_N[0]}"
        self.q = {e: [] for e in ENGINES}

    def on(self, eng):
        def reg(fn):
            self.q[eng].append(fn)
            return fn
        return reg

    def run(self):
        with self.kb.nc.Block(self.name) as b:
            for eng in ENGINES:
                fns = self.q[eng]
                if not fns:
                    continue
                def body(e, fns=tuple(fns)):
                    for f in fns:
                        f(e)
                getattr(b, eng)(body)


class KB:
    def __init__(self):
        self.nc = bass.Bass("TRN2", target_bir_lowering=False,
                            detect_race_conditions=False)
        self._sems = {}
        self.t = {}

    def sem(self, name):
        if name not in self._sems:
            h = self.nc.semaphore(name).__enter__()
            self._sems[name] = [h, 0]
        return self._sems[name][0]

    def bump(self, name, k=1):
        s = self._sems[name]
        s[1] += k
        return s[1]

    def val(self, name):
        return self._sems[name][1]

    def sb(self, name, shape, dtype=F32, **kw):
        h = self.nc.alloc_sbuf_tensor(name, shape, dtype, **kw)
        self.t[name] = h
        return h

    def ps(self, name, shape, dtype=F32):
        h = self.nc.alloc_psum_tensor(name, shape, dtype)
        self.t[name] = h
        return h

    def dram(self, name, shape, dtype=F32, kind="Internal", **kw):
        h = self.nc.dram_tensor(name, shape, dtype, kind=kind, **kw)
        self.t[name] = h
        return h


def fence(e, n=8):
    for _ in range(n):
        e.nop()


def flush(kb, e, ins):
    h = kb.sem("flv")
    v = kb.bump("flv")
    ins.then_inc(h, 1)
    e.wait_ge(h, v)



def _host_prep(inp):
    f = np.ascontiguousarray

    def row(v):
        return f(np.asarray(v, np.float32).reshape(1, -1))

    shared = {
        "wqkvT": f(np.asarray(inp["L0_attn_w_qkv"], np.float32).T),
        "woutT": f(np.asarray(inp["L0_attn_w_out"], np.float32).T),
        "proj": f(np.asarray(inp["L0_attn_proj"], np.float32)),
        "aw1T": f(np.asarray(inp["L0_attn_w1"], np.float32).T),
        "aw2T": f(np.asarray(inp["L0_attn_w2"], np.float32).T),
        "ab_out": row(inp["L0_attn_b_out"]), "agamma": row(inp["L0_attn_gamma"]),
        "abeta": row(inp["L0_attn_beta"]), "ab1": row(inp["L0_attn_b1"]),
        "ab2": row(inp["L0_attn_b2"]),
        "fb1": row(inp["L0_ffn_b1"]), "fb2": row(inp["L0_ffn_b2"]),
        "qw": f(np.asarray(inp["L1_attn_q_w"], np.float32)),
        "kw": f(np.asarray(inp["L1_attn_k_w"], np.float32)),
        "vw": f(np.asarray(inp["L1_attn_v_w"], np.float32)),
        "ow": f(np.asarray(inp["L1_attn_o_w"], np.float32)),
        "qb": row(inp["L1_attn_q_b"]), "kbias": row(inp["L1_attn_k_b"]),
        "vb": row(inp["L1_attn_v_b"]), "ob": row(inp["L1_attn_o_b"]),
        "ln1w": row(inp["L1_attn_ln1_w"]), "ln1b": row(inp["L1_attn_ln1_b"]),
        "ln2w": row(inp["L1_attn_ln2_w"]), "ln2b": row(inp["L1_attn_ln2_b"]),
        "decw": f(np.asarray(inp["L1_attn_dec_w"], np.float32)),
        "decb": row(inp["L1_attn_dec_b"]),
        "f1w": f(np.asarray(inp["L1_attn_f1_w"], np.float32)),
        "f1b": row(inp["L1_attn_f1_b"]),
        "f2w": f(np.asarray(inp["L1_attn_f2_w"], np.float32)),
        "f2b": row(inp["L1_attn_f2_b"]),
    }
    fw1 = np.asarray(inp["L0_ffn_w1"], np.float32)
    fw2 = np.asarray(inp["L0_ffn_w2"], np.float32)
    gw = np.asarray(inp["L1_ffn_w_gate"], np.float32)
    uw = np.asarray(inp["L1_ffn_w_up"], np.float32)
    dw = np.asarray(inp["L1_ffn_w_down"], np.float32)

    x = np.asarray(inp["L0_attn_x"], np.float32).reshape(N_CORES * T, D)
    per_core = []
    for c in range(N_CORES):
        m = dict(shared)
        m["x_in"] = f(x[c * T:(c + 1) * T])
        m["fw1_sh"] = f(fw1[c * 512:(c + 1) * 512])
        m["fw2_sh"] = f(fw2[c * 128:(c + 1) * 128])
        m["gw_sh"] = f(gw[c * 384:(c + 1) * 384])
        m["uw_sh"] = f(uw[c * 384:(c + 1) * 384])
        m["dw_sh"] = f(dw[c * 128:(c + 1) * 128])
        m["m_lt"] = row((np.arange(N_CORES) < c).astype(np.float32))
        m["m_gt"] = row((np.arange(N_CORES) > c).astype(np.float32))
        m["m_me"] = row((np.arange(N_CORES) == c).astype(np.float32))
        per_core.append(m)
    return per_core



def build_program():
    kb = KB()
    nc = kb.nc

    x_in = kb.dram("x_in", [T, D], F32, kind="ExternalInput")
    out = kb.dram("out", [T, D], F32, kind="ExternalOutput")
    for name, shape in [
        ("wqkvT", [D, 3072]), ("woutT", [D, D]), ("proj", [16, 64, 256]),
        ("aw1T", [D, 4096]), ("aw2T", [4096, D]),
        ("qw", [D, D]), ("kw", [D, D]), ("vw", [D, D]), ("ow", [D, D]),
        ("decw", [D, 16]), ("f1w", [D, 4096]), ("f2w", [4096, D]),
        ("fw1_sh", [512, D]), ("fw2_sh", [128, 4096]),
        ("gw_sh", [384, D]), ("uw_sh", [384, D]), ("dw_sh", [128, 3072]),
    ]:
        kb.dram(name, shape, F32, kind="ExternalInput")
    for name, n in [("ab_out", D), ("agamma", D), ("abeta", D), ("ab1", 4096),
                    ("ab2", D), ("fb1", 4096), ("fb2", D), ("qb", D),
                    ("kbias", D), ("vb", D), ("ob", D), ("ln1w", D),
                    ("ln1b", D), ("ln2w", D), ("ln2b", D), ("decb", 16),
                    ("f1b", 4096), ("f2b", D), ("m_lt", N_CORES),
                    ("m_gt", N_CORES), ("m_me", N_CORES)]:
        kb.dram(name, [1, n], F32, kind="ExternalInput")

    eye_bf = nc.inline_tensor(np.eye(128, dtype=ml_dtypes.bfloat16), "eye_bf")
    eye_fd = nc.inline_tensor(np.eye(128, dtype=np.float32), "eye_fd")
    sel_np = np.zeros((16, 2048), np.float32)
    for _h in range(16):
        sel_np[_h, _h * 128:(_h + 1) * 128] = 1.0
    kb.t["sel_d"] = nc.inline_tensor(sel_np, "sel_d")
    ones_d = nc.inline_tensor(np.ones((1, 128), np.float32), "ones_d")
    ones_bf_d = nc.inline_tensor(np.ones((1, 128), ml_dtypes.bfloat16), "ones_bf_d")

    kb.PPA, kb.PPB, kb.PPC, kb.PPD, kb.PPE = 0, 524288, 1048576, 1441792, 1835008
    kb.PPTOT = 2228224
    kb.dram("pp_in", [kb.PPTOT], BF16)
    kb.dram("pp_out", [N_CORES, kb.PPTOT], BF16, addr_space="Shared")
    kb.dram("bw2Td", [32, 128, 1024], BF16)
    kb.dram("dwTd", [24, 128, 1024], BF16)
    kb.dram("aux_spill", [T, D], F32)
    kb.dram("v_stage", [NCH, 128, D], BF16)
    kb.dram("kv_loc", [128, 2080], F32)
    kb.dram("kv_red", [128, 2080], F32, addr_space="Shared")
    for nm in ["up_d", "gate_d", "gs_d", "prod_d"]:
        kb.dram(nm, [24, 128, 2048], F32)
    kb.dram("mx_loc", [1, 8], F32)
    kb.dram("mx_red", [1, 8], F32)
    kb.dram("hop_d", [1024], F32)
    kb.dram("stream_spill", [T, D], F32)
    kb.dram("r_d", [16, 128, 1040], F32)
    kb.dram("s_d", [16, 128, 1040], F32)
    kb.dram("ll_d", [16, 2048], F32)
    kb.dram("lion_in", [133136], F32)
    kb.dram("lion_out", [N_CORES, 133136], F32, addr_space="Shared")

    stream = kb.sb("stream", [128, NCH * D], F32)
    aux = kb.sb("aux", [128, NCH * D], F32)
    auxb = aux.bitcast(BF16)
    xnt = kb.sb("xnt", [128, KD * T], BF16)
    wb = [kb.sb(f"wb{i}", [128, 1024], BF16) for i in range(2)]
    wbig = kb.sb("wbig", [128, 16384], BF16)
    kb.t["stg0"], kb.t["stg1"] = wb[0], wb[1]
    for nm in ["s_sum", "s_sq", "s_mu", "s_rs", "s_nmr", "s_t0"]:
        kb.sb(nm, [128, NCH], F32)
    eye_s = kb.sb("eye_s", [128, 128], BF16)
    eye_f = kb.sb("eye_f", [128, 128], F32)
    kb.sb("ones1b", [1, 128], BF16)
    kb.sb("onesc_b", [128, 16], BF16)
    kb.sb("ones_all", [128, 128], F32)
    rowa = kb.sb("rowa", [128, 1024], F32)
    kb.row_slots = {}

    def row_ap(nm, c0, n):
        part = kb.row_slots[nm]
        return rowa[part:part + 1, c0:c0 + n]
    kb.row_ap = row_ap

    def row_ones(nm, n=128):
        part = kb.row_slots[nm]
        return kb.t["ones_all"][part:part + 1, 0:n]
    kb.row_ones = row_ones

    def row_load(e, dma_sem, assigns):
        for i, a in enumerate(assigns):
            nm, src, n = a[0], a[1], a[2]
            slot = a[3] if len(a) > 3 else i
            assert slot <= 2, "matmul base partition must be 0/32/64"
            kb.row_slots[nm] = slot * 32
            e.dma_start(out=rowa[slot * 32:slot * 32 + 1, 0:n],
                        in_=kb.t[src][:, :]).then_inc(dma_sem, 16)
            kb.bump("dma", 16)
    kb.row_load = row_load
    kb.sb("agamma_c", [128, KD], F32)
    kb.sb("abeta_c", [128, KD], F32)
    kb.sb("fb1_c", [128, 32], F32)
    kb.sb("ab1_c", [128, 32], F32)
    kb.sb("f1b_c", [128, 32], F32)
    kb.sb("qb_c", [128, KD], F32)
    kb.sb("kb_c", [128, KD], F32)
    kb.sb("ln1w_c", [128, KD], F32)
    kb.sb("ln1b_c", [128, KD], F32)
    kb.sb("ln2w_c", [128, KD], F32)
    kb.sb("ln2b_c", [128, KD], F32)

    kb.sb("qsc", [128, 16], F32)
    kb.sb("phi_st", [128, 1024], F32)
    kb.sb("s_rs_sl", [128, NCH], F32)
    kb.sb("s_nmr_sl", [128, NCH], F32)
    psum = [kb.ps(f"psb{i}", [128, 512], F32) for i in range(6)]
    kb.ps("ps_bf", [128, 1024], BF16)

    for s in ["dma", "dmag", "mm", "ev", "ea", "cc", "free", "ldw"]:
        kb.sem(s)

    kb.t["auxb"] = auxb
    kb.t["xntf"] = xnt.bitcast(F32)
    kb.t["wbigf"] = wbig.bitcast(F32)
    kb.t["out"] = out
    kb.t["x_in"] = x_in

    p = Phase(kb, "p0")
    dma = kb.sem("dma")

    @p.on("vector")
    def _(e):
        e.memset(kb.t["onesc_b"][:, :], 1.0)
        e.memset(kb.t["ones_all"][:, :], 1.0)

    @p.on("sync")
    def _(e):
        def ld(dst, src_ap):
            e.dma_start(out=dst, in_=src_ap).then_inc(dma, 16)
            kb.bump("dma", 16)
        ld(eye_s[:, :], eye_bf[:, :])
        ld(eye_f[:, :], eye_fd[:, :])
        ld(kb.t["ones1b"][:, :], ones_bf_d[:, :])

        for cn, src, n in [("agamma_c", "agamma", KD), ("abeta_c", "abeta", KD),
                           ("fb1_c", "fb1", 32), ("ab1_c", "ab1", 32),
                           ("f1b_c", "f1b", 32), ("qb_c", "qb", KD),
                           ("ln1w_c", "ln1w", KD), ("ln1b_c", "ln1b", KD),
                           ("ln2w_c", "ln2w", KD), ("ln2b_c", "ln2b", KD)]:
            e.dma_start(out=kb.t[cn][:, :],
                        in_=bass.AP(kb.t[src], 0, [[1, 128], [128, n]]),
                        allow_slow_non_contiguous=True).then_inc(dma, 16)
            kb.bump("dma", 16)
        e.dma_start(out=kb.t["kb_c"][:, :],
                    in_=bass.AP(kb.t["kbias"], 0, [[1, 128], [128, KD]]),
                    allow_slow_non_contiguous=True).then_inc(dma, 16)
        kb.bump("dma", 16)
        for tc in range(NCH):
            ld(stream[:, tc * D:(tc + 1) * D], x_in[tc * 128:(tc + 1) * 128, :])
        e.wait_ge(dma, kb.val("dma"))

    p.run()

    p = Phase(kb, "p0z")

    @p.on("vector")
    def _(e):
        e.memset(kb.t["wb0"].bitcast(F32)[0:1, 0:8], 0.0)

    p.run()
    p = Phase(kb, "p0z2")

    @p.on("sync")
    def _(e):
        e.dma_start(out=kb.t["mx_loc"][:, :],
                    in_=kb.t["wb0"].bitcast(F32)[0:1, 0:8]).then_inc(dma, 16)
        kb.bump("dma", 16)
        e.wait_ge(dma, kb.val("dma"))

    p.run()

    if STAGE >= 1:
        emit_prep(kb)
    p = Phase(kb, "p1_ag")
    scc = kb.sem("cc")

    if STAGE >= 2:
        @p.on("gpsimd")
        def _(e):
            e.collective_compute(
                "AllGather", ALU.bypass, replica_groups=[list(range(N_CORES))],
                ins=[kb.t["pp_in"].ap().opt()], outs=[kb.t["pp_out"].ap().opt()],
            ).then_inc(scc, 1)
            kb.bump("cc", 1)

    p.run()
    prep_cc = kb.val("cc")

    p = Phase(kb, "p1_repack")
    dma2 = kb.sem("dma")

    @(p.on("sync") if STAGE >= 3 else (lambda f: f))
    def _(e):
        e.wait_ge(scc, prep_cc)
        rp = kb.t["stg0"]
        rp2 = kb.t["stg1"]
        n0 = kb.val("dma")
        for kH in range(32):
            buf = rp if kH % 2 == 0 else rp2
            if kH >= 2:
                e.wait_ge(dma2, n0 + (kH - 1) * 32)
            e.dma_start(out=buf[:, :],
                        in_=bass.AP(kb.t["pp_out"], kb.PPB + kH * 16384,
                                    [[128, 128], [kb.PPTOT, 8], [1, 128]]),
                        allow_slow_non_contiguous=True).then_inc(dma2, 16)
            kb.bump("dma", 16)
            e.wait_ge(dma2, n0 + kH * 32 + 16)
            e.dma_start(out=bass.AP(kb.t["bw2Td"], kH * 131072,
                                    [[1024, 128], [1, 1024]]),
                        in_=buf[:, :]).then_inc(dma2, 16)
            kb.bump("dma", 16)
        n1 = kb.val("dma")
        for kH in range(24):
            buf = rp if kH % 2 == 0 else rp2
            if kH >= 2:
                e.wait_ge(dma2, n1 + (kH - 1) * 32)
            e.dma_start(out=buf[:, :],
                        in_=bass.AP(kb.t["pp_out"], kb.PPE + kH * 16384,
                                    [[128, 128], [kb.PPTOT, 8], [1, 128]]),
                        allow_slow_non_contiguous=True).then_inc(dma2, 16)
            kb.bump("dma", 16)
            e.wait_ge(dma2, n1 + kH * 32 + 16)
            e.dma_start(out=bass.AP(kb.t["dwTd"], kH * 131072,
                                    [[1024, 128], [1, 1024]]),
                        in_=buf[:, :]).then_inc(dma2, 16)
            kb.bump("dma", 16)
        e.wait_ge(dma2, kb.val("dma"))

    p.run()

    if N_SUBLAYERS >= 1 and STAGE >= 4:
        emit_L0_ffn_sublayer(kb, tag="sl0", wait_cc=prep_cc)
    if N_SUBLAYERS >= 2:
        emit_L0_attn_sublayer(kb, tag="sl1")
    if N_SUBLAYERS >= 3:
        emit_L0_ffn_sublayer(kb, tag="sl2", wait_cc=prep_cc)
    if N_SUBLAYERS >= 4:
        emit_L1_ffn_sublayer(kb, tag="sl3")
    if N_SUBLAYERS >= 5:
        emit_L1_attn_sublayer(kb, tag="sl4")
    if N_SUBLAYERS >= 6:
        emit_L1_ffn_sublayer(kb, tag="sl5")

    p = Phase(kb, "p_out")

    @p.on("sync")
    def _(e):
        for tc in range(NCH):
            e.dma_start(out=out[tc * 128:(tc + 1) * 128, :],
                        in_=stream[:, tc * D:(tc + 1) * D]).then_inc(dma, 16)
            kb.bump("dma", 16)
        e.wait_ge(dma, kb.val("dma"))

    p.run()
    return kb



def emit_prep(kb):
    dma = kb.sem("dma")
    w_nat = kb.t["aux"]
    w_bin = kb.t["xnt"]
    evb = kb.t["xnt"]
    EV0 = 4096
    alpha = kb.sb("pp_al", [128, 8], F32)
    scl = kb.sb("pp_scl", [128, 96], F32)
    rcl = kb.sb("pp_rcl", [128, 96], F32)
    eye_s = kb.t["eye_s"]
    pp_in = kb.t["pp_in"]

    def load_shard(name, ncols, packed):
        p = Phase(kb, f"prep_ld_{name}")

        @p.on("sync")
        def _(e):
            if packed:
                na = ncols // 1024
                src = bass.AP(kb.t[name], 0,
                              [[1024, 128], [128 * 1024, na], [1, 1024]])
                dst = bass.AP(w_nat, 0, [[16384, 128], [1024, na], [1, 1024]])
            else:
                src = kb.t[name][:, :]
                dst = w_nat[:, 0:ncols]
            e.dma_start(out=dst, in_=src).then_inc(dma, 16)
            kb.bump("dma", 16)
            e.wait_ge(dma, kb.val("dma"))

        p.run()

    def binarize(n_tiles, tile_cols):
        p = Phase(kb, "prep_bin_a")

        @p.on("vector")
        def _(e):
            for a in range(n_tiles):
                e.tensor_reduce(alpha[:, a:a + 1],
                                w_nat[:, a * tile_cols:(a + 1) * tile_cols],
                                AX.X, ALU.add, apply_absolute_value=True)

        @p.on("scalar")
        def _(e):
            e.activation(w_bin[:, 0:n_tiles * tile_cols],
                         w_nat[:, 0:n_tiles * tile_cols], AF.Sign,
                         bias=0.0, scale=1.0)

        p.run()
        p = Phase(kb, "prep_bin_b")

        @p.on("vector")
        def _(e):
            for a in range(n_tiles):
                sl = w_bin[:, a * tile_cols:(a + 1) * tile_cols]
                e.tensor_scalar(sl, sl, alpha[:, a:a + 1], 1.0 / tile_cols,
                                ALU.mult, ALU.mult)

        p.run()

    def lpbq(ncols):
        nb = ncols // 32
        p = Phase(kb, "prep_lpbq_a")

        @p.on("vector")
        def _(e):
            view = bass.AP(w_nat, 0, [[16384, 128], [32, nb], [1, 32]])
            e.tensor_reduce(scl[:, 0:nb], view, AX.X, ALU.max,
                            apply_absolute_value=True)
            fence(e)
            e.tensor_scalar(scl[:, 0:nb], scl[:, 0:nb], 1.0 / 7.0, 1e-12,
                            ALU.mult, ALU.max)
            fence(e)
            e.reciprocal(rcl[:, 0:nb], scl[:, 0:nb])

        p.run()
        p = Phase(kb, "prep_lpbq_b")

        @p.on("vector")
        def _(e):
            wv = bass.AP(w_nat, 0, [[16384, 128], [32, nb], [1, 32]])
            rb = bass.AP(rcl, 0, [[96, 128], [1, nb], [0, 32]])
            sbb = bass.AP(scl, 0, [[96, 128], [1, nb], [0, 32]])
            e.tensor_tensor(wv, wv, rb, ALU.mult)
            fence(e)
            e.tensor_scalar(w_nat[:, 0:ncols], w_nat[:, 0:ncols], MAGIC, MAGIC,
                            ALU.add, ALU.subtract)
            fence(e)
            bv = bass.AP(w_bin, 0, [[16384, 128], [32, nb], [1, 32]])
            e.tensor_tensor(bv, wv, sbb, ALU.mult)

        p.run()

    def transpose_out(base, n_a, tile_major):
        pst = kb.t["ps_bf"]
        for a in range(n_a):
            p = Phase(kb, f"prep_tr_{base}_{a}")

            @p.on("tensor")
            def _(e, a=a):
                for k in range(8):
                    e.transpose(pst[:, k * 128:(k + 1) * 128],
                                w_bin[:, a * 1024 + k * 128:a * 1024 + (k + 1) * 128],
                                eye_s[:, :])

            p.run()
            p = Phase(kb, f"prep_ev_{base}_{a}")

            @p.on("vector")
            def _(e):
                e.tensor_copy(evb[:, EV0:EV0 + 1024], pst[:, :])

            p.run()
            p = Phase(kb, f"prep_st_{base}_{a}")

            @p.on("sync")
            def _(e, a=a):
                if tile_major:
                    dst = bass.AP(pp_in, base + a * 131072,
                                  [[1024, 128], [1, 1024]])
                else:
                    dst = bass.AP(pp_in, base + a * 131072,
                                  [[128, 128], [16384, 8], [1, 128]])
                e.dma_start(out=dst,
                            in_=evb[:, EV0:EV0 + 1024]).then_inc(dma, 16)
                kb.bump("dma", 16)
                e.wait_ge(dma, kb.val("dma"))

            p.run()

    load_shard("fw1_sh", 4096, packed=True)
    binarize(4, 1024)
    transpose_out(kb.PPA, 4, tile_major=True)
    load_shard("fw2_sh", 4096, packed=False)
    binarize(1, 4096)
    transpose_out(kb.PPB, 4, tile_major=False)
    load_shard("gw_sh", 3072, packed=True)
    lpbq(3072)
    transpose_out(kb.PPC, 3, tile_major=True)
    load_shard("uw_sh", 3072, packed=True)
    lpbq(3072)
    transpose_out(kb.PPD, 3, tile_major=True)
    load_shard("dw_sh", 3072, packed=False)
    lpbq(3072)
    transpose_out(kb.PPE, 3, tile_major=False)



def emit_ln_transpose(kb, src, dst_xnt, tag, gamma_beta=None,
                      tm_affine_out=None):
    LN_STAGE = int(_os.environ.get("KMOD_LN_STAGE", "99"))
    dma, smm, sev, sea = (kb.sem("dma"), kb.sem("mm"), kb.sem("ev"),
                          kb.sem("ea"))
    s_sum, s_sq = kb.t["s_sum"], kb.t["s_sq"]
    s_mu, s_rs, s_nmr, s_t0 = (kb.t["s_mu"], kb.t["s_rs"], kb.t["s_nmr"],
                               kb.t["s_t0"])
    stg = [kb.t["stg0"], kb.t["stg1"]]
    eye_s = kb.t["eye_s"]
    pst = kb.t["ps_bf"]

    p = Phase(kb, f"{tag}_st")

    if LN_STAGE >= 1:
        @p.on("vector")
        def _(e):
            for tc in range(NCH):
                e.tensor_reduce(s_sum[:, tc:tc + 1], src[:, tc * D:(tc + 1) * D],
                                AX.X, ALU.add)

    if LN_STAGE >= 2:
        @p.on("scalar")
        def _(e):
            for tc in range(NCH):
                e.activation(stg[tc % 2][:, :], src[:, tc * D:(tc + 1) * D],
                             AF.Square, bias=0.0, scale=1.0,
                             accum_out=s_sq[:, tc:tc + 1])

    p.run()

    p = Phase(kb, f"{tag}_mu")

    @(p.on("vector") if LN_STAGE >= 3 else (lambda f: f))
    def _(e):
        e.tensor_scalar(s_mu[:, :], s_sum[:, :], 1.0 / D, None, ALU.mult)
        e.tensor_scalar(s_t0[:, :], s_sq[:, :], 1.0 / D, None, ALU.mult)
        fence(e)
        e.tensor_tensor(s_sum[:, :], s_mu[:, :], s_mu[:, :], ALU.mult)
        fence(e)
        e.tensor_tensor(s_t0[:, :], s_t0[:, :], s_sum[:, :], ALU.subtract)
        fence(e)
        e.tensor_scalar(s_t0[:, :], s_t0[:, :], LN_EPS, None, ALU.add)

    p.run()

    p = Phase(kb, f"{tag}_rs")

    @(p.on("scalar") if LN_STAGE >= 4 else (lambda f: f))
    def _(e):
        e.activation(s_rs[:, :], s_t0[:, :], AF.Ln, bias=0.0, scale=1.0)
        fence(e, 12)
        e.activation(s_rs[:, :], s_rs[:, :], AF.Exp, bias=0.0, scale=-0.5)

    p.run()

    p = Phase(kb, f"{tag}_nm")

    @(p.on("vector") if LN_STAGE >= 5 else (lambda f: f))
    def _(e):
        e.tensor_tensor(s_nmr[:, :], s_mu[:, :], s_rs[:, :], ALU.mult)
        fence(e)
        e.tensor_scalar(s_nmr[:, :], s_nmr[:, :], -1.0, None, ALU.mult)

    p.run()

    if LN_STAGE < 6:
        return
    p = Phase(kb, f"{tag}_tr")
    ea0, mm0, ev0 = kb.val("ea"), kb.val("mm"), kb.val("ev")

    @p.on("scalar")
    def _(e):
        for tc in range(NCH):
            if tc >= 2:
                e.wait_ge(smm, mm0 + KD * (tc - 1))
            e.activation(stg[tc % 2][:, :], src[:, tc * D:(tc + 1) * D],
                         AF.Identity, bias=s_nmr[:, tc:tc + 1],
                         scale=s_rs[:, tc:tc + 1]).then_inc(sea, 1)
    kb.bump("ea", NCH)

    @p.on("tensor")
    def _(e):
        for tc in range(NCH):
            e.wait_ge(sea, ea0 + tc + 1)
            if tc >= 1:
                e.wait_ge(sev, ev0 + tc)
            for j in range(KD):
                e.transpose(pst[:, j * 128:(j + 1) * 128],
                            stg[tc % 2][:, j * 128:(j + 1) * 128],
                            eye_s[:, :]).then_inc(smm, 1)
    kb.bump("mm", NCH * KD)

    @p.on("vector")
    def _(e):
        for tc in range(NCH):
            e.wait_ge(smm, mm0 + KD * (tc + 1))
            if gamma_beta is None:
                dst = bass.AP(dst_xnt, tc * 128, [[KD * T, 128], [T, KD], [1, 128]])
                e.tensor_copy(dst, pst[:, :].rearrange("p (j f) -> p j f", j=KD)
                              ).then_inc(sev, 1)
            else:
                g, bt = gamma_beta
                for j in range(KD):
                    dj = bass.AP(dst_xnt, j * T + tc * 128, [[KD * T, 128], [1, 128]])
                    ins = e.tensor_scalar(dj, pst[:, j * 128:(j + 1) * 128],
                                          g[:, j:j + 1], bt[:, j:j + 1],
                                          ALU.mult, ALU.add)
                    if j == KD - 1:
                        ins.then_inc(sev, 1)
    kb.bump("ev", NCH)

    p.run()

    if tm_affine_out is not None:
        dst_tm, gslot, bslot = tm_affine_out
        psum = [kb.t[f"psb{i}"] for i in range(6)]
        ust = kb.t["phi_st"]
        p = Phase(kb, f"{tag}_tmbc")

        @p.on("tensor")
        def _(e):
            for du in range(2):
                e.matmul(psum[du][:, :], kb.row_ones(gslot),
                         kb.row_ap(gslot, du * 512, 512), start=True,
                         stop=True).then_inc(smm, 1)
                e.matmul(psum[2 + du][:, :], kb.row_ones(bslot),
                         kb.row_ap(bslot, du * 512, 512), start=True,
                         stop=True).then_inc(smm, 1)
        kb.bump("mm", 4)

        p.run()
        p = Phase(kb, f"{tag}_tmap")
        ea1, ev1 = kb.val("ea"), kb.val("ev")

        @p.on("scalar")
        def _(e):
            for tc in range(NCH):
                if tc >= 1:
                    e.wait_ge(sev, ev1 + tc)
                e.activation(ust[:, 0:1024], src[:, tc * D:(tc + 1) * D],
                             AF.Identity, bias=s_nmr[:, tc:tc + 1],
                             scale=s_rs[:, tc:tc + 1]).then_inc(sea, 1)
        kb.bump("ea", NCH)

        @p.on("vector")
        def _(e):
            for tc in range(NCH):
                e.wait_ge(sea, ea1 + tc + 1)
                last = None
                for du in range(2):
                    u = ust[:, du * 512:(du + 1) * 512]
                    flush(kb, e, e.tensor_tensor(u, u, psum[du][:, :], ALU.mult))
                    sl = dst_tm[:, tc * D + du * 512:tc * D + (du + 1) * 512]
                    last = e.tensor_tensor(sl, u, psum[2 + du][:, :], ALU.add)
                last.then_inc(sev, 1)
        kb.bump("ev", NCH)

        p.run()



def emit_L0_ffn_sublayer(kb, tag, wait_cc):
    dma, smm, sev, sea, scc = (kb.sem("dma"), kb.sem("mm"), kb.sem("ev"),
                               kb.sem("ea"), kb.sem("cc"))
    stream, xnt, auxb = kb.t["stream"], kb.t["xnt"], kb.t["auxb"]
    hbuf = auxb
    wbufs = [kb.t["wb0"], kb.t["wb1"]]
    psum = [kb.t[f"psb{i}"] for i in range(6)]
    pp_out = kb.t["pp_out"]
    fb1_c = kb.t["fb1_c"]

    emit_ln_transpose(kb, stream, xnt, tag + "_ln")

    p = Phase(kb, tag + "_rows")

    @p.on("sync")
    def _(e):
        e.wait_ge(scc, wait_cc)
        kb.row_load(e, dma, [("fb2_r", "fb2", D)])
        e.wait_ge(dma, kb.val("dma"))

    p.run()

    for ttH in range(2):
        if STAGE >= 5:
            _l0ffn_h_half(kb, tag, ttH, hbuf, wbufs, psum, pp_out, xnt, fb1_c)
        if STAGE >= 6:
            _l0ffn_d_half(kb, tag, ttH, hbuf, wbufs, psum, pp_out, None, stream)


def _l0ffn_h_half(kb, tag, ttH, hbuf, wbufs, psum, pp_out, xnt, fb1_c):
    dma, smm, sea = kb.sem("dma"), kb.sem("mm"), kb.sem("ea")
    NB = 4
    p = Phase(kb, f"{tag}_h{ttH}")
    dm0, mm0, ea0 = kb.val("dma"), kb.val("mm"), kb.val("ea")

    @p.on("sync")
    def _(e):
        for m in range(32):
            buf = wbufs[m % 2]
            if m >= 2:
                e.wait_ge(smm, mm0 + (m - 1) * 2)
            e.dma_start(out=buf[:, :],
                        in_=bass.AP(pp_out, (m // 4) * kb.PPTOT + kb.PPA + (m % 4) * 131072,
                                    [[1024, 128], [1, 1024]])
                        ).then_inc(dma, 16)
            kb.bump("dma", 16)

    @p.on("tensor")
    def _(e):
        n = 0
        for m in range(32):
            buf = wbufs[m % 2]
            e.wait_ge(dma, dm0 + (m + 1) * 16)
            for t2 in range(2):
                bank = psum[n % NB]
                if n >= NB:
                    e.wait_ge(sea, ea0 + n - NB + 1)
                tok0 = ttH * 1024 + t2 * 512
                for k in range(8):
                    ins = e.matmul(bank[:, :], buf[:, k * 128:(k + 1) * 128],
                                   xnt[:, k * T + tok0:k * T + tok0 + 512],
                                   start=(k == 0), stop=(k == 7))
                ins.then_inc(smm, 1)
                n += 1
    kb.bump("mm", 64)

    @p.on("scalar")
    def _(e):
        n = 0
        for m in range(32):
            for t2 in range(2):
                e.wait_ge(smm, mm0 + n + 1)
                bank = psum[n % NB]
                e.activation(hbuf[:, m * 1024 + t2 * 512:m * 1024 + (t2 + 1) * 512],
                             bank[:, :], AF.Gelu, bias=fb1_c[:, m:m + 1],
                             scale=1.0).then_inc(sea, 1)
                n += 1
    kb.bump("ea", 64)

    p.run()


def _l0ffn_d_half(kb, tag, ttH, hbuf, wbufs, psum, pp_out, ones1, stream):
    dma, smm, sev = kb.sem("dma"), kb.sem("mm"), kb.sem("ev")
    groups = [[0, 1, 2], [3, 4, 5], [6, 7]]
    for gi, taus in enumerate(groups):
        p = Phase(kb, f"{tag}_d{ttH}_{gi}")
        dm0, mm0, ev0 = kb.val("dma"), kb.val("mm"), kb.val("ev")
        nt = len(taus)

        @p.on("sync")
        def _(e, taus=taus, nt=nt, mm0=mm0):
            for kH in range(32):
                buf = wbufs[kH % 2]
                if kH >= 2:
                    e.wait_ge(smm, mm0 + kH - 1)
                e.dma_start(out=buf[:, :],
                            in_=bass.AP(kb.t["bw2Td"], kH * 131072,
                                        [[1024, 128], [1, 1024]])
                            ).then_inc(dma, 16)
                kb.bump("dma", 16)

        @p.on("tensor")
        def _(e, taus=taus, nt=nt, ttH=ttH, dm0=dm0, mm0=mm0):
            for kH in range(32):
                buf = wbufs[kH % 2]
                e.wait_ge(dma, dm0 + (kH + 1) * 16)
                last = None
                for ti, tau in enumerate(taus):
                    for dh in range(2):
                        bank = psum[ti * 2 + dh]
                        last = e.matmul(
                            bank[:, :],
                            hbuf[:, kH * 1024 + tau * 128:kH * 1024 + (tau + 1) * 128],
                            buf[:, dh * 512:(dh + 1) * 512],
                            start=(kH == 0), stop=False)
                last.then_inc(smm, 1)
            for ti, tau in enumerate(taus):
                for dh in range(2):
                    bank = psum[ti * 2 + dh]
                    e.matmul(bank[:, :], kb.row_ones("fb2_r"),
                             kb.row_ap("fb2_r", dh * 512, 512),
                             start=False, stop=True).then_inc(smm, 1)
        kb.bump("mm", 32 + nt * 2)

        @p.on("vector")
        def _(e, taus=taus, ttH=ttH, mm0=mm0, nt=nt):
            e.wait_ge(smm, mm0 + 32 + nt * 2)
            for ti, tau in enumerate(taus):
                tc = ttH * 8 + tau
                for dh in range(2):
                    bank = psum[ti * 2 + dh]
                    sl = stream[:, tc * D + dh * 512:tc * D + (dh + 1) * 512]
                    e.scalar_tensor_tensor(sl, bank[:, :], 0.5, sl, ALU.mult, ALU.add)

        p.run()



def emit_proj_B(kb, tag, lhs_fn, n_kH, wsrc_fn, cast_w, bias_slot, out_cb,
                wait_first=None):
    dma, dmag, smm, sev = (kb.sem("dma"), kb.sem("dmag"), kb.sem("mm"),
                           kb.sem("ev"))
    wbufs = [kb.t["wb0"], kb.t["wb1"]]
    psum = [kb.t[f"psb{i}"] for i in range(6)]
    groups = [[0, 1, 2], [3, 4, 5], [6, 7, 8], [9, 10, 11], [12, 13], [14, 15]]
    for gi, taus in enumerate(groups):
        p = Phase(kb, f"{tag}_pB{gi}")
        wsem = dmag if cast_w else dma
        wsname = "dmag" if cast_w else "dma"
        dm0, mm0, ev0 = kb.val(wsname), kb.val("mm"), kb.val("ev")
        nt = len(taus)

        @p.on("gpsimd" if cast_w else "sync")
        def _(e, taus=taus, dm0=dm0, mm0=mm0, gi=gi):
            if gi == 0 and wait_first is not None:
                sem_h, val = wait_first
                e.wait_ge(sem_h, val)
            for kH in range(n_kH):
                buf = wbufs[kH % 2]
                if kH >= 2:
                    e.wait_ge(smm, mm0 + kH - 1)
                e.dma_start(out=buf[:, :], in_=wsrc_fn(kH)).then_inc(wsem, 16)
                kb.bump(wsname, 16)

        @p.on("tensor")
        def _(e, taus=taus, nt=nt, dm0=dm0, mm0=mm0, ev0=ev0):
            for kH in range(n_kH):
                buf = wbufs[kH % 2]
                e.wait_ge(wsem, dm0 + (kH + 1) * 16)
                last = None
                stop = (kH == n_kH - 1) and (bias_slot is None)
                for ti, tau in enumerate(taus):
                    for dh in range(2):
                        bank = psum[ti * 2 + dh]
                        last = e.matmul(bank[:, :], lhs_fn(kH, tau),
                                        buf[:, dh * 512:(dh + 1) * 512],
                                        start=(kH == 0), stop=stop)
                last.then_inc(smm, 1)
            if bias_slot is not None:
                nm, width = bias_slot
                for ti, tau in enumerate(taus):
                    for dh in range(2):
                        bank = psum[ti * 2 + dh]
                        e.matmul(bank[:, :], kb.row_ones(nm),
                                 kb.row_ap(nm, dh * 512, 512),
                                 start=False, stop=True).then_inc(smm, 1)
        nb = nt * 2 if bias_slot is not None else 0
        kb.bump("mm", n_kH + nb)

        @p.on("vector")
        def _(e, taus=taus, nt=nt, mm0=mm0):
            e.wait_ge(smm, mm0 + n_kH + (nt * 2 if bias_slot is not None else 0))
            for ti, tau in enumerate(taus):
                for dh in range(2):
                    bank = psum[ti * 2 + dh]
                    ins = out_cb(e, bank, tau, dh)
                    if ins is not None:
                        ins.then_inc(sev, 1)
                        kb.bump("ev", 1)

        p.run()



def emit_L0_attn_sublayer(kb, tag):
    SL1S = int(_os.environ.get("KMOD_SL1_STAGE", "99"))
    nc = kb.nc
    dma, dmag, smm, sev, sea, scc = (kb.sem("dma"), kb.sem("dmag"),
                                     kb.sem("mm"), kb.sem("ev"),
                                     kb.sem("ea"), kb.sem("cc"))
    stream, xnt, auxb, aux = kb.t["stream"], kb.t["xnt"], kb.t["auxb"], kb.t["aux"]
    wbig = kb.t["wbig"]
    wbufs = [kb.t["wb0"], kb.t["wb1"]]
    psum = [kb.t[f"psb{i}"] for i in range(6)]
    scr = kb.t["xntf"]
    R0 = lambda c0, n: auxb[:, c0:c0 + n]
    R1 = lambda c0, n: auxb[:, 16384 + c0:16384 + c0 + n]
    PRJ, KFH, QFB, KVB = 0, 4096, 8192, 12288
    phist = kb.t["phi_st"]

    emit_ln_transpose(kb, stream, xnt, tag + "_ln")
    p = Phase(kb, tag + "_sv")

    @p.on("vector")
    def _(e):
        e.tensor_copy(kb.t["s_rs_sl"][:, :], kb.t["s_rs"][:, :])
        e.tensor_copy(kb.t["s_nmr_sl"][:, :], kb.t["s_nmr"][:, :])

    @p.on("sync")
    def _(e):
        kb.row_load(e, dma, [("ab_out_r", "ab_out", D, 0),
                             ("ab2_r", "ab2", D, 2)])
        e.wait_ge(dma, kb.val("dma"))

    @p.on("gpsimd")
    def _(e):
        e.dma_start(out=bass.AP(wbig, PRJ, [[16384, 64], [256, 16], [1, 256]]),
                    in_=bass.AP(kb.t["proj"], 0,
                                [[256, 64], [16384, 16], [1, 256]])
                    ).then_inc(dmag, 16)
        kb.bump("dmag", 16)
        e.dma_start(out=bass.AP(wbig, PRJ + 64 * 16384,
                                [[16384, 64], [256, 16], [1, 256]]),
                    in_=bass.AP(kb.t["proj"], 0,
                                [[256, 64], [16384, 16], [1, 256]])
                    ).then_inc(dmag, 16)
        kb.bump("dmag", 16)
        e.wait_ge(dmag, kb.val("dmag"))

    p.run()

    p = Phase(kb, tag + "_qk")
    dg0, mm0, ea0 = kb.val("dmag"), kb.val("mm"), kb.val("ea")

    @p.on("gpsimd")
    def _(e):
        for m in range(16):
            buf = wbufs[m % 2]
            if m >= 2:
                e.wait_ge(smm, mm0 + (m - 1) * 4)
            e.dma_start(out=bass.AP(buf, 0, [[1024, 128], [128, 8], [1, 128]]),
                        in_=bass.AP(kb.t["wqkvT"], 128 * m,
                                    [[3072, 128], [128 * 3072, 8], [1, 128]])
                        ).then_inc(dmag, 16)
            kb.bump("dmag", 16)

    @p.on("tensor")
    def _(e):
        n = 0
        for m in range(16):
            buf = wbufs[m % 2]
            e.wait_ge(dmag, dg0 + (m + 1) * 16)
            for t4 in range(4):
                bank = psum[n % 4]
                if n >= 4:
                    e.wait_ge(sea, ea0 + n - 4 + 1)
                last = None
                for k in range(8):
                    last = e.matmul(bank[:, :], buf[:, k * 128:(k + 1) * 128],
                                    xnt[:, k * T + t4 * 512:k * T + (t4 + 1) * 512],
                                    start=(k == 0), stop=(k == 7))
                last.then_inc(smm, 1)
                n += 1
    kb.bump("mm", 64)

    @p.on("scalar")
    def _(e):
        n = 0
        for m in range(16):
            for t4 in range(4):
                e.wait_ge(smm, mm0 + n + 1)
                bank = psum[n % 4]
                dst = (R0(m * 2048 + t4 * 512, 512) if m < 8
                       else R1((m - 8) * 2048 + t4 * 512, 512))
                e.activation(dst, bank[:, :], AF.Identity, bias=0.0,
                             scale=1.0).then_inc(sea, 1)
                n += 1
    kb.bump("ea", 64)

    p.run()

    def v_lhs(kH, tc):
        return xnt[:, kH * T + tc * 128:kH * T + (tc + 1) * 128]

    def v_wsrc(kH):
        return bass.AP(kb.t["wqkvT"], 128 * kH * 3072 + 2048,
                       [[3072, 128], [1, 1024]])

    if SL1S < 3:
        return
    emit_sl1_v(kb, tag + "_v", v_lhs, v_wsrc, wbig, QFB)

    if SL1S < 4:
        return
    emit_sl1_kfkv(kb, tag)

    p = Phase(kb, tag + "_ar")

    @p.on("sync")
    def _(e):
        e.dma_start(out=kb.t["kv_loc"][:, :], in_=scr[:, 0:2080]).then_inc(dma, 16)
        kb.bump("dma", 16)
        e.wait_ge(dma, kb.val("dma"))

    p.run()
    p = Phase(kb, tag + "_ar2")

    @p.on("gpsimd")
    def _(e):
        e.collective_compute(
            "AllReduce", ALU.add, replica_groups=[list(range(N_CORES))],
            ins=[kb.t["kv_loc"].ap().opt()], outs=[kb.t["kv_red"].ap().opt()],
        ).then_inc(scc, 1)
        kb.bump("cc", 1)

    p.run()
    ar_cc = kb.val("cc")

    if SL1S < 6:
        return
    emit_sl1_qfattn(kb, tag, ar_cc)

    if SL1S < 7:
        return
    p = Phase(kb, tag + "_rec")

    @p.on("scalar")
    def _(e):
        for tc in range(NCH):
            e.activation(aux[:, tc * D:(tc + 1) * D],
                         stream[:, tc * D:(tc + 1) * D], AF.Identity,
                         bias=kb.t["s_nmr_sl"][:, tc:tc + 1],
                         scale=kb.t["s_rs_sl"][:, tc:tc + 1])

    p.run()

    def wo_lhs(kH, tc):
        return xnt[:, kH * 2048 + tc * 128:kH * 2048 + (tc + 1) * 128]

    def wo_wsrc(kH):
        return bass.AP(kb.t["woutT"], 128 * kH * 1024, [[1024, 128], [1, 1024]])

    def wo_out(e, bank, tc, dh):
        sl = aux[:, tc * D + dh * 512:tc * D + (dh + 1) * 512]
        return e.scalar_tensor_tensor(sl, bank[:, :], 1.0, sl, ALU.mult, ALU.add)

    emit_proj_B(kb, tag + "_wo", wo_lhs, 8, wo_wsrc, True,
                ("ab_out_r", D), wo_out)

    if SL1S < 8:
        return
    p = Phase(kb, tag + "_gbrows")

    @p.on("sync")
    def _(e):
        kb.row_load(e, dma, [("lng_r", "agamma", D, 0), ("lnb_r", "abeta", D, 1)])
        e.wait_ge(dma, kb.val("dma"))

    p.run()
    emit_ln_transpose(kb, aux, xnt, tag + "_ln1",
                      gamma_beta=(kb.t["agamma_c"], kb.t["abeta_c"]),
                      tm_affine_out=(aux, "lng_r", "lnb_r"))

    if SL1S < 9:
        return
    emit_sl1_ffn(kb, tag)

    if SL1S < 10:
        return
    emit_ln_final_add(kb, tag + "_ln2", aux, stream, "agamma", "abeta",
                      rows_loaded=True)


def emit_fm_transpose(kb, tag, src_tm, dst_xnt):
    smm, sev = kb.sem("mm"), kb.sem("ev")
    eye_s = kb.t["eye_s"]
    pst = kb.t["ps_bf"]
    p = Phase(kb, f"{tag}_tr")
    mm0, ev0 = kb.val("mm"), kb.val("ev")

    @p.on("tensor")
    def _(e):
        for tc in range(NCH):
            if tc >= 1:
                e.wait_ge(sev, ev0 + tc)
            for j in range(KD):
                e.transpose(pst[:, j * 128:(j + 1) * 128],
                            src_tm[:, tc * 1024 + j * 128:tc * 1024 + (j + 1) * 128],
                            eye_s[:, :]).then_inc(smm, 1)
    kb.bump("mm", NCH * KD)

    @p.on("vector")
    def _(e):
        for tc in range(NCH):
            e.wait_ge(smm, mm0 + KD * (tc + 1))
            dst = bass.AP(dst_xnt, tc * 128, [[KD * T, 128], [T, KD], [1, 128]])
            e.tensor_copy(dst, pst[:, :].rearrange("p (j f) -> p j f", j=KD)
                          ).then_inc(sev, 1)
    kb.bump("ev", NCH)

    p.run()


def _qsc_from_max(kb, tag, sources):
    dma, smm, sev, scc = (kb.sem("dma"), kb.sem("mm"), kb.sem("ev"),
                          kb.sem("cc"))
    psum = [kb.t[f"psb{i}"] for i in range(6)]
    qsc = kb.t["qsc"]
    scrrow = kb.t["wb0"].bitcast(F32)
    scrrow2 = kb.t["wb1"].bitcast(F32)

    p = Phase(kb, tag + "_r1")

    @p.on("vector")
    def _(e):
        for i, (ap, qcol, qmax) in enumerate(sources):
            e.tensor_reduce(scrrow[:, i:i + 1], ap, AX.X, ALU.max,
                            apply_absolute_value=True)

    p.run()
    p = Phase(kb, tag + "_hop")

    @p.on("sync")
    def _(e):
        n = len(sources)
        for i in range(n):
            e.dma_start(out=bass.AP(kb.t["hop_d"], i * 128, [[1, 128]]),
                        in_=scrrow[:, i:i + 1]).then_inc(dma, 16)
            kb.bump("dma", 16)
        e.wait_ge(dma, kb.val("dma"))
        for i in range(n):
            e.dma_start(out=scrrow[0:1, 128 + i * 128:128 + (i + 1) * 128],
                        in_=bass.AP(kb.t["hop_d"], i * 128, [[128, 1], [1, 128]])
                        ).then_inc(dma, 16)
            kb.bump("dma", 16)
        e.wait_ge(dma, kb.val("dma"))

    p.run()
    p = Phase(kb, tag + "_r2")

    @p.on("vector")
    def _(e):
        for i in range(len(sources)):
            e.tensor_reduce(scrrow2[0:1, i:i + 1],
                            scrrow[0:1, 128 + i * 128:128 + (i + 1) * 128],
                            AX.X, ALU.max)

    p.run()
    p = Phase(kb, tag + "_st")

    @p.on("sync")
    def _(e):
        e.dma_start(out=kb.t["mx_loc"][0:1, 0:len(sources)],
                    in_=scrrow2[0:1, 0:len(sources)]).then_inc(dma, 16)
        kb.bump("dma", 16)
        e.wait_ge(dma, kb.val("dma"))

    p.run()
    p = Phase(kb, tag + "_ar")

    @p.on("gpsimd")
    def _(e):
        e.collective_compute(
            "AllReduce", ALU.max, replica_groups=[list(range(N_CORES))],
            ins=[kb.t["mx_loc"].ap().opt()], outs=[kb.t["mx_red"].ap().opt()],
        ).then_inc(scc, 1)
        kb.bump("cc", 1)

    p.run()
    p = Phase(kb, tag + "_ld")

    @p.on("sync")
    def _(e):
        e.wait_ge(scc, kb.val("cc"))
        e.dma_start(out=scrrow2[0:1, 8:8 + len(sources)],
                    in_=kb.t["mx_red"][0:1, 0:len(sources)]).then_inc(dma, 16)
        kb.bump("dma", 16)
        e.wait_ge(dma, kb.val("dma"))

    p.run()
    p = Phase(kb, tag + "_bc")
    mm0 = kb.val("mm")

    @p.on("tensor")
    def _(e):
        for i in range(len(sources)):
            e.matmul(psum[i][:, 0:1], kb.t["ones_all"][0:1, 0:128],
                     scrrow2[0:1, 8 + i:9 + i], start=True,
                     stop=True).then_inc(smm, 1)
    kb.bump("mm", len(sources))

    p.run()
    p = Phase(kb, tag + "_sc")

    @p.on("vector")
    def _(e):
        for i, (ap, qcol, qmax) in enumerate(sources):
            ins = e.tensor_scalar(qsc[:, qcol:qcol + 1], psum[i][:, 0:1],
                                  1.0 / qmax, 1e-12, ALU.mult, ALU.max)
            ins.then_inc(sev, 1)
            kb.bump("ev", 1)
            e.wait_ge(sev, kb.val("ev"))
            e.reciprocal(qsc[:, qcol + 1:qcol + 2], qsc[:, qcol:qcol + 1]
                         ).then_inc(sev, 1)
            kb.bump("ev", 1)
            e.wait_ge(sev, kb.val("ev"))

    p.run()


def emit_L1_ffn_sublayer(kb, tag):
    dma, smm, sev, sea, scc = (kb.sem("dma"), kb.sem("mm"), kb.sem("ev"),
                               kb.sem("ea"), kb.sem("cc"))
    stream, xnt, auxb, aux = kb.t["stream"], kb.t["xnt"], kb.t["auxb"], kb.t["aux"]
    wbig = kb.t["wbig"]
    psum = [kb.t[f"psb{i}"] for i in range(6)]
    qsc = kb.t["qsc"]
    ustage = kb.t["phi_st"]
    s_rs, s_nmr, s_t0 = kb.t["s_rs"], kb.t["s_nmr"], kb.t["s_t0"]
    QMAXA = 2.0 ** 15 - 1.0

    emit_ln_stats(kb, stream, tag + "_ln")

    p = Phase(kb, tag + "_xmax")
    ea0, ev0 = kb.val("ea"), kb.val("ev")

    @p.on("scalar")
    def _(e):
        for tc in range(NCH):
            if tc >= 1:
                e.wait_ge(sev, ev0 + tc)
            e.activation(ustage[:, 0:1024], stream[:, tc * D:(tc + 1) * D],
                         AF.Identity, bias=s_nmr[:, tc:tc + 1],
                         scale=s_rs[:, tc:tc + 1]).then_inc(sea, 1)
    kb.bump("ea", NCH)

    @p.on("vector")
    def _(e):
        for tc in range(NCH):
            e.wait_ge(sea, ea0 + tc + 1)
            e.tensor_reduce(s_t0[:, tc:tc + 1], ustage[:, 0:1024], AX.X,
                            ALU.max, apply_absolute_value=True).then_inc(sev, 1)
    kb.bump("ev", NCH)

    p.run()
    _qsc_from_max(kb, tag + "_a1", [(s_t0[:, :], 0, QMAXA)])

    xq_tm = auxb[:, 0:16384]
    p = Phase(kb, tag + "_xq")
    ea0, ev0 = kb.val("ea"), kb.val("ev")

    @p.on("scalar")
    def _(e):
        for tc in range(NCH):
            if tc >= 1:
                e.wait_ge(sev, ev0 + 2 * tc)
            e.activation(ustage[:, 0:1024], stream[:, tc * D:(tc + 1) * D],
                         AF.Identity, bias=s_nmr[:, tc:tc + 1],
                         scale=s_rs[:, tc:tc + 1]).then_inc(sea, 1)
    kb.bump("ea", NCH)

    @p.on("vector")
    def _(e):
        for tc in range(NCH):
            e.wait_ge(sea, ea0 + tc + 1)
            e.tensor_scalar(ustage[:, 0:1024], ustage[:, 0:1024],
                                         qsc[:, 1:2], MAGIC, ALU.mult,
                                         ALU.add).then_inc(sev, 1)
            fence(e, 12)
            e.tensor_scalar(xq_tm[:, tc * 1024:(tc + 1) * 1024],
                            ustage[:, 0:1024], MAGIC, qsc[:, 0:1],
                            ALU.subtract, ALU.mult).then_inc(sev, 1)
    kb.bump("ev", 2 * NCH)

    p.run()
    emit_fm_transpose(kb, tag + "_xqt", xq_tm, xnt)

    for nm, ppoff, dnm, mxc in [("up", kb.PPD, "up_d", 0),
                                ("gate", kb.PPC, "gate_d", 96)]:
        _l1ffn_proj_spill(kb, f"{tag}_{nm}", ppoff, dnm, mxc)
    _qsc_from_max(kb, tag + "_a2",
                  [(kb.t["phi_st"][:, 0:96], 2, QMAXA),
                   (kb.t["phi_st"][:, 96:192], 4, QMAXA)])

    p = Phase(kb, tag + "_sb")

    @p.on("vector")
    def _(e):
        e.tensor_scalar(qsc[:, 8:9], qsc[:, 4:5], -MAGIC, None, ALU.mult
                        ).then_inc(sev, 1)
        kb.bump("ev", 1)
        e.wait_ge(sev, kb.val("ev"))

    p.run()

    _l1ffn_stageB(kb, tag + "_gs")
    _qsc_from_max(kb, tag + "_a3", [(kb.t["phi_st"][:, 0:24], 6, QMAXA)])

    _l1ffn_stageC(kb, tag + "_pr")
    _qsc_from_max(kb, tag + "_a4", [(kb.t["phi_st"][:, 0:24], 10, QMAXA)])

    _l1ffn_stageD(kb, tag + "_oq")

    _l1ffn_down(kb, tag + "_dn")


def _l1ffn_proj_spill(kb, tag, ppoff, dram_name, mxcol0):
    dma, dmag, smm, sev = (kb.sem("dma"), kb.sem("dmag"), kb.sem("mm"),
                           kb.sem("ev"))
    xnt, aux = kb.t["xnt"], kb.t["aux"]
    wbufs = [kb.t["wb0"], kb.t["wb1"]]
    psum = [kb.t[f"psb{i}"] for i in range(6)]
    mxb = kb.t["phi_st"]
    pp_out = kb.t["pp_out"]
    p = Phase(kb, tag)
    dm0, dg0, mm0, ev0 = (kb.val("dma"), kb.val("dmag"), kb.val("mm"),
                          kb.val("ev"))

    @p.on("sync")
    def _(e):
        for m in range(24):
            buf = wbufs[m % 2]
            if m >= 2:
                e.wait_ge(smm, mm0 + (m - 1) * 4)
            c, ml = m // 3, m % 3
            e.dma_start(out=buf[:, :],
                        in_=bass.AP(pp_out, c * kb.PPTOT + ppoff + ml * 131072,
                                    [[1024, 128], [1, 1024]])).then_inc(dma, 16)
            kb.bump("dma", 16)

    @p.on("gpsimd")
    def _(e):
        for n in range(96):
            e.wait_ge(sev, ev0 + n * 2 + 1)
            m, t4 = n // 4, n % 4
            e.dma_start(out=bass.AP(kb.t[dram_name], m * 262144 + t4 * 512,
                                    [[2048, 128], [1, 512]]),
                        in_=aux[:, (n % 4) * 512:(n % 4 + 1) * 512]
                        ).then_inc(dmag, 16)
            kb.bump("dmag", 16)
        e.wait_ge(dmag, kb.val("dmag"))

    @p.on("tensor")
    def _(e):
        n = 0
        for m in range(24):
            buf = wbufs[m % 2]
            e.wait_ge(dma, dm0 + (m + 1) * 16)
            for t4 in range(4):
                bank = psum[n % 4]
                if n >= 4:
                    e.wait_ge(sev, ev0 + (n - 4) * 2 + 2)
                last = None
                for k in range(8):
                    last = e.matmul(bank[:, :], buf[:, k * 128:(k + 1) * 128],
                                    xnt[:, k * T + t4 * 512:k * T + (t4 + 1) * 512],
                                    start=(k == 0), stop=(k == 7))
                last.then_inc(smm, 1)
                n += 1
    kb.bump("mm", 96)

    @p.on("vector")
    def _(e):
        n = 0
        for m in range(24):
            for t4 in range(4):
                e.wait_ge(smm, mm0 + n + 1)
                bank = psum[n % 4]
                sbuf = aux[:, (n % 4) * 512:(n % 4 + 1) * 512]
                if n >= 4:
                    e.wait_ge(dmag, dg0 + (n - 3) * 16)
                e.tensor_copy(sbuf, bank[:, :]).then_inc(sev, 1)
                e.tensor_reduce(mxb[:, mxcol0 + n:mxcol0 + n + 1], bank[:, :],
                                AX.X, ALU.max,
                                apply_absolute_value=True).then_inc(sev, 1)
                n += 1
    kb.bump("ev", 192)

    p.run()


def _l1ffn_stageB(kb, tag):
    dma, dmag, sev, sea = (kb.sem("dma"), kb.sem("dmag"), kb.sem("ev"),
                           kb.sem("ea"))
    aux = kb.t["aux"]
    qsc = kb.t["qsc"]
    mxb = kb.t["phi_st"]
    gb = [aux[:, 2048:4096], aux[:, 4096:6144]]
    sigb = aux[:, 6144:8192]
    gsb = [aux[:, 8192:10240], aux[:, 10240:12288]]
    p = Phase(kb, tag)
    dm0, dg0, ev0, ea0 = (kb.val("dma"), kb.val("dmag"), kb.val("ev"),
                          kb.val("ea"))

    @p.on("sync")
    def _(e):
        for m in range(24):
            if m >= 2:
                e.wait_ge(sev, ev0 + (m - 1) * 5)
            e.dma_start(out=gb[m % 2][:, :],
                        in_=bass.AP(kb.t["gate_d"], m * 262144,
                                    [[2048, 128], [1, 2048]])).then_inc(dma, 16)
            kb.bump("dma", 16)

    @p.on("gpsimd")
    def _(e):
        for m in range(24):
            e.wait_ge(sev, ev0 + m * 5 + 5)
            e.dma_start(out=bass.AP(kb.t["gs_d"], m * 262144,
                                    [[2048, 128], [1, 2048]]),
                        in_=gsb[m % 2][:, :]).then_inc(dmag, 16)
            kb.bump("dmag", 16)
        e.wait_ge(dmag, kb.val("dmag"))

    @p.on("vector")
    def _(e):
        for m in range(24):
            g = gb[m % 2]
            gs = gsb[m % 2]
            e.wait_ge(dma, dm0 + (m + 1) * 16)
            if m >= 2:
                e.wait_ge(dmag, dg0 + (m - 1) * 16)
            e.tensor_scalar(g[:, :], g[:, :], qsc[:, 5:6], MAGIC,
                                         ALU.mult, ALU.add).then_inc(sev, 1)
            fence(e, 12)
            e.wait_ge(sea, ea0 + m + 1)
            e.tensor_scalar(sigb[:, :], sigb[:, :], 65536.0,
                                         MAGIC, ALU.mult,
                                         ALU.add).then_inc(sev, 1)
            fence(e, 12)
            e.tensor_scalar(sigb[:, :], sigb[:, :],
                                         MAGIC + 65535.0, MAGIC, ALU.min,
                                         ALU.subtract).then_inc(sev, 1)
            fence(e, 12)
            e.tensor_scalar(g[:, :], g[:, :], MAGIC, qsc[:, 4:5],
                                         ALU.subtract,
                                         ALU.mult).then_inc(sev, 1)
            fence(e, 12)
            e.scalar_tensor_tensor(gs[:, :], sigb[:, :],
                                                1.0 / 65536.0, g[:, :],
                                                ALU.mult,
                                                ALU.mult).then_inc(sev, 1)
            fence(e, 12)
            e.tensor_reduce(mxb[:, m:m + 1], gs[:, :], AX.X, ALU.max,
                            apply_absolute_value=True)
    kb.bump("ev", 24 * 5)

    @p.on("scalar")
    def _(e):
        for m in range(24):
            g = gb[m % 2]
            e.wait_ge(sev, ev0 + m * 5 + 1)
            e.activation(sigb[:, :], g[:, :], AF.Sigmoid, bias=qsc[:, 8:9],
                         scale=qsc[:, 4:5]).then_inc(sea, 1)
    kb.bump("ea", 24)

    p.run()


def _l1ffn_stageC(kb, tag):
    dma, dmag, sev = kb.sem("dma"), kb.sem("dmag"), kb.sem("ev")
    aux = kb.t["aux"]
    qsc = kb.t["qsc"]
    mxb = kb.t["phi_st"]
    ub = [aux[:, 2048:4096], aux[:, 4096:6144]]
    gb = [aux[:, 6144:8192], aux[:, 8192:10240]]
    pb = [aux[:, 10240:12288], aux[:, 12288:14336]]
    p = Phase(kb, tag)
    dm0, dg0, ev0 = kb.val("dma"), kb.val("dmag"), kb.val("ev")

    @p.on("sync")
    def _(e):
        for m in range(24):
            if m >= 2:
                e.wait_ge(sev, ev0 + (m - 1) * 5)
            e.dma_start(out=ub[m % 2][:, :],
                        in_=bass.AP(kb.t["up_d"], m * 262144,
                                    [[2048, 128], [1, 2048]])).then_inc(dma, 16)
            kb.bump("dma", 16)
            e.dma_start(out=gb[m % 2][:, :],
                        in_=bass.AP(kb.t["gs_d"], m * 262144,
                                    [[2048, 128], [1, 2048]])).then_inc(dma, 16)
            kb.bump("dma", 16)

    @p.on("gpsimd")
    def _(e):
        for m in range(24):
            e.wait_ge(sev, ev0 + m * 5 + 5)
            e.dma_start(out=bass.AP(kb.t["prod_d"], m * 262144,
                                    [[2048, 128], [1, 2048]]),
                        in_=pb[m % 2][:, :]).then_inc(dmag, 16)
            kb.bump("dmag", 16)
        e.wait_ge(dmag, kb.val("dmag"))

    @p.on("vector")
    def _(e):
        for m in range(24):
            u, g, pr = ub[m % 2], gb[m % 2], pb[m % 2]
            e.wait_ge(dma, dm0 + (m + 1) * 32)
            if m >= 2:
                e.wait_ge(dmag, dg0 + (m - 1) * 16)
            e.tensor_scalar(u[:, :], u[:, :], qsc[:, 3:4],
                                         MAGIC, ALU.mult,
                                         ALU.add).then_inc(sev, 1)
            fence(e, 12)
            e.tensor_scalar(u[:, :], u[:, :], MAGIC, qsc[:, 2:3],
                                         ALU.subtract,
                                         ALU.mult).then_inc(sev, 1)
            fence(e, 12)
            e.tensor_scalar(g[:, :], g[:, :], qsc[:, 7:8], MAGIC,
                                         ALU.mult, ALU.add).then_inc(sev, 1)
            fence(e, 12)
            e.tensor_scalar(g[:, :], g[:, :], MAGIC, qsc[:, 6:7],
                                         ALU.subtract,
                                         ALU.mult).then_inc(sev, 1)
            fence(e, 12)
            e.tensor_tensor(pr[:, :], g[:, :], u[:, :], ALU.mult
                                         ).then_inc(sev, 1)
            fence(e, 12)
            e.tensor_reduce(mxb[:, m:m + 1], pr[:, :], AX.X, ALU.max,
                            apply_absolute_value=True)
    kb.bump("ev", 24 * 5)

    p.run()


def _l1ffn_stageD(kb, tag):
    dma, sev = kb.sem("dma"), kb.sem("ev")
    aux, auxb = kb.t["aux"], kb.t["auxb"]
    xnt, wbig = kb.t["xnt"], kb.t["wbig"]
    qsc = kb.t["qsc"]
    pb = [aux[:, 8192:10240], aux[:, 10240:12288]]
    p = Phase(kb, tag)
    dm0, ev0 = kb.val("dma"), kb.val("ev")

    def oslot(m):
        if m < 8:
            return xnt[:, m * 2048:(m + 1) * 2048]
        if m < 16:
            return wbig[:, (m - 8) * 2048:(m - 7) * 2048]
        return auxb[:, (m - 16) * 2048:(m - 15) * 2048]

    @p.on("sync")
    def _(e):
        for m in range(24):
            if m >= 2:
                e.wait_ge(sev, ev0 + (m - 1) * 2)
            e.dma_start(out=pb[m % 2][:, :],
                        in_=bass.AP(kb.t["prod_d"], m * 262144,
                                    [[2048, 128], [1, 2048]])).then_inc(dma, 16)
            kb.bump("dma", 16)

    @p.on("vector")
    def _(e):
        for m in range(24):
            pr = pb[m % 2]
            e.wait_ge(dma, dm0 + (m + 1) * 16)
            e.tensor_scalar(pr[:, :], pr[:, :], qsc[:, 11:12],
                                         MAGIC, ALU.mult,
                                         ALU.add).then_inc(sev, 1)
            fence(e, 12)
            e.tensor_scalar(oslot(m), pr[:, :], MAGIC, qsc[:, 10:11],
                            ALU.subtract, ALU.mult).then_inc(sev, 1)
    kb.bump("ev", 24 * 2)

    p.run()


def _l1ffn_down(kb, tag):
    dma, smm, sev = kb.sem("dma"), kb.sem("mm"), kb.sem("ev")
    stream, aux, auxb = kb.t["stream"], kb.t["aux"], kb.t["auxb"]
    xnt, wbig = kb.t["xnt"], kb.t["wbig"]
    wbufs = [kb.t["wb0"], kb.t["wb1"]]
    psum = [kb.t[f"psb{i}"] for i in range(6)]

    def oslot_tile(m, tau):
        if m < 8:
            base = xnt
            c0 = m * 2048
        elif m < 16:
            base = wbig
            c0 = (m - 8) * 2048
        else:
            base = auxb
            c0 = (m - 16) * 2048
        return base[:, c0 + tau * 128:c0 + (tau + 1) * 128]

    groups = [[0, 1, 2], [3, 4, 5], [6, 7, 8], [9, 10, 11], [12, 13], [14, 15]]
    for gi, taus in enumerate(groups):
        p = Phase(kb, f"{tag}_{gi}")
        dm0, mm0 = kb.val("dma"), kb.val("mm")
        nt = len(taus)

        @p.on("sync")
        def _(e, taus=taus, mm0=mm0):
            for m in range(24):
                buf = wbufs[m % 2]
                if m >= 2:
                    e.wait_ge(smm, mm0 + m - 1)
                e.dma_start(out=buf[:, :],
                            in_=bass.AP(kb.t["dwTd"], m * 131072,
                                        [[1024, 128], [1, 1024]])
                            ).then_inc(dma, 16)
                kb.bump("dma", 16)

        @p.on("tensor")
        def _(e, taus=taus, dm0=dm0):
            for m in range(24):
                buf = wbufs[m % 2]
                e.wait_ge(dma, dm0 + (m + 1) * 16)
                last = None
                for ti, tau in enumerate(taus):
                    for dh in range(2):
                        bank = psum[ti * 2 + dh]
                        last = e.matmul(bank[:, :], oslot_tile(m, tau),
                                        buf[:, dh * 512:(dh + 1) * 512],
                                        start=(m == 0), stop=(m == 23))
                last.then_inc(smm, 1)
        kb.bump("mm", 24)

        @p.on("vector")
        def _(e, taus=taus, mm0=mm0):
            e.wait_ge(smm, mm0 + 24)
            for ti, tau in enumerate(taus):
                for dh in range(2):
                    bank = psum[ti * 2 + dh]
                    sl = stream[:, tau * D + dh * 512:tau * D + (dh + 1) * 512]
                    e.scalar_tensor_tensor(sl, bank[:, :], 0.5, sl, ALU.mult,
                                           ALU.add)

        p.run()



def emit_sl1_v(kb, tag, lhs_fn, wsrc_fn, wbig, QFB):
    dma, dmag, smm, sev = (kb.sem("dma"), kb.sem("dmag"), kb.sem("mm"),
                           kb.sem("ev"))
    wbufs = [kb.t["wb0"], kb.t["wb1"]]
    psum = [kb.t[f"psb{i}"] for i in range(6)]
    vstg = [wbig[:, QFB + i * 1024:QFB + (i + 1) * 1024] for i in range(4)]
    groups = [[0, 1, 2], [3, 4, 5], [6, 7, 8], [9, 10, 11], [12, 13], [14, 15]]
    dm_start = kb.val("dma")
    n_kH = 8
    for gi, taus in enumerate(groups):
        p = Phase(kb, f"{tag}_{gi}")
        dg0, mm0, ev0 = kb.val("dmag"), kb.val("mm"), kb.val("ev")
        nt = len(taus)

        @p.on("gpsimd")
        def _(e, taus=taus, mm0=mm0):
            for kH in range(n_kH):
                buf = wbufs[kH % 2]
                if kH >= 2:
                    e.wait_ge(smm, mm0 + kH - 1)
                e.dma_start(out=buf[:, :], in_=wsrc_fn(kH)).then_inc(dmag, 16)
                kb.bump("dmag", 16)

        @p.on("tensor")
        def _(e, taus=taus, dg0=dg0):
            for kH in range(n_kH):
                buf = wbufs[kH % 2]
                e.wait_ge(dmag, dg0 + (kH + 1) * 16)
                last = None
                for ti, tau in enumerate(taus):
                    for dh in range(2):
                        bank = psum[ti * 2 + dh]
                        last = e.matmul(bank[:, :], lhs_fn(kH, tau),
                                        buf[:, dh * 512:(dh + 1) * 512],
                                        start=(kH == 0), stop=(kH == n_kH - 1))
                last.then_inc(smm, 1)
        kb.bump("mm", n_kH)

        @p.on("vector")
        def _(e, taus=taus, mm0=mm0):
            e.wait_ge(smm, mm0 + n_kH)
            for ti, tau in enumerate(taus):
                for dh in range(2):
                    bank = psum[ti * 2 + dh]
                    if tau >= 4:
                        e.wait_ge(dma, dm_start + (tau - 3) * 16)
                    ins = e.tensor_copy(vstg[tau % 4][:, dh * 512:(dh + 1) * 512],
                                        bank[:, :])
                    if dh == 1:
                        ins.then_inc(sev, 1)
                        kb.bump("ev", 1)

        @p.on("sync")
        def _(e, taus=taus, ev0=ev0):
            for ti, tau in enumerate(taus):
                e.wait_ge(sev, ev0 + ti + 1)
                e.dma_start(out=bass.AP(kb.t["v_stage"], tau * 131072,
                                        [[1024, 128], [1, 1024]]),
                            in_=vstg[tau % 4][:, :]).then_inc(dma, 16)
                kb.bump("dma", 16)

        p.run()
    p = Phase(kb, f"{tag}_fin")

    @p.on("sync")
    def _(e):
        e.wait_ge(dma, kb.val("dma"))

    p.run()


def emit_sl1_kfkv(kb, tag):
    dma, smm, sev, sea = (kb.sem("dma"), kb.sem("mm"), kb.sem("ev"),
                          kb.sem("ea"))
    wbig, auxb = kb.t["wbig"], kb.t["auxb"]
    psum = [kb.t[f"psb{i}"] for i in range(6)]
    scr = kb.t["xntf"]
    phist = kb.t["phi_st"]
    onesc = kb.t["onesc_b"]
    vh = [kb.t["wb0"], kb.t["wb1"]]
    PRJ, KFH = 0, 4096

    p = Phase(kb, tag + "_kfkv")
    dm0, mm0, ev0, ea0 = (kb.val("dma"), kb.val("mm"), kb.val("ev"),
                          kb.val("ea"))
    KF_MM, KV_INC = 8, 2
    V_PER_H = 8 * 2 + 2
    A_PER_H = 8

    @p.on("sync")
    def _(e):
        for h in range(16):
            if h >= 2:
                e.wait_ge(smm, mm0 + (h - 1) * (KF_MM + KV_INC))
            e.dma_start(out=vh[h % 2][:, :],
                        in_=bass.AP(kb.t["v_stage"], 64 * h,
                                    [[1024, 128], [131072, 16], [1, 64]]),
                        allow_slow_non_contiguous=True).then_inc(dma, 16)
            kb.bump("dma", 16)

    @p.on("tensor")
    def _(e):
        for h in range(16):
            base = (h % 2) * 64
            for tcp in range(8):
                bank = psum[tcp % 2]
                if h > 0 or tcp >= 2:
                    gp = h * 8 + tcp
                    if gp >= 2:
                        e.wait_ge(sev, ev0 + (gp - 2) // 8 * V_PER_H
                                  + _p3_count((gp - 2) % 8))
                last = None
                for u in range(2):
                    tc = tcp * 2 + u
                    lhsT = auxb[base:base + 64,
                                16384 + (h // 2) * 2048 + tc * 128:
                                16384 + (h // 2) * 2048 + (tc + 1) * 128]
                    last = e.matmul(bank[:, u * 256:(u + 1) * 256], lhsT,
                                    wbig[base:base + 64,
                                         PRJ + 256 * h:PRJ + 256 * (h + 1)],
                                    start=(u == 0), stop=(u == 1))
                last.then_inc(smm, 1)
            e.wait_ge(sea, ea0 + (h + 1) * A_PER_H)
            e.wait_ge(sev, ev0 + h * V_PER_H + 16)
            e.wait_ge(dma, dm0 + (h + 1) * 16)
            for fc in range(2):
                bank = psum[2 + (h % 2) * 2 + fc]
                if h >= 2:
                    e.wait_ge(sev, ev0 + (h - 2) * V_PER_H + 16 + 2)
                last = None
                for tc in range(16):
                    lhsT = wbig[:, KFH + 256 * tc + 128 * fc:
                                KFH + 256 * tc + 128 * (fc + 1)]
                    last = e.matmul(bank[:, 0:64], lhsT,
                                    vh[h % 2][:, 64 * tc:64 * (tc + 1)],
                                    start=(tc == 0), stop=False)
                    last = e.matmul(bank[:, 64:65], lhsT, onesc[:, 0:1],
                                    start=False, stop=(tc == 15))
                last.then_inc(smm, 1)
    kb.bump("mm", 16 * (KF_MM + KV_INC))

    @p.on("vector")
    def _(e):
        for h in range(16):
            for tcp in range(8):
                gp = h * 8 + tcp
                bank = psum[tcp % 2]
                e.wait_ge(smm, mm0 + h * (KF_MM + KV_INC) + tcp + 1)
                e.tensor_scalar(phist[:, (tcp % 2) * 512:(tcp % 2 + 1) * 512],
                                bank[:, :], 0.0, None,
                                ALU.min).then_inc(sev, 1)
                e.wait_ge(sea, ea0 + h * A_PER_H + tcp + 1)
                e.scalar_tensor_tensor(
                    wbig[:, 4096 + 512 * tcp:4096 + 512 * (tcp + 1)],
                    bank[:, :], 0.0,
                    phist[:, (tcp % 2) * 512:(tcp % 2 + 1) * 512],
                    ALU.max, ALU.add).then_inc(sev, 1)
            for fc in range(2):
                bank = psum[2 + (h % 2) * 2 + fc]
                e.wait_ge(smm, mm0 + h * (KF_MM + KV_INC) + KF_MM + fc + 1)
                e.tensor_copy(scr[:, 65 * (2 * h + fc):65 * (2 * h + fc) + 65],
                              bank[:, 0:65]).then_inc(sev, 1)
    kb.bump("ev", 16 * V_PER_H)

    @p.on("scalar")
    def _(e):
        for h in range(16):
            for tcp in range(8):
                e.wait_ge(sev, ev0 + h * V_PER_H + _p1_count(tcp))
                half = phist[:, (tcp % 2) * 512:(tcp % 2 + 1) * 512]
                e.activation(half, half, AF.Exp, bias=0.0,
                             scale=1.0).then_inc(sea, 1)
    kb.bump("ea", 16 * A_PER_H)

    p.run()


def _p1_count(tcp):
    return 2 * tcp + 1


def _p3_count(tcp):
    return 2 * tcp + 2


def emit_sl1_qfattn(kb, tag, ar_cc):
    dma, smm, sev, sea, scc = (kb.sem("dma"), kb.sem("mm"), kb.sem("ev"),
                               kb.sem("ea"), kb.sem("cc"))
    wbig, auxb, xnt = kb.t["wbig"], kb.t["auxb"], kb.t["xnt"]
    psum = [kb.t[f"psb{i}"] for i in range(6)]
    scr = kb.t["xntf"]
    dstage = kb.t["wb0"].bitcast(F32)[0:1, 0:512]
    astage = kb.t["wb1"].bitcast(F32)
    phist = kb.t["phi_st"]
    PRJ, QFB, KVB = 0, 8192, 12288

    p = Phase(kb, tag + "_ld")

    @p.on("sync")
    def _(e):
        e.wait_ge(scc, ar_cc)
        e.dma_start(out=scr[:, 0:2080], in_=kb.t["kv_red"][:, :]).then_inc(dma, 16)
        kb.bump("dma", 16)
        e.wait_ge(dma, kb.val("dma"))

    p.run()
    p = Phase(kb, tag + "_cast")

    @p.on("vector")
    def _(e):
        e.tensor_copy(wbig[:, KVB:KVB + 2080], scr[:, 0:2080])

    p.run()

    p = Phase(kb, tag + "_main")
    mm0, ev0, ea0 = kb.val("mm"), kb.val("ev"), kb.val("ea")
    T_PER_H, V_PER_H, A_PER_H = 20, 28, 8

    @p.on("tensor")
    def _(e):
        for h in range(16):
            base = (h % 2) * 64
            hm, ht = mm0 + h * T_PER_H, h * T_PER_H
            for i in range(8):
                fh, tt = i // 4, i % 4
                bank = psum[i % 2]
                gq = h * 8 + i
                if gq >= 2:
                    ph, pi = (gq - 2) // 8, (gq - 2) % 8
                    e.wait_ge(sev, ev0 + ph * V_PER_H + _p3_count(pi))
                e.matmul(bank[:, :],
                         wbig[base:base + 64, PRJ + 256 * h + 128 * fh:
                              PRJ + 256 * h + 128 * (fh + 1)],
                         auxb[base:base + 64,
                              (h // 2) * 2048 + tt * 512:(h // 2) * 2048 + (tt + 1) * 512],
                         start=True, stop=True).then_inc(smm, 1)
            for tt in range(4):
                e.wait_ge(sev, ev0 + h * V_PER_H + 16)
                A = psum[2 + tt % 2]
                Dk = psum[4]
                gidx = h * 4 + tt
                if gidx >= 2:
                    pg, pt = (gidx - 2) // 4, (gidx - 2) % 4
                    e.wait_ge(sev, ev0 + pg * V_PER_H + 16 + (pt + 1) * 3)
                if gidx >= 1:
                    pg2, pt2 = (gidx - 1) // 4, (gidx - 1) % 4
                    e.wait_ge(sev, ev0 + pg2 * V_PER_H + 16 + pt2 * 3 + 2)
                last = None
                for fc in range(2):
                    last = e.matmul(A[base:base + 64, :],
                                    wbig[:, KVB + 65 * (2 * h + fc):
                                         KVB + 65 * (2 * h + fc) + 64],
                                    wbig[:, QFB + fc * 2048 + tt * 512:
                                         QFB + fc * 2048 + (tt + 1) * 512],
                                    start=(fc == 0), stop=(fc == 1))
                last.then_inc(smm, 1)
                last = None
                for fc in range(2):
                    last = e.matmul(Dk[0:1, :],
                                    wbig[:, KVB + 65 * (2 * h + fc) + 64:
                                         KVB + 65 * (2 * h + fc) + 65],
                                    wbig[:, QFB + fc * 2048 + tt * 512:
                                         QFB + fc * 2048 + (tt + 1) * 512],
                                    start=(fc == 0), stop=(fc == 1))
                last.then_inc(smm, 1)
                e.wait_ge(sev, ev0 + h * V_PER_H + 16 + tt * 3 + 2)
                e.matmul(psum[5][base:base + 64, :],
                         kb.t["ones_all"][0:1, 0:64], dstage,
                         start=True, stop=True).then_inc(smm, 1)
    kb.bump("mm", 16 * T_PER_H)

    @p.on("vector")
    def _(e):
        for h in range(16):
            base = (h % 2) * 64
            for i in range(8):
                fh, tt = i // 4, i % 4
                bank = psum[i % 2]
                e.wait_ge(smm, mm0 + h * T_PER_H + i + 1)
                e.tensor_scalar(phist[:, (i % 2) * 512:(i % 2 + 1) * 512],
                                bank[:, :], 0.0, None, ALU.min).then_inc(sev, 1)
                e.wait_ge(sea, ea0 + h * A_PER_H + i + 1)
                e.scalar_tensor_tensor(
                    wbig[:, QFB + fh * 2048 + tt * 512:QFB + fh * 2048 + (tt + 1) * 512],
                    bank[:, :], 0.0,
                    phist[:, (i % 2) * 512:(i % 2 + 1) * 512],
                    ALU.max, ALU.add).then_inc(sev, 1)
            for tt in range(4):
                e.wait_ge(smm, mm0 + h * T_PER_H + 8 + tt * 3 + 2)
                e.tensor_scalar(dstage, psum[4][0:1, :], 1e-8,
                                None, ALU.add).then_inc(sev, 1)
                e.wait_ge(sev, ev0 + h * V_PER_H + 16 + tt * 3 + 1)
                e.reciprocal(dstage, dstage).then_inc(sev, 1)
                e.wait_ge(smm, mm0 + h * T_PER_H + 8 + tt * 3 + 3)
                flush(kb, e, e.tensor_copy(astage[base:base + 64, 0:512],
                                           psum[2 + tt % 2][base:base + 64, :]))
                e.tensor_tensor(
                    xnt[base:base + 64,
                        (h // 2) * 2048 + tt * 512:(h // 2) * 2048 + (tt + 1) * 512],
                    astage[base:base + 64, 0:512],
                    psum[5][base:base + 64, :], ALU.mult).then_inc(sev, 1)
    kb.bump("ev", 16 * V_PER_H)

    @p.on("scalar")
    def _(e):
        for h in range(16):
            for i in range(8):
                e.wait_ge(sev, ev0 + h * V_PER_H + _p1_count(i))
                half = phist[:, (i % 2) * 512:(i % 2 + 1) * 512]
                e.activation(half, half, AF.Exp, bias=0.0,
                             scale=1.0).then_inc(sea, 1)
    kb.bump("ea", 16 * A_PER_H)

    p.run()


def emit_sl1_ffn(kb, tag):
    dmag, smm, sev, sea = (kb.sem("dmag"), kb.sem("mm"), kb.sem("ev"),
                           kb.sem("ea"))
    wbig, xnt, aux = kb.t["wbig"], kb.t["xnt"], kb.t["aux"]
    wbufs = [kb.t["wb0"], kb.t["wb1"]]
    psum = [kb.t[f"psb{i}"] for i in range(6)]
    ab1_c = kb.t["ab1_c"]

    for ttQ in range(4):
        p = Phase(kb, f"{tag}_fh{ttQ}")
        dg0, mm0, ea0 = kb.val("dmag"), kb.val("mm"), kb.val("ea")

        @p.on("gpsimd")
        def _(e, ttQ=ttQ, mm0=mm0):
            for m in range(32):
                buf = wbufs[m % 2]
                if m >= 2:
                    e.wait_ge(smm, mm0 + m - 1)
                e.dma_start(out=bass.AP(buf, 0, [[1024, 128], [128, 8], [1, 128]]),
                            in_=bass.AP(kb.t["aw1T"], 128 * m,
                                        [[4096, 128], [128 * 4096, 8], [1, 128]])
                            ).then_inc(dmag, 16)
                kb.bump("dmag", 16)

        @p.on("tensor")
        def _(e, ttQ=ttQ, dg0=dg0, ea0=ea0):
            for m in range(32):
                buf = wbufs[m % 2]
                e.wait_ge(dmag, dg0 + (m + 1) * 16)
                bank = psum[m % 4]
                if m >= 4:
                    e.wait_ge(sea, ea0 + m - 3)
                last = None
                for k in range(8):
                    last = e.matmul(bank[:, :], buf[:, k * 128:(k + 1) * 128],
                                    xnt[:, k * T + ttQ * 512:k * T + (ttQ + 1) * 512],
                                    start=(k == 0), stop=(k == 7))
                last.then_inc(smm, 1)
        kb.bump("mm", 32)

        @p.on("scalar")
        def _(e, ttQ=ttQ, mm0=mm0):
            for m in range(32):
                e.wait_ge(smm, mm0 + m + 1)
                bank = psum[m % 4]
                e.activation(wbig[:, m * 512:(m + 1) * 512], bank[:, :],
                             AF.Gelu, bias=ab1_c[:, m:m + 1],
                             scale=1.0).then_inc(sea, 1)
        kb.bump("ea", 32)

        p.run()

        for gi, taus in enumerate([[0, 1], [2, 3]]):
            p = Phase(kb, f"{tag}_fd{ttQ}_{gi}")
            dg0, mm0, ev0 = kb.val("dmag"), kb.val("mm"), kb.val("ev")
            nt = len(taus)

            @p.on("gpsimd")
            def _(e, taus=taus, mm0=mm0):
                for kH in range(32):
                    buf = wbufs[kH % 2]
                    if kH >= 2:
                        e.wait_ge(smm, mm0 + kH - 1)
                    e.dma_start(out=buf[:, :],
                                in_=bass.AP(kb.t["aw2T"], 128 * kH * 1024,
                                            [[1024, 128], [1, 1024]])
                                ).then_inc(dmag, 16)
                    kb.bump("dmag", 16)

            @p.on("tensor")
            def _(e, taus=taus, ttQ=ttQ, dg0=dg0):
                for kH in range(32):
                    buf = wbufs[kH % 2]
                    e.wait_ge(dmag, dg0 + (kH + 1) * 16)
                    last = None
                    for ti, tau in enumerate(taus):
                        for dh in range(2):
                            bank = psum[ti * 2 + dh]
                            last = e.matmul(
                                bank[:, :],
                                wbig[:, kH * 512 + tau * 128:kH * 512 + (tau + 1) * 128],
                                buf[:, dh * 512:(dh + 1) * 512],
                                start=(kH == 0), stop=False)
                    last.then_inc(smm, 1)
                for ti, tau in enumerate(taus):
                    for dh in range(2):
                        bank = psum[ti * 2 + dh]
                        e.matmul(bank[:, :], kb.row_ones("ab2_r"),
                                 kb.row_ap("ab2_r", dh * 512, 512),
                                 start=False, stop=True).then_inc(smm, 1)
            kb.bump("mm", 32 + nt * 2)

            @p.on("vector")
            def _(e, taus=taus, ttQ=ttQ, mm0=mm0, nt=nt):
                e.wait_ge(smm, mm0 + 32 + nt * 2)
                for ti, tau in enumerate(taus):
                    tc = ttQ * 4 + tau
                    for dh in range(2):
                        bank = psum[ti * 2 + dh]
                        sl = aux[:, tc * D + dh * 512:tc * D + (dh + 1) * 512]
                        e.scalar_tensor_tensor(sl, bank[:, :], 1.0, sl,
                                               ALU.mult, ALU.add)

            p.run()



_SC = dict(zrow=0, CLS=2048, zer=4096, T_all=4224, totr=4480, Gr=4736,
           lam=4992, pre=5248, suf=5504, DF=5760, DE=6016, S=6272, Rb=7312,
           Sin=8352, Rin=9392, gat=10432, tmp=11472, psx=12512, attf=12544,
           attg=13824, dmat=15104, wwk=15120, den=15248, cew=15264,
           wfl=15360, mrow=15616, wwk2=15648, negT=15776)
_WB = dict(decw=0, ktm=128, kdf=1152, dfb=2176, sbf=2304, rbf=3344,
           att=4384, wwork=4608)


def emit_L1_attn_sublayer(kb, tag):
    SL4S = int(_os.environ.get("KMOD_SL4_STAGE", "99"))
    dma, dmag, smm, sev, sea, scc = (kb.sem("dma"), kb.sem("dmag"),
                                     kb.sem("mm"), kb.sem("ev"),
                                     kb.sem("ea"), kb.sem("cc"))
    stream, xnt, auxb, aux = (kb.t["stream"], kb.t["xnt"], kb.t["auxb"],
                              kb.t["aux"])
    wbig = kb.t["wbig"]
    psum = [kb.t[f"psb{i}"] for i in range(6)]

    emit_ln_stats(kb, stream, tag + "_ln0")
    p = Phase(kb, tag + "_xp")
    s_rs, s_nmr = kb.t["s_rs"], kb.t["s_nmr"]

    @p.on("scalar")
    def _(e):
        for tc in range(NCH):
            e.activation(aux[:, tc * D:(tc + 1) * D],
                         stream[:, tc * D:(tc + 1) * D], AF.Identity,
                         bias=s_nmr[:, tc:tc + 1], scale=s_rs[:, tc:tc + 1])

    @p.on("sync")
    def _(e):
        e.dma_start(out=kb.t["qsc"][0:16, 14:15],
                    in_=bass.AP(kb.t["decb"], 0, [[1, 16], [1, 1]]),
                    allow_slow_non_contiguous=True).then_inc(dma, 16)
        kb.bump("dma", 16)
        e.wait_ge(dma, kb.val("dma"))

    p.run()

    emit_ln_transpose(kb, aux, xnt, tag + "_ln1",
                      gamma_beta=(kb.t["ln1w_c"], kb.t["ln1b_c"]))

    p = Phase(kb, tag + "_sp")

    @p.on("vector")
    def _(e):
        e.tensor_scalar(kb.t["qsc"][0:16, 14:15], kb.t["qsc"][0:16, 14:15],
                        -1.0, None, ALU.mult)

    @p.on("sync")
    def _(e):
        for tc in range(NCH):
            e.dma_start(out=kb.t["stream_spill"][tc * 128:(tc + 1) * 128, :],
                        in_=stream[:, tc * D:(tc + 1) * D]).then_inc(dma, 16)
            kb.bump("dma", 16)
            e.dma_start(out=kb.t["aux_spill"][tc * 128:(tc + 1) * 128, :],
                        in_=aux[:, tc * D:(tc + 1) * D]).then_inc(dma, 16)
            kb.bump("dma", 16)
        e.wait_ge(dma, kb.val("dma"))

    @p.on("gpsimd")
    def _(e):
        e.dma_start(out=bass.AP(wbig, _WB["decw"],
                                [[16384, 128], [16, 8], [1, 16]]),
                    in_=bass.AP(kb.t["decw"], 0,
                                [[16, 128], [128 * 16, 8], [1, 16]])
                    ).then_inc(dmag, 16)
        kb.bump("dmag", 16)
        e.wait_ge(dmag, kb.val("dmag"))

    p.run()

    _sl4_dec(kb, tag)
    if SL4S < 2:
        return
    _sl4_qkv(kb, tag)
    if SL4S < 3:
        return
    _sl4_anchors(kb, tag)
    if SL4S < 4:
        return
    _sl4_bwd(kb, tag)
    if SL4S < 5:
        return
    _sl4_ag(kb, tag)
    if SL4S < 6:
        return
    _sl4_combine(kb, tag)
    if SL4S < 7:
        return
    _sl4_fwd_main(kb, tag)
    if SL4S < 8:
        return

    p = Phase(kb, tag + "_rld")

    @p.on("sync")
    def _(e):
        for tc in range(NCH):
            e.dma_start(out=stream[:, tc * D:(tc + 1) * D],
                        in_=kb.t["aux_spill"][tc * 128:(tc + 1) * 128, :]
                        ).then_inc(dma, 16)
            kb.bump("dma", 16)
        kb.row_load(e, dma, [("ob_row", "ob", D, 0), ("f2b_row", "f2b", D, 2)])
        e.wait_ge(dma, kb.val("dma"))

    p.run()

    def wo_lhs(kH, tc):
        return auxb[:, kH * 2048 + tc * 128:kH * 2048 + (tc + 1) * 128]

    def wo_wsrc(kH):
        return bass.AP(kb.t["ow"], 128 * kH * 1024, [[1024, 128], [1, 1024]])

    def wo_out(e, bank, tc, dh):
        sl = stream[:, tc * D + dh * 512:tc * D + (dh + 1) * 512]
        return e.scalar_tensor_tensor(sl, bank[:, :], 1.0, sl, ALU.mult, ALU.add)

    emit_proj_B(kb, tag + "_wo", wo_lhs, 8, wo_wsrc, True, ("ob_row", D), wo_out)
    if SL4S < 9:
        return

    emit_ln_transpose(kb, stream, xnt, tag + "_ln2",
                      gamma_beta=(kb.t["ln2w_c"], kb.t["ln2b_c"]))
    _sl4_ffn(kb, tag, stream)
    if SL4S < 10:
        return

    p = Phase(kb, tag + "_rls")

    @p.on("sync")
    def _(e):
        for tc in range(NCH):
            e.dma_start(out=aux[:, tc * D:(tc + 1) * D],
                        in_=kb.t["stream_spill"][tc * 128:(tc + 1) * 128, :]
                        ).then_inc(dma, 16)
            kb.bump("dma", 16)
        e.wait_ge(dma, kb.val("dma"))

    p.run()
    p = Phase(kb, tag + "_add")

    @p.on("vector")
    def _(e):
        for tc in range(NCH):
            sl = stream[:, tc * D:(tc + 1) * D]
            e.tensor_tensor(sl, sl, aux[:, tc * D:(tc + 1) * D], ALU.add)

    p.run()


def _scv(kb, name, p0=0, np_=128, w=None):
    c0 = _SC[name]
    widths = dict(zrow=2048, CLS=2048, zer=128, T_all=256, totr=256, Gr=256,
                  lam=256, pre=256, suf=256, DF=256, DE=256, S=1040, Rb=1040,
                  Sin=1040, Rin=1040, gat=1040, tmp=1040, psx=32, attf=1280,
                  attg=1280, dmat=16, wwk=128, den=16, cew=96, wfl=256,
                  mrow=32, wwk2=128, negT=256)
    ww = w if w is not None else widths[name]
    return kb.t["stream"][p0:p0 + np_, c0:c0 + ww]


def _scap(kb, name, c_extra, dims):
    return bass.AP(kb.t["stream"], _SC[name] + c_extra,
                   [[NCH * D, 128]] + dims)


def _sl4_dec(kb, tag):
    dma, smm, sev, sea = (kb.sem("dma"), kb.sem("mm"), kb.sem("ev"),
                          kb.sem("ea"))
    xnt, wbig = kb.t["xnt"], kb.t["wbig"]
    psum = [kb.t[f"psb{i}"] for i in range(6)]
    zrow = _scv(kb, "zrow", 0, 16)
    CLS = _scv(kb, "CLS", 0, 16)
    zer = _scv(kb, "zer", 0, 16)
    p = Phase(kb, tag + "_dec")
    mm0, ea0 = kb.val("mm"), kb.val("ea")

    @p.on("vector")
    def _(e):
        e.memset(zer[:, :], 0.0)

    @p.on("tensor")
    def _(e):
        for t4 in range(4):
            bank = psum[t4 % 2]
            if t4 >= 2:
                e.wait_ge(sea, ea0 + t4 - 1)
            last = None
            for k in range(8):
                last = e.matmul(
                    bank[0:16, :],
                    wbig[:, _WB["decw"] + k * 16:_WB["decw"] + (k + 1) * 16],
                    xnt[:, k * T + t4 * 512:k * T + (t4 + 1) * 512],
                    start=(k == 0), stop=(k == 7))
            last.then_inc(smm, 1)
    kb.bump("mm", 4)

    @p.on("scalar")
    def _(e):
        for t4 in range(4):
            e.wait_ge(smm, mm0 + t4 + 1)
            bank = psum[t4 % 2]
            e.activation(zrow[:, t4 * 512:(t4 + 1) * 512], bank[0:16, :],
                         AF.Exp, bias=kb.t["qsc"][0:16, 14:15],
                         scale=-1.0).then_inc(sea, 1)
    kb.bump("ea", 4)

    p.run()
    p = Phase(kb, tag + "_ll")

    @p.on("scalar")
    def _(e):
        e.activation(zrow[:, :], zrow[:, :], AF.Ln, bias=1.0, scale=1.0
                     ).then_inc(sea, 1)
        kb.bump("ea", 1)

    p.run()
    p = Phase(kb, tag + "_cls")

    @p.on("vector")
    def _(e):
        e.tensor_scalar(zrow[:, :], zrow[:, :], 18.420680743952367, None,
                        ALU.min).then_inc(sev, 1)
        kb.bump("ev", 1)
        e.wait_ge(sev, kb.val("ev"))
        for c in range(NCH):
            e.tensor_tensor_scan(CLS[:, c * 128:(c + 1) * 128],
                                 zrow[:, c * 128:(c + 1) * 128], zer[:, :],
                                 0.0, ALU.add, ALU.add)

    p.run()
    p = Phase(kb, tag + "_clsd")

    @p.on("sync")
    def _(e):
        e.dma_start(out=kb.t["ll_d"][:, :], in_=CLS[:, :]).then_inc(dma, 16)
        kb.bump("dma", 16)
        e.wait_ge(dma, kb.val("dma"))

    p.run()


def _sl4_qkv(kb, tag):
    dma, dmag, smm, sev, sea = (kb.sem("dma"), kb.sem("dmag"), kb.sem("mm"),
                                kb.sem("ev"), kb.sem("ea"))
    xnt, auxb = kb.t["xnt"], kb.t["auxb"]
    wbufs = [kb.t["wb0"], kb.t["wb1"]]
    psum = [kb.t[f"psb{i}"] for i in range(6)]
    phist = kb.t["phi_st"]

    for pj, (wname, bcol, dst0) in enumerate([("qw", "qb_c", 0),
                                              ("kw", "kb_c", 16384)]):
        p = Phase(kb, f"{tag}_{wname}")
        dg0, mm0, ev0, ea0 = (kb.val("dmag"), kb.val("mm"), kb.val("ev"),
                              kb.val("ea"))

        @p.on("gpsimd")
        def _(e, wname=wname, mm0=mm0):
            for m in range(8):
                buf = wbufs[m % 2]
                if m >= 2:
                    e.wait_ge(smm, mm0 + (m - 1) * 4)
                e.dma_start(out=bass.AP(buf, 0, [[1024, 128], [128, 8], [1, 128]]),
                            in_=bass.AP(kb.t[wname], 128 * m,
                                        [[1024, 128], [128 * 1024, 8], [1, 128]])
                            ).then_inc(dmag, 16)
                kb.bump("dmag", 16)

        @p.on("tensor")
        def _(e, dg0=dg0, ev0=ev0):
            n = 0
            for m in range(8):
                buf = wbufs[m % 2]
                e.wait_ge(dmag, dg0 + (m + 1) * 16)
                for t4 in range(4):
                    bank = psum[n % 4]
                    if n >= 4:
                        e.wait_ge(sev, ev0 + (n - 4) * 3 + 2)
                    last = None
                    for k in range(8):
                        last = e.matmul(
                            bank[:, :], buf[:, k * 128:(k + 1) * 128],
                            xnt[:, k * T + t4 * 512:k * T + (t4 + 1) * 512],
                            start=(k == 0), stop=(k == 7))
                    last.then_inc(smm, 1)
                    n += 1
        kb.bump("mm", 32)

        @p.on("vector")
        def _(e, dst0=dst0, bcol=bcol, mm0=mm0, ea0=ea0):
            n = 0
            for m in range(8):
                for t4 in range(4):
                    e.wait_ge(smm, mm0 + n + 1)
                    bank = psum[n % 4]
                    half = phist[:, (n % 2) * 512:(n % 2 + 1) * 512]
                    dst = auxb[:, dst0 + m * 2048 + t4 * 512:
                               dst0 + m * 2048 + (t4 + 1) * 512]
                    e.tensor_scalar(half, bank[:, :],
                                    kb.t[bcol][:, m:m + 1], 0.0,
                                    ALU.add, ALU.min).then_inc(sev, 1)
                    e.tensor_scalar(dst, bank[:, :],
                                    kb.t[bcol][:, m:m + 1], 0.0,
                                    ALU.add, ALU.max).then_inc(sev, 1)
                    e.wait_ge(sea, ea0 + n + 1)
                    e.tensor_tensor(dst, dst, half, ALU.add).then_inc(sev, 1)
                    n += 1
        kb.bump("ev", 96)

        @p.on("scalar")
        def _(e, mm0=mm0, ev0=ev0):
            for n in range(32):
                e.wait_ge(sev, ev0 + n * 3 + 1)
                half = phist[:, (n % 2) * 512:(n % 2 + 1) * 512]
                e.activation(half, half, AF.Exp, bias=0.0,
                             scale=1.0).then_inc(sea, 1)
        kb.bump("ea", 32)

        p.run()

    p = Phase(kb, tag + "_vrows")

    @p.on("sync")
    def _(e):
        kb.row_load(e, dma, [("vb_row", "vb", D, 0)])
        e.wait_ge(dma, kb.val("dma"))

    p.run()

    def v_lhs(kH, tc):
        return kb.t["xnt"][:, kH * T + tc * 128:kH * T + (tc + 1) * 128]

    vstg = [kb.t["wb0"], kb.t["wb1"]]
    p = Phase(kb, tag + "_v")
    dg0, mm0, ev0 = kb.val("dmag"), kb.val("mm"), kb.val("ev")
    groups = [[0, 1, 2], [3, 4, 5], [6, 7, 8], [9, 10, 11], [12, 13], [14, 15]]
    dm_start = kb.val("dma")
    wbig = kb.t["wbig"]
    vst = [wbig[:, _WB["wwork"] + i * 1024:_WB["wwork"] + (i + 1) * 1024]
           for i in range(4)]

    for gi, taus in enumerate(groups):
        p = Phase(kb, f"{tag}_v{gi}")
        dg0, mm0, ev0 = kb.val("dmag"), kb.val("mm"), kb.val("ev")
        nt = len(taus)

        @p.on("gpsimd")
        def _(e, taus=taus, mm0=mm0):
            for kH in range(8):
                buf = wbufs[kH % 2]
                if kH >= 2:
                    e.wait_ge(smm, mm0 + kH - 1)
                e.dma_start(out=buf[:, :],
                            in_=bass.AP(kb.t["vw"], 128 * kH * 1024,
                                        [[1024, 128], [1, 1024]])
                            ).then_inc(dmag, 16)
                kb.bump("dmag", 16)

        @p.on("tensor")
        def _(e, taus=taus, dg0=dg0):
            for kH in range(8):
                buf = wbufs[kH % 2]
                e.wait_ge(dmag, dg0 + (kH + 1) * 16)
                last = None
                for ti, tau in enumerate(taus):
                    for dh in range(2):
                        bank = psum[ti * 2 + dh]
                        last = e.matmul(bank[:, :], v_lhs(kH, tau),
                                        buf[:, dh * 512:(dh + 1) * 512],
                                        start=(kH == 0), stop=False)
                last.then_inc(smm, 1)
            for ti, tau in enumerate(taus):
                for dh in range(2):
                    bank = psum[ti * 2 + dh]
                    e.matmul(bank[:, :], kb.row_ones("vb_row"),
                             kb.row_ap("vb_row", dh * 512, 512),
                             start=False, stop=True).then_inc(smm, 1)
        kb.bump("mm", 8 + nt * 2)

        @p.on("vector")
        def _(e, taus=taus, nt=nt, mm0=mm0):
            e.wait_ge(smm, mm0 + 8 + nt * 2)
            for ti, tau in enumerate(taus):
                for dh in range(2):
                    bank = psum[ti * 2 + dh]
                    if tau >= 4:
                        e.wait_ge(dma, dm_start + (tau - 3) * 16)
                    ins = e.tensor_copy(vst[tau % 4][:, dh * 512:(dh + 1) * 512],
                                        bank[:, :])
                    if dh == 1:
                        ins.then_inc(sev, 1)
                        kb.bump("ev", 1)

        @p.on("sync")
        def _(e, taus=taus, ev0=ev0):
            for ti, tau in enumerate(taus):
                e.wait_ge(sev, ev0 + ti + 1)
                e.dma_start(out=bass.AP(kb.t["v_stage"], tau * 131072,
                                        [[1024, 128], [1, 1024]]),
                            in_=vst[tau % 4][:, :]).then_inc(dma, 16)
                kb.bump("dma", 16)

        p.run()

    p = Phase(kb, tag + "_vld")

    @p.on("sync")
    def _(e):
        e.dma_start(out=kb.t["xnt"][:, :],
                    in_=bass.AP(kb.t["v_stage"], 0,
                                [[1024, 128], [131072, 16], [1, 1024]])
                    ).then_inc(dma, 16)
        kb.bump("dma", 16)
        e.wait_ge(dma, kb.val("dma"))

    p.run()


def _sl4_anchors(kb, tag):
    dma, smm, sev, sea = (kb.sem("dma"), kb.sem("mm"), kb.sem("ev"),
                          kb.sem("ea"))
    psum = [kb.t[f"psb{i}"] for i in range(6)]
    eye_f = kb.t["eye_f"]
    CLS = _scv(kb, "CLS", 0, 16)
    T_all, totr, Gr = _scv(kb, "T_all"), _scv(kb, "totr", 0, 1), _scv(kb, "Gr", 0, 1)
    lam, pre, suf = (_scv(kb, "lam", 0, 1), _scv(kb, "pre", 0, 1),
                     _scv(kb, "suf", 0, 1))
    DF, DE = _scv(kb, "DF"), _scv(kb, "DE")
    p = Phase(kb, tag + "_anc")
    mm0, ev0, ea0, dm0 = kb.val("mm"), kb.val("ev"), kb.val("ea"), kb.val("dma")

    @p.on("sync")
    def _(e):
        e.dma_start(out=bass.AP(kb.t["stream"], _SC["totr"],
                                [[NCH * D, 1], [16, 16], [1, 16]]),
                    in_=bass.AP(kb.t["ll_d"], 127, [[128, 16], [2048, 16]]),
                    allow_slow_non_contiguous=True).then_inc(dma, 16)
        kb.bump("dma", 16)

    @p.on("tensor")
    def _(e):
        for c in range(NCH):
            e.matmul(psum[0][:, c * 16:(c + 1) * 16],
                     CLS[:, c * 128:(c + 1) * 128], eye_f[0:16, 0:16],
                     start=True, stop=True).then_inc(smm, 1)
        e.wait_ge(dma, dm0 + 16)
        e.wait_ge(sev, ev0 + 1)
        e.matmul(psum[1][:, 0:256], kb.t["ones_all"][0:1, 0:128],
                 totr[:, :], start=True, stop=True).then_inc(smm, 1)
    kb.bump("mm", NCH + 1)

    @p.on("vector")
    def _(e):
        e.wait_ge(smm, mm0 + NCH)
        e.tensor_copy(T_all[:, :], psum[0][:, 0:256]).then_inc(sev, 1)
        e.wait_ge(dma, dm0 + 16)
        flush(kb, e, e.tensor_copy(Gr[:, 0:16], totr[:, 0:16]))
        for c in range(1, NCH):
            flush(kb, e, e.tensor_tensor(Gr[:, c * 16:(c + 1) * 16],
                                         Gr[:, (c - 1) * 16:c * 16],
                                         totr[:, c * 16:(c + 1) * 16], ALU.add))
        e.nop().then_inc(sev, 1)
        e.tensor_tensor(pre[:, :], totr[:, :], Gr[:, :],
                        ALU.subtract).then_inc(sev, 1)
        gtot = bass.AP(kb.t["stream"], _SC["Gr"] + 240,
                       [[NCH * D, 1], [0, 16], [1, 16]])
        e.tensor_tensor(suf[:, :], Gr[:, :], gtot,
                        ALU.subtract).then_inc(sev, 1)
        e.wait_ge(smm, mm0 + NCH + 1)
        e.scalar_tensor_tensor(DE[:, :], psum[1][:, 0:256], -1.0, T_all[:, :],
                               ALU.mult, ALU.add).then_inc(sev, 1)
        e.tensor_scalar(_scv(kb, "negT")[:, :], T_all[:, :], -1.0, None,
                        ALU.mult).then_inc(sev, 1)
    kb.bump("ev", 6)

    @p.on("scalar")
    def _(e):
        e.wait_ge(sev, ev0 + 2)
        e.activation(lam[:, :], totr[:, :], AF.Exp, bias=0.0, scale=-1.0)
        e.wait_ge(sev, ev0 + 3)
        e.activation(pre[:, :], pre[:, :], AF.Exp, bias=0.0, scale=1.0)
        e.wait_ge(sev, ev0 + 4)
        e.activation(suf[:, :], suf[:, :], AF.Exp, bias=0.0, scale=1.0)
        e.activation(DF[:, :], T_all[:, :], AF.Exp, bias=0.0, scale=-1.0)
        e.wait_ge(sev, ev0 + 5)
        e.activation(DE[:, :], DE[:, :], AF.Exp, bias=0.0,
                     scale=1.0).then_inc(sea, 1)
    kb.bump("ea", 1)

    p.run()


def _sl4_scan(kb, tag, fwd):
    dma, smm, sev, sea = (kb.sem("dma"), kb.sem("mm"), kb.sem("ev"),
                          kb.sem("ea"))
    psum = [kb.t[f"psb{i}"] for i in range(6)]
    wbig, xnt, auxb = kb.t["wbig"], kb.t["xnt"], kb.t["auxb"]
    eye_s = kb.t["eye_s"]
    ps_bf = kb.t["ps_bf"]
    order = list(range(NCH)) if fwd else list(range(NCH - 1, -1, -1))
    dname = "DE" if fwd else "DF"
    snap = kb.t["s_d"] if fwd else kb.t["r_d"]
    st = _scv(kb, "S") if fwd else _scv(kb, "Rb")
    lam = _scv(kb, "lam", 0, 1)
    psx = _scv(kb, "psx")
    ktm = wbig[:, _WB["ktm"]:_WB["ktm"] + 1024]
    kdf = wbig[:, _WB["kdf"]:_WB["kdf"] + 1024]
    dfb = wbig[:, _WB["dfb"]:_WB["dfb"] + 16]

    p = Phase(kb, tag)
    mm0, ev0, dm0 = kb.val("mm"), kb.val("ev"), kb.val("dma")
    T_IT, V_IT = 73, 6

    @p.on("tensor")
    def _(e):
        for i, c in enumerate(order):
            if i >= 1:
                e.wait_ge(sev, ev0 + 1 + (i - 1) * V_IT + 1)
            for m in range(8):
                e.transpose(ps_bf[:, m * 128:(m + 1) * 128],
                            auxb[:, 16384 + m * 2048 + c * 128:
                                 16384 + m * 2048 + (c + 1) * 128],
                            eye_s[:, :]).then_inc(smm, 1)
            if i >= 1:
                e.wait_ge(sev, ev0 + 1 + (i - 1) * V_IT + 3)
            e.matmul(psum[3][:, 0:16], kb.t["ones_all"][0:1, 0:128],
                     lam[:, c * 16:(c + 1) * 16], start=True,
                     stop=True).then_inc(smm, 1)
            e.wait_ge(sev, ev0 + 1 + i * V_IT + 3)
            for h in range(16):
                b = 0 if h < 7 else (1 if h < 14 else 2)
                c0 = (h - 7 * b) * 65
                kl = kdf[:, h * 64:(h + 1) * 64]
                vs = xnt[:, c * 1024 + h * 64:c * 1024 + (h + 1) * 64]
                for base in (0, 64):
                    e.matmul(psum[b][base:base + 64, c0:c0 + 64], kl, vs,
                             start=True, stop=True).then_inc(smm, 1)
                    e.matmul(psum[b][base:base + 64, c0 + 64:c0 + 65], kl,
                             kb.t["onesc_b"][:, h:h + 1], start=True,
                             stop=True).then_inc(smm, 1)
    kb.bump("mm", T_IT * NCH)

    @p.on("vector")
    def _(e):
        e.memset(st[:, :], 0.0).then_inc(sev, 1)
        for i, c in enumerate(order):
            base = ev0 + 1 + i * V_IT
            e.wait_ge(smm, mm0 + i * T_IT + 8)
            e.tensor_copy(ktm[:, :], ps_bf[:, 0:1024]).then_inc(sev, 1)
            e.tensor_copy(dfb[:, :],
                          _scv(kb, dname)[:, c * 16:(c + 1) * 16])
            fence(e, 12)
            e.tensor_tensor(kdf[:, :], ktm[:, :],
                            bass.AP(wbig, _WB["dfb"],
                                    [[16384, 128], [1, 16], [0, 64]]),
                            ALU.mult).then_inc(sev, 1)
            e.wait_ge(smm, mm0 + i * T_IT + 9)
            e.tensor_copy(psx[:, 0:16], psum[3][:, 0:16]).then_inc(sev, 1)
            fence(e, 12)
            e.wait_ge(dma, dm0 + 16 * (i + 1))
            e.tensor_tensor(st[:, :], st[:, :],
                            _scap(kb, "psx", 0, [[1, 16], [0, 65]]),
                            ALU.mult).then_inc(sev, 1)
            e.wait_ge(smm, mm0 + (i + 1) * T_IT)
            e.tensor_tensor(st[:, 0:455], st[:, 0:455], psum[0][:, 0:455],
                            ALU.add).then_inc(sev, 1)
            e.tensor_tensor(st[:, 455:910], st[:, 455:910], psum[1][:, 0:455],
                            ALU.add)
            e.tensor_tensor(st[:, 910:1040], st[:, 910:1040],
                            psum[2][:, 0:130], ALU.add).then_inc(sev, 1)
    kb.bump("ev", 1 + V_IT * NCH)

    @p.on("sync")
    def _(e):
        for i, c in enumerate(order):
            if i == 0:
                e.wait_ge(sev, ev0 + 1)
            else:
                e.wait_ge(sev, ev0 + 1 + i * V_IT)
            e.dma_start(out=bass.AP(snap, c * 133120,
                                    [[1040, 128], [1, 1040]]),
                        in_=st[:, :]).then_inc(dma, 16)
            kb.bump("dma", 16)
        e.wait_ge(dma, kb.val("dma"))

    p.run()


def _sl4_bwd(kb, tag):
    _sl4_scan(kb, tag + "_bsc", fwd=False)


def _sl4_ag(kb, tag):
    _sl4_scan(kb, tag + "_fsc", fwd=True)
    dma, scc = kb.sem("dma"), kb.sem("cc")
    p = Phase(kb, tag + "_pub")

    @p.on("sync")
    def _(e):
        e.dma_start(out=bass.AP(kb.t["lion_in"], 0, [[1040, 64], [1, 1040]]),
                    in_=_scv(kb, "S", 0, 64)).then_inc(dma, 16)
        kb.bump("dma", 16)
        e.dma_start(out=bass.AP(kb.t["lion_in"], 66560,
                                [[1040, 64], [1, 1040]]),
                    in_=_scv(kb, "Rb", 0, 64)).then_inc(dma, 16)
        kb.bump("dma", 16)
        e.dma_start(out=bass.AP(kb.t["lion_in"], 133120, [[16, 1], [1, 16]]),
                    in_=_scv(kb, "Gr", 0, 1)[:, 240:256]).then_inc(dma, 16)
        kb.bump("dma", 16)
        e.wait_ge(dma, kb.val("dma"))

    p.run()
    p = Phase(kb, tag + "_cc")

    @p.on("gpsimd")
    def _(e):
        e.collective_compute(
            "AllGather", ALU.bypass, replica_groups=[list(range(N_CORES))],
            ins=[kb.t["lion_in"].ap().opt()],
            outs=[kb.t["lion_out"].ap().opt()],
        ).then_inc(scc, 1)
        kb.bump("cc", 1)

    p.run()


def _sl4_combine(kb, tag):
    dma, smm, sev, sea, scc = (kb.sem("dma"), kb.sem("mm"), kb.sem("ev"),
                               kb.sem("ea"), kb.sem("cc"))
    psum = [kb.t[f"psb{i}"] for i in range(6)]
    cew = _scv(kb, "cew", 0, 16)
    zer = _scv(kb, "zer", 0, 16)
    mrow = _scv(kb, "mrow", 0, 1)
    wfl = _scv(kb, "wfl", 0, 1)
    Sin, Rin = _scv(kb, "Sin"), _scv(kb, "Rin")
    Sg, Rg = _scv(kb, "gat"), _scv(kb, "tmp")
    tS, tR = _scv(kb, "S"), _scv(kb, "Rb")
    psx = _scv(kb, "psx")
    cc_target = kb.val("cc")

    p = Phase(kb, tag + "_w")
    mm0, ev0, ea0, dm0 = kb.val("mm"), kb.val("ev"), kb.val("ea"), kb.val("dma")

    @p.on("sync")
    def _(e):
        e.wait_ge(scc, cc_target)
        for r in range(N_CORES):
            e.dma_start(out=cew[:, r:r + 1],
                        in_=bass.AP(kb.t["lion_out"], r * 133136 + 133120,
                                    [[1, 16], [1, 1]]),
                        allow_slow_non_contiguous=True).then_inc(dma, 16)
            kb.bump("dma", 16)
        for j, nm in enumerate(["m_lt", "m_gt", "m_me"]):
            e.dma_start(out=mrow[:, j * 8:(j + 1) * 8],
                        in_=kb.t[nm][:, :]).then_inc(dma, 16)
            kb.bump("dma", 16)
        e.wait_ge(dma, kb.val("dma"))

    @p.on("tensor")
    def _(e):
        e.wait_ge(dma, dm0 + 11 * 16)
        for j in range(3):
            e.matmul(psum[4][0:16, j * 8:(j + 1) * 8],
                     kb.t["ones_all"][0:1, 0:16],
                     mrow[:, j * 8:(j + 1) * 8], start=True,
                     stop=True).then_inc(smm, 1)
    kb.bump("mm", 3)

    @p.on("vector")
    def _(e):
        e.wait_ge(smm, mm0 + 3)
        e.memset(cew[:, 8:9], 0.0)
        e.tensor_copy(cew[:, 17:41], psum[4][0:16, 0:24])
        fence(e, 12)
        e.tensor_tensor_scan(cew[:, 9:17], cew[:, 0:8], zer[:, 0:8], 0.0,
                             ALU.add, ALU.add)
        fence(e, 12)
        e.tensor_tensor(cew[:, 41:49], cew[:, 0:8], cew[:, 17:25], ALU.mult)
        fence(e, 12)
        e.tensor_reduce(cew[:, 49:50], cew[:, 41:49], AX.X, ALU.add)
        fence(e, 12)
        e.tensor_tensor(cew[:, 41:49], cew[:, 0:8], cew[:, 33:41], ALU.mult)
        fence(e, 12)
        e.tensor_reduce(cew[:, 50:51], cew[:, 41:49], AX.X, ALU.add)
        fence(e, 12)
        e.tensor_tensor(cew[:, 51:52], cew[:, 49:50], cew[:, 50:51], ALU.add)
        fence(e, 12)
        e.tensor_scalar(cew[:, 52:60], cew[:, 9:17], cew[:, 49:50], 0.0,
                        ALU.subtract, ALU.min).then_inc(sev, 1)
        e.tensor_scalar(cew[:, 60:68], cew[:, 8:16], cew[:, 51:52], 0.0,
                        ALU.subtract, ALU.max).then_inc(sev, 1)
        e.wait_ge(sea, ea0 + 2)
        e.tensor_tensor(cew[:, 52:60], cew[:, 52:60], cew[:, 17:25],
                        ALU.mult).then_inc(sev, 1)
        e.tensor_tensor(cew[:, 60:68], cew[:, 60:68], cew[:, 25:33],
                        ALU.mult).then_inc(sev, 1)
    kb.bump("ev", 4)

    @p.on("scalar")
    def _(e):
        e.wait_ge(sev, ev0 + 1)
        e.activation(cew[:, 52:60], cew[:, 52:60], AF.Exp, bias=0.0,
                     scale=1.0).then_inc(sea, 1)
        e.wait_ge(sev, ev0 + 2)
        e.activation(cew[:, 60:68], cew[:, 60:68], AF.Exp, bias=0.0,
                     scale=-1.0).then_inc(sea, 1)
    kb.bump("ea", 2)

    p.run()

    p = Phase(kb, tag + "_hop")

    @p.on("sync")
    def _(e):
        e.dma_start(out=bass.AP(kb.t["hop_d"], 0, [[8, 16], [1, 8]]),
                    in_=cew[:, 52:60]).then_inc(dma, 16)
        kb.bump("dma", 16)
        e.dma_start(out=bass.AP(kb.t["hop_d"], 128, [[8, 16], [1, 8]]),
                    in_=cew[:, 60:68]).then_inc(dma, 16)
        kb.bump("dma", 16)
        e.wait_ge(dma, kb.val("dma"))
        e.dma_start(out=bass.AP(kb.t["stream"], _SC["wfl"],
                                [[NCH * D, 1], [16, 8], [1, 16]]),
                    in_=bass.AP(kb.t["hop_d"], 0, [[1, 8], [8, 16]]),
                    allow_slow_non_contiguous=True).then_inc(dma, 16)
        kb.bump("dma", 16)
        e.dma_start(out=bass.AP(kb.t["stream"], _SC["wfl"] + 128,
                                [[NCH * D, 1], [16, 8], [1, 16]]),
                    in_=bass.AP(kb.t["hop_d"], 128, [[1, 8], [8, 16]]),
                    allow_slow_non_contiguous=True).then_inc(dma, 16)
        kb.bump("dma", 16)
        e.wait_ge(dma, kb.val("dma"))

    p.run()

    p = Phase(kb, tag + "_acc")
    mm0, ev0, dm0 = kb.val("mm"), kb.val("ev"), kb.val("dma")

    @p.on("sync")
    def _(e):
        for r in range(N_CORES):
            if r >= 1:
                e.wait_ge(sev, ev0 + 2 + (r - 1) * 3 + 3)
            for half in range(2):
                e.dma_start(out=Sg[half * 64:half * 64 + 64, :],
                            in_=bass.AP(kb.t["lion_out"], r * 133136,
                                        [[1040, 64], [1, 1040]])
                            ).then_inc(dma, 16)
                kb.bump("dma", 16)
                e.dma_start(out=Rg[half * 64:half * 64 + 64, :],
                            in_=bass.AP(kb.t["lion_out"], r * 133136 + 66560,
                                        [[1040, 64], [1, 1040]])
                            ).then_inc(dma, 16)
                kb.bump("dma", 16)

    @p.on("tensor")
    def _(e):
        for r in range(N_CORES):
            if r >= 1:
                e.wait_ge(sev, ev0 + 2 + (r - 1) * 3 + 1)
            e.matmul(psum[3][:, 0:16], kb.t["ones_all"][0:1, 0:128],
                     wfl[:, r * 16:(r + 1) * 16], start=True,
                     stop=True).then_inc(smm, 1)
            e.matmul(psum[3][:, 16:32], kb.t["ones_all"][0:1, 0:128],
                     wfl[:, 128 + r * 16:128 + (r + 1) * 16], start=True,
                     stop=True).then_inc(smm, 1)
    kb.bump("mm", 2 * N_CORES)

    @p.on("vector")
    def _(e):
        e.memset(Sin[:, :], 0.0)
        e.memset(Rin[:, :], 0.0).then_inc(sev, 2)
        for r in range(N_CORES):
            e.wait_ge(smm, mm0 + 2 * (r + 1))
            e.tensor_copy(psx[:, 0:32], psum[3][:, 0:32]).then_inc(sev, 1)
            fence(e, 12)
            e.wait_ge(dma, dm0 + 64 * (r + 1))
            e.tensor_tensor(tS[:, :], Sg[:, :],
                            _scap(kb, "psx", 0, [[1, 16], [0, 65]]),
                            ALU.mult)
            e.tensor_tensor(tR[:, :], Rg[:, :],
                            _scap(kb, "psx", 16, [[1, 16], [0, 65]]),
                            ALU.mult).then_inc(sev, 1)
            fence(e, 12)
            e.tensor_tensor(Sin[:, :], Sin[:, :], tS[:, :], ALU.add)
            e.tensor_tensor(Rin[:, :], Rin[:, :], tR[:, :],
                            ALU.add).then_inc(sev, 1)
    kb.bump("ev", 2 + 3 * N_CORES)

    p.run()


def _sl4_fwd_main(kb, tag):
    dma, smm, sev, sea = (kb.sem("dma"), kb.sem("mm"), kb.sem("ev"),
                          kb.sem("ea"))
    psum = [kb.t[f"psb{i}"] for i in range(6)]
    wbig, xnt, auxb = kb.t["wbig"], kb.t["xnt"], kb.t["auxb"]
    wbigf = kb.t["wbigf"]
    eye_s, ps_bf = kb.t["eye_s"], kb.t["ps_bf"]
    CLS = _scv(kb, "CLS", 0, 16)
    T_all = _scv(kb, "T_all")
    Sin, Rin = _scv(kb, "Sin"), _scv(kb, "Rin")
    St, Rt = _scv(kb, "S"), _scv(kb, "Rb")
    gat, tmp = _scv(kb, "gat"), _scv(kb, "tmp")
    psx, dmat = _scv(kb, "psx"), _scv(kb, "dmat")
    attf, attg = _scv(kb, "attf"), _scv(kb, "attg")
    wwkb = [_scv(kb, "wwk"), _scv(kb, "wwk2")]
    sel = wbigf[0:16, 5120:7168]
    sbf = wbig[:, _WB["sbf"]:_WB["sbf"] + 1040]
    rbf = wbig[:, _WB["rbf"]:_WB["rbf"] + 1040]
    att = wbig[:, _WB["att"]:_WB["att"] + 1024]
    asb = [wbig[:, 14336:14464], wbig[:, 14464:14592]]

    def phq(m, c, b=0, n=128):
        return auxb[b:b + n, m * 2048 + c * 128:m * 2048 + (c + 1) * 128]

    def phk(m, c, b=0, n=128):
        return auxb[b:b + n, 16384 + m * 2048 + c * 128:
                    16384 + m * 2048 + (c + 1) * 128]

    p = Phase(kb, tag + "_sel")

    @p.on("sync")
    def _(e):
        e.dma_start(out=sel[:, :], in_=kb.t["sel_d"][:, :]).then_inc(dma, 16)
        kb.bump("dma", 16)
        e.dma_start(out=gat[:, :],
                    in_=bass.AP(kb.t["s_d"], 0, [[1040, 128], [1, 1040]])
                    ).then_inc(dma, 16)
        kb.bump("dma", 16)
        e.dma_start(out=tmp[:, :],
                    in_=bass.AP(kb.t["r_d"], 0, [[1040, 128], [1, 1040]])
                    ).then_inc(dma, 16)
        kb.bump("dma", 16)
        e.wait_ge(dma, kb.val("dma"))

    p.run()

    p = Phase(kb, tag + "_mn")
    mm0, ev0, ea0, dm0 = kb.val("mm"), kb.val("ev"), kb.val("ea"), kb.val("dma")
    NPF = int(_os.environ.get("KMOD_FM_NP", "8"))
    NCF = int(_os.environ.get("KMOD_FM_NC", str(NCH)))
    DOTR = _os.environ.get("KMOD_FM_TR", "1") != "0"
    NTR = NPF if DOTR else 0
    T_C = 2 + NPF * 12 + NTR
    V_C = 7 + NPF * 10 + 1 + 2 * NPF + NTR

    def vbase(c):
        return ev0 + c * V_C

    def tbase(c):
        return mm0 + c * T_C

    @p.on("sync")
    def _(e):
        for c in range(1, NCF):
            e.wait_ge(sev, vbase(c - 1) + 7)
            e.dma_start(out=gat[:, :],
                        in_=bass.AP(kb.t["s_d"], c * 133120,
                                    [[1040, 128], [1, 1040]])
                        ).then_inc(dma, 16)
            kb.bump("dma", 16)
            e.dma_start(out=tmp[:, :],
                        in_=bass.AP(kb.t["r_d"], c * 133120,
                                    [[1040, 128], [1, 1040]])
                        ).then_inc(dma, 16)
            kb.bump("dma", 16)

    @p.on("tensor")
    def _(e):
        for c in range(NCF):
            if c >= 1:
                e.wait_ge(sev, vbase(c - 1) + 1)
            e.matmul(psum[3][:, 0:16], kb.t["ones_all"][0:1, 0:128],
                     _scv(kb, "pre", 0, 1)[:, c * 16:(c + 1) * 16],
                     start=True, stop=True).then_inc(smm, 1)
            e.matmul(psum[3][:, 16:32], kb.t["ones_all"][0:1, 0:128],
                     _scv(kb, "suf", 0, 1)[:, c * 16:(c + 1) * 16],
                     start=True, stop=True).then_inc(smm, 1)
            for m in range(NPF):
                tp = tbase(c) + 2 + m * 12
                g = c * NPF + m
                if g >= 1:
                    pc, pm = (g - 1) // NPF, (g - 1) % NPF
                    e.wait_ge(sev, vbase(pc) + 7 + pm * 10 + 2)
                for j, h in enumerate((2 * m, 2 * m + 1)):
                    b = 64 * j
                    e.matmul(psum[0][:, j * 256:j * 256 + 128],
                             sel[:, h * 128:(h + 1) * 128],
                             CLS[:, c * 128:(c + 1) * 128],
                             start=True, stop=True).then_inc(smm, 1)
                    e.matmul(psum[0][:, j * 256 + 128:j * 256 + 256],
                             phk(m, c, b, 64), phq(m, c, b, 64),
                             start=True, stop=True).then_inc(smm, 1)
                for j, h in enumerate((2 * m, 2 * m + 1)):
                    b = 64 * j
                    pa = psum[1 + j]
                    e.wait_ge(sev, vbase(c) + 7 + m * 10 + 1 + j)
                    if g >= 1:
                        pc, pm = (g - 1) // NPF, (g - 1) % NPF
                        e.wait_ge(sev, vbase(pc) + 7 + pm * 10 + 10)
                    e.matmul(pa[:, 0:64], asb[j][:, :],
                             xnt[:, c * 1024 + h * 64:
                                 c * 1024 + (h + 1) * 64],
                             start=True, stop=True).then_inc(smm, 1)
                    e.matmul(pa[:, 64:65], asb[j][:, :],
                             kb.t["onesc_b"][:, h:h + 1],
                             start=True, stop=True).then_inc(smm, 1)
                    if m == 0 and j == 0:
                        e.wait_ge(sev, vbase(c) + 4)
                    e.matmul(pa[:, 80:145], phq(m, c, b, 64),
                             sbf[b:b + 64, h * 65:(h + 1) * 65],
                             start=True, stop=True).then_inc(smm, 1)
                    if m == 0 and j == 0:
                        e.wait_ge(sev, vbase(c) + 7)
                    e.matmul(pa[:, 160:225], phq(m, c, b, 64),
                             rbf[b:b + 64, h * 65:(h + 1) * 65],
                             start=True, stop=True).then_inc(smm, 1)
            for m in range(NPF if DOTR else 0):
                e.wait_ge(sev, vbase(c) + 7 + NPF * 10 + 1 + 2 * (m + 1))
                if c >= 1 and m == 0:
                    e.wait_ge(sev, vbase(c - 1) + V_C)
                e.transpose(ps_bf[:, m * 128:(m + 1) * 128],
                            att[:, m * 128:(m + 1) * 128],
                            eye_s[:, :]).then_inc(smm, 1)
    kb.bump("mm", T_C * NCF)

    @p.on("scalar")
    def _(e):
        negT = _scv(kb, "negT")
        for c in range(NCF):
            for m in range(NPF):
                g = c * NPF + m
                tp = tbase(c) + 2 + m * 12
                for j, h in enumerate((2 * m, 2 * m + 1)):
                    e.wait_ge(smm, tp + 4)
                    if g >= 1:
                        pc, pm = (g - 1) // NPF, (g - 1) % NPF
                        e.wait_ge(sev, vbase(pc) + 7 + pm * 10 + 1 + j)
                    e.activation(wwkb[j][:, :],
                                 psum[0][:, j * 256:j * 256 + 128], AF.Abs,
                                 bias=negT[:, c * 16 + h:c * 16 + h + 1],
                                 scale=1.0).then_inc(sea, 1)
                for j in range(2):
                    e.activation(wwkb[j][:, :], wwkb[j][:, :], AF.Exp,
                                 bias=0.0, scale=-1.0).then_inc(sea, 1)
    kb.bump("ea", 4 * NPF * NCF)

    @p.on("vector")
    def _(e):
        for c in range(NCF):
            e.wait_ge(smm, tbase(c) + 2)
            if c >= 1:
                e.wait_ge(smm, tbase(c - 1) + 2 + NPF * 12)
            e.tensor_copy(psx[:, 0:32], psum[3][:, 0:32]).then_inc(sev, 1)
            fence(e, 12)
            if c >= 1:
                e.wait_ge(dma, dm0 + 32 * c)
            e.tensor_tensor(St[:, :], Sin[:, :],
                            _scap(kb, "psx", 0, [[1, 16], [0, 65]]),
                            ALU.mult).then_inc(sev, 1)
            e.tensor_tensor(St[:, :], St[:, :], gat[:, :],
                            ALU.add).then_inc(sev, 1)
            e.tensor_copy(sbf[:, :], St[:, :]).then_inc(sev, 1)
            e.tensor_tensor(Rt[:, :], Rin[:, :],
                            _scap(kb, "psx", 16, [[1, 16], [0, 65]]),
                            ALU.mult).then_inc(sev, 1)
            e.tensor_tensor(Rt[:, :], Rt[:, :], tmp[:, :],
                            ALU.add).then_inc(sev, 1)
            e.tensor_copy(rbf[:, :], Rt[:, :]).then_inc(sev, 1)
            for m in range(NPF):
                tp = tbase(c) + 2 + m * 12
                g = c * NPF + m
                for j, h in enumerate((2 * m, 2 * m + 1)):
                    e.wait_ge(sea, ea0 + 4 * g + 3 + j)
                    e.wait_ge(smm, tp + 4)
                    if g >= 1:
                        pc, pm = (g - 1) // NPF, (g - 1) % NPF
                        e.wait_ge(smm, tbase(pc) + 2 + pm * 12 + 4 + 4 * j + 2)
                    e.tensor_tensor(asb[j][:, :],
                                    psum[0][:, j * 256 + 128:j * 256 + 256],
                                    wwkb[j][:, :], ALU.mult).then_inc(sev, 1)
                for j, h in enumerate((2 * m, 2 * m + 1)):
                    e.wait_ge(smm, tp + 4 + 4 * j + 4)
                    e.tensor_scalar(attf[:, h * 80:h * 80 + 65],
                                    psum[1 + j][:, 80:145],
                                    _scv(kb, "DF")[:, c * 16 + h:
                                                   c * 16 + h + 1],
                                    None, ALU.mult).then_inc(sev, 1)
                for j, h in enumerate((2 * m, 2 * m + 1)):
                    e.scalar_tensor_tensor(attg[:, h * 80:h * 80 + 65],
                                           psum[1 + j][:, 160:225],
                                           _scv(kb, "DE")[:, c * 16 + h:
                                                          c * 16 + h + 1],
                                           attf[:, h * 80:h * 80 + 65],
                                           ALU.mult, ALU.add).then_inc(sev, 1)
                for j, h in enumerate((2 * m, 2 * m + 1)):
                    e.scalar_tensor_tensor(attf[:, h * 80:h * 80 + 65],
                                           psum[1 + j][:, 0:65], 1.0,
                                           attg[:, h * 80:h * 80 + 65],
                                           ALU.mult, ALU.add).then_inc(sev, 1)
                for j, h in enumerate((2 * m, 2 * m + 1)):
                    e.tensor_scalar(dmat[:, h:h + 1],
                                    attf[:, h * 80 + 64:h * 80 + 65], 1e-6,
                                    None, ALU.max).then_inc(sev, 1)
            fence(e, 12)
            e.reciprocal(dmat[:, :], dmat[:, :]).then_inc(sev, 1)
            fence(e, 12)
            for h in range(2 * NPF):
                if c >= 1 and h == 0:
                    e.wait_ge(smm, tbase(c - 1) + T_C)
                e.tensor_scalar(att[:, h * 64:(h + 1) * 64],
                                attf[:, h * 80:h * 80 + 64],
                                dmat[:, h:h + 1], None,
                                ALU.mult).then_inc(sev, 1)
            for m in range(NPF if DOTR else 0):
                e.wait_ge(smm, tbase(c) + T_C)
                e.tensor_copy(phq(m, c, 0, 128),
                              ps_bf[:, m * 128:(m + 1) * 128]
                              ).then_inc(sev, 1)
    kb.bump("ev", V_C * NCF)

    p.run()


def _sl4_ffn(kb, tag, dst):
    dmag, smm, sev, sea = (kb.sem("dmag"), kb.sem("mm"), kb.sem("ev"),
                           kb.sem("ea"))
    wbig, xnt = kb.t["wbig"], kb.t["xnt"]
    wbufs = [kb.t["wb0"], kb.t["wb1"]]
    psum = [kb.t[f"psb{i}"] for i in range(6)]
    f1b_c = kb.t["f1b_c"]

    for ttQ in range(4):
        p = Phase(kb, f"{tag}_fh{ttQ}")
        dg0, mm0, ea0 = kb.val("dmag"), kb.val("mm"), kb.val("ea")

        @p.on("gpsimd")
        def _(e, ttQ=ttQ, mm0=mm0):
            for m in range(32):
                buf = wbufs[m % 2]
                if m >= 2:
                    e.wait_ge(smm, mm0 + m - 1)
                e.dma_start(out=bass.AP(buf, 0, [[1024, 128], [128, 8], [1, 128]]),
                            in_=bass.AP(kb.t["f1w"], 128 * m,
                                        [[4096, 128], [128 * 4096, 8], [1, 128]])
                            ).then_inc(dmag, 16)
                kb.bump("dmag", 16)

        @p.on("tensor")
        def _(e, ttQ=ttQ, dg0=dg0, ea0=ea0):
            for m in range(32):
                buf = wbufs[m % 2]
                e.wait_ge(dmag, dg0 + (m + 1) * 16)
                bank = psum[m % 4]
                if m >= 4:
                    e.wait_ge(sea, ea0 + m - 3)
                last = None
                for k in range(8):
                    last = e.matmul(bank[:, :], buf[:, k * 128:(k + 1) * 128],
                                    xnt[:, k * T + ttQ * 512:k * T + (ttQ + 1) * 512],
                                    start=(k == 0), stop=(k == 7))
                last.then_inc(smm, 1)
        kb.bump("mm", 32)

        @p.on("scalar")
        def _(e, ttQ=ttQ, mm0=mm0):
            for m in range(32):
                e.wait_ge(smm, mm0 + m + 1)
                bank = psum[m % 4]
                e.activation(wbig[:, m * 512:(m + 1) * 512], bank[:, :],
                             AF.Gelu, bias=f1b_c[:, m:m + 1],
                             scale=1.0).then_inc(sea, 1)
        kb.bump("ea", 32)

        p.run()

        for gi, taus in enumerate([[0, 1], [2, 3]]):
            p = Phase(kb, f"{tag}_fd{ttQ}_{gi}")
            dg0, mm0, ev0 = kb.val("dmag"), kb.val("mm"), kb.val("ev")
            nt = len(taus)

            @p.on("gpsimd")
            def _(e, taus=taus, mm0=mm0):
                for kH in range(32):
                    buf = wbufs[kH % 2]
                    if kH >= 2:
                        e.wait_ge(smm, mm0 + kH - 1)
                    e.dma_start(out=buf[:, :],
                                in_=bass.AP(kb.t["f2w"], 128 * kH * 1024,
                                            [[1024, 128], [1, 1024]])
                                ).then_inc(dmag, 16)
                    kb.bump("dmag", 16)

            @p.on("tensor")
            def _(e, taus=taus, ttQ=ttQ, dg0=dg0):
                for kH in range(32):
                    buf = wbufs[kH % 2]
                    e.wait_ge(dmag, dg0 + (kH + 1) * 16)
                    last = None
                    for ti, tau in enumerate(taus):
                        for dh in range(2):
                            bank = psum[ti * 2 + dh]
                            last = e.matmul(
                                bank[:, :],
                                wbig[:, kH * 512 + tau * 128:
                                     kH * 512 + (tau + 1) * 128],
                                buf[:, dh * 512:(dh + 1) * 512],
                                start=(kH == 0), stop=False)
                    last.then_inc(smm, 1)
                for ti, tau in enumerate(taus):
                    for dh in range(2):
                        bank = psum[ti * 2 + dh]
                        e.matmul(bank[:, :], kb.row_ones("f2b_row"),
                                 kb.row_ap("f2b_row", dh * 512, 512),
                                 start=False, stop=True).then_inc(smm, 1)
            kb.bump("mm", 32 + nt * 2)

            @p.on("vector")
            def _(e, taus=taus, ttQ=ttQ, mm0=mm0, nt=nt):
                e.wait_ge(smm, mm0 + 32 + nt * 2)
                for ti, tau in enumerate(taus):
                    tc = ttQ * 4 + tau
                    for dh in range(2):
                        bank = psum[ti * 2 + dh]
                        sl = dst[:, tc * D + dh * 512:tc * D + (dh + 1) * 512]
                        e.scalar_tensor_tensor(sl, bank[:, :], 1.0, sl,
                                               ALU.mult, ALU.add)

            p.run()


def emit_ln_stats(kb, src, tag):
    s_sum, s_sq = kb.t["s_sum"], kb.t["s_sq"]
    s_mu, s_rs, s_nmr, s_t0 = (kb.t["s_mu"], kb.t["s_rs"], kb.t["s_nmr"],
                               kb.t["s_t0"])
    stg = [kb.t["stg0"], kb.t["stg1"]]
    p = Phase(kb, f"{tag}_st")

    @p.on("vector")
    def _(e):
        for tc in range(NCH):
            e.tensor_reduce(s_sum[:, tc:tc + 1], src[:, tc * D:(tc + 1) * D],
                            AX.X, ALU.add)

    @p.on("scalar")
    def _(e):
        for tc in range(NCH):
            e.activation(stg[tc % 2][:, :], src[:, tc * D:(tc + 1) * D],
                         AF.Square, bias=0.0, scale=1.0,
                         accum_out=s_sq[:, tc:tc + 1])

    p.run()
    p = Phase(kb, f"{tag}_mu")

    @p.on("vector")
    def _(e):
        e.tensor_scalar(s_mu[:, :], s_sum[:, :], 1.0 / D, None, ALU.mult)
        e.tensor_scalar(s_t0[:, :], s_sq[:, :], 1.0 / D, None, ALU.mult)
        fence(e)
        e.tensor_tensor(s_sum[:, :], s_mu[:, :], s_mu[:, :], ALU.mult)
        fence(e)
        e.tensor_tensor(s_t0[:, :], s_t0[:, :], s_sum[:, :], ALU.subtract)
        fence(e)
        e.tensor_scalar(s_t0[:, :], s_t0[:, :], LN_EPS, None, ALU.add)

    p.run()
    p = Phase(kb, f"{tag}_rs")

    @p.on("scalar")
    def _(e):
        e.activation(s_rs[:, :], s_t0[:, :], AF.Ln, bias=0.0, scale=1.0)
        fence(e, 12)
        e.activation(s_rs[:, :], s_rs[:, :], AF.Exp, bias=0.0, scale=-0.5)

    p.run()
    p = Phase(kb, f"{tag}_nm")

    @p.on("vector")
    def _(e):
        e.tensor_tensor(s_nmr[:, :], s_mu[:, :], s_rs[:, :], ALU.mult)
        fence(e)
        e.tensor_scalar(s_nmr[:, :], s_nmr[:, :], -1.0, None, ALU.mult)

    p.run()


def emit_ln_final_add(kb, tag, src, dst_stream, gamma_name, beta_name,
                      rows_loaded=False):
    dma, smm, sev, sea = (kb.sem("dma"), kb.sem("mm"), kb.sem("ev"),
                          kb.sem("ea"))
    psum = [kb.t[f"psb{i}"] for i in range(6)]
    lnst = kb.t["xntf"]
    emit_ln_stats(kb, src, tag)

    if not rows_loaded:
        p = Phase(kb, tag + "_rows")

        @p.on("sync")
        def _(e):
            kb.row_load(e, dma, [("lng_r", gamma_name, D),
                                 ("lnb_r", beta_name, D)])
            e.wait_ge(dma, kb.val("dma"))

        p.run()
    p = Phase(kb, tag + "_bc")

    @p.on("tensor")
    def _(e):
        for du in range(2):
            e.matmul(psum[du][:, :], kb.row_ones("lng_r"),
                     kb.row_ap("lng_r", du * 512, 512), start=True, stop=True
                     ).then_inc(smm, 1)
            e.matmul(psum[2 + du][:, :], kb.row_ones("lnb_r"),
                     kb.row_ap("lnb_r", du * 512, 512), start=True, stop=True
                     ).then_inc(smm, 1)
    kb.bump("mm", 4)

    p.run()
    p = Phase(kb, tag + "_app")
    s_rs, s_nmr = kb.t["s_rs"], kb.t["s_nmr"]
    ea0, ev0 = kb.val("ea"), kb.val("ev")

    @p.on("scalar")
    def _(e):
        for tc in range(NCH):
            if tc >= 1:
                e.wait_ge(sev, ev0 + tc)
            e.activation(lnst[:, 0:1024], src[:, tc * D:(tc + 1) * D],
                         AF.Identity, bias=s_nmr[:, tc:tc + 1],
                         scale=s_rs[:, tc:tc + 1]).then_inc(sea, 1)
    kb.bump("ea", NCH)

    @p.on("vector")
    def _(e):
        for tc in range(NCH):
            e.wait_ge(sea, ea0 + tc + 1)
            last = None
            for du in range(2):
                u = lnst[:, du * 512:(du + 1) * 512]
                flush(kb, e, e.tensor_tensor(u, u, psum[du][:, :], ALU.mult))
                flush(kb, e, e.tensor_tensor(u, u, psum[2 + du][:, :], ALU.add))
                sl = dst_stream[:, tc * D + du * 512:tc * D + (du + 1) * 512]
                last = e.tensor_tensor(sl, sl, u, ALU.add)
            last.then_inc(sev, 1)
    kb.bump("ev", NCH)

    p.run()



_CACHE = {}


def kernel(**inp):
    per_core = _host_prep(inp)
    kbk = build_program()
    res = run_bass_kernel_spmd(kbk.nc, per_core, core_ids=list(range(N_CORES)))
    outs = [np.asarray(r["out"], np.float32) for r in res.results]
    return np.concatenate(outs, axis=0).reshape(1, N_CORES * T, D)
